# version 11
# speedup vs baseline: 2.4263x; 1.0317x over previous
import math
from contextlib import ExitStack

import numpy as np
import ml_dtypes

import concourse.bass as bass
import concourse.mybir as mybir
from concourse.bass_utils import run_bass_kernel_spmd

F32 = mybir.dt.float32
BF16 = mybir.dt.bfloat16
AF = mybir.ActivationFunctionType
ALU = mybir.AluOpType
AX = mybir.AxisListType.X
BF = ml_dtypes.bfloat16

NCORES = 8
T = 2048
S = 8192
D = 1024
DCH = D // 128
NTT = T // 512
SCALE0 = 1.0 / math.sqrt(160.0)
SCALE1 = 64.0 ** -0.5
LN_EPS = 1e-5


class P:

    def __init__(self, name):
        self.nc = bass.Bass("TRN2", target_bir_lowering=False)
        self.es = ExitStack()
        self.name = name
        self.c = {}
        self.sems = {}
        self.groups = {}

    def inp(self, name, shape, dt=F32):
        return self.nc.dram_tensor(name, shape, dt, kind="ExternalInput").ap()

    def out(self, name, shape, dt=F32):
        return self.nc.dram_tensor(name, shape, dt, kind="ExternalOutput").ap()

    def sb(self, name, shape, dt):
        return self.es.enter_context(self.nc.sbuf_tensor(name, shape, dt))

    def ps(self, name, shape, dt=F32):
        return self.es.enter_context(self.nc.psum_tensor(name, shape, dt))

    def sem(self, name):
        h = self.es.enter_context(self.nc.semaphore(name))
        self.sems[name] = h
        self.c[name] = 0
        return h

    def inc(self, instr, name, k):
        instr.then_inc(self.sems[name], k)
        self.c[name] += k
        return self.c[name]

    def block(self):
        return self.es.enter_context(self.nc.Block())

    def dma_group(self, name, k):
        names = ["%s%d" % (name, i) for i in range(k)]
        for n in names:
            self.sem(n)
        self.groups[name] = {"names": names, "i": 0}

    def dma(self, eng, name, out, in_, **kw):
        g = self.groups[name]
        sn = g["names"][g["i"] % len(g["names"])]
        if self.c[sn] > 0:
            eng.wait_ge(self.sems[sn], self.c[sn])
        ins = eng.dma_start(out=out, in_=in_, **kw)
        self.inc(ins, sn, 16)
        g["i"] += 1
        return (sn, self.c[sn])

    def gwait(self, eng, name):
        for sn in self.groups[name]["names"]:
            if self.c[sn] > 0:
                eng.wait_ge(self.sems[sn], self.c[sn])


def _featmaj_ap(dram_ap, d_in, cols):
    return dram_ap.rearrange("(c p) t -> p c t", p=128)


def _chunked_sbuf_ap(t, nchunk, cols):
    return bass.AP(t, 0, [[nchunk * cols, 128], [cols, nchunk], [1, cols]])


def build_s_ck():
    p = P("s_ck")
    nc = p.nc
    xT = p.inp("xT", [D, T])
    Wck = p.inp("Wck", [D, 64])
    bck = p.inp("bck", [64])
    ck_dup = p.out("ck_dup", [T, 128], BF16)
    vhat = p.out("vhat", [128, (T // 128) * 33], BF16)
    NT = T // 128
    with p.es:
        xTb = p.sb("xTb", [128, DCH * T], BF16)
        wckb = p.sb("wckb", [128, DCH * 64], BF16)
        bckb = p.sb("bckb", [1, 64], BF16)
        onesr = p.sb("onesr", [1, 128], BF16)
        ckd = p.sb("ckd", [128, NT * 128], BF16)
        vh = p.sb("vh", [128, NT * 33], BF16)
        pck = [p.ps("pck%d" % i, [128, 64]) for i in range(2)]
        ld = p.sem("ld")
        vm = p.sem("vm")
        mm = p.sem("mm")
        ev = p.sem("ev")
        so = p.sem("so")
        blk = p.block()

        @blk.gpsimd
        def _(g):
            g.dma_start(out=_chunked_sbuf_ap(xTb, DCH, T), in_=_featmaj_ap(xT, D, T)).then_inc(p.sems["ld"], 16)
            g.dma_start(out=_chunked_sbuf_ap(wckb, DCH, 64), in_=_featmaj_ap(Wck, D, 64)).then_inc(p.sems["ld"], 16)
            g.dma_start(out=bckb[:, :], in_=bck[None, :]).then_inc(p.sems["ld"], 16)
        p.c["ld"] = 48

        @blk.vector
        def _(v):
            v.memset(onesr[:, :], 1.0)
            v.memset(bass.AP(vh, 32, [[vh.shape[1], 128], [33, NT], [1, 1]]), 1.0)
            v.memset(bass.AP(ckd, 64, [[ckd.shape[1], 128], [128, NT], [1, 1]]), 1.0)
            v.memset(bass.AP(ckd, 65, [[ckd.shape[1], 128], [128, NT], [1, 63]]), 0.0)
            p.inc(v.engine_nop(), "vm", 1)

        @blk.tensor
        def _(t):
            t.wait_ge(p.sems["ld"], 48)
            t.wait_ge(p.sems["vm"], 1)
            for tk in range(NT):
                if tk >= 2:
                    t.wait_ge(p.sems["ev"], 2 * (tk - 1))
                for c in range(DCH):
                    t.matmul(pck[tk % 2][:, :], xTb[:, c * T + tk * 128: c * T + (tk + 1) * 128],
                             wckb[:, c * 64:(c + 1) * 64], start=(c == 0), stop=False)
                ins = t.matmul(pck[tk % 2][:, :], onesr[:, :], bckb[:, :], start=False, stop=True)
                p.inc(ins, "mm", 1)

        @blk.scalar
        def _(s):
            for tk in range(NT):
                s.wait_ge(p.sems["mm"], tk + 1)
                src = pck[tk % 2]
                s.copy(ckd[:, tk * 128: tk * 128 + 64], src[:, :])
                ins = s.copy(vh[:, tk * 33: tk * 33 + 32], src[:, 0:32])
                p.inc(ins, "ev", 2)

        @blk.sync
        def _(sy):
            sy.wait_ge(p.sems["ev"], 2 * NT)
            sy.wait_ge(p.sems["vm"], 1)
            sy.dma_start(out=ck_dup.rearrange("(k p) c -> p k c", p=128), in_=_chunked_sbuf_ap(ckd, T // 128, 128)).then_inc(p.sems["so"], 16)
            sy.dma_start(out=vhat[:, :], in_=vh[:, :]).then_inc(p.sems["so"], 16)
            sy.wait_ge(p.sems["so"], 32)
    return nc


def build_s_l0attn(upto=99, skip_mid=False):
    p = P("s_l0attn")
    nc = p.nc
    sk = int(skip_mid) if skip_mid else 0
    NKB = S // 256
    H = 8
    xT = p.inp("xT", [D, T])
    ckdup = p.inp("ckdup", [S, 128], BF16)
    vhat = p.inp("vhat", [128, (S // 128) * 33], BF16)
    W_DQ = p.inp("W_DQ", [D, 32]); b_DQ = p.inp("b_DQ", [32])
    W_UQ = p.inp("W_UQ", [32, D]); b_UQ = p.inp("b_UQ", [D])
    W_UKt = p.inp("W_UKt", [D, 32])
    W_QR = p.inp("W_QR", [32, 256]); b_QR = p.inp("b_QR", [256])
    W_UVt = p.inp("W_UVt", [D, 32])
    W_O = p.inp("W_O", [D, D]); b_O = p.inp("b_O", [D]); b_UV = p.inp("b_UV", [D])
    b_UK = p.inp("b_UK", [D])
    x1T = p.out("x1T", [D, T])
    with p.es:
        xq = p.sb("xq", [128, DCH * 512], BF16)
        khat = p.sb("khat", [128, S], BF16)
        vh = p.sb("vh", [128, (S // 128) * 33], BF16)
        wdqb = p.sb("wdqb", [128, DCH * 32], BF16)
        wuqb = p.sb("wuqb", [32, D], BF16)
        wuktb = p.sb("wuktb", [128, DCH * 32], BF16)
        wqrb = p.sb("wqrb", [32, 256], BF16)
        wuvtb = p.sb("wuvtb", [128, DCH * 32], BF16)
        wob = p.sb("wob", [128, DCH * D], BF16)
        bdq = p.sb("bdq", [32, 1], F32)
        buq = p.sb("buq", [128, DCH], F32)
        qbias = p.sb("qbias", [65, H], F32)
        bukb = p.sb("bukb", [128, 4 * DCH], BF16)
        buvb = p.sb("buvb", [128, 4 * DCH], BF16)
        bo_row = p.sb("bo_row", [1, D], F32)
        onesr128 = p.sb("onesr128", [1, 128], BF16)
        ones1x128f = p.sb("ones1x128f", [1, 128], F32)
        ones64c = p.sb("ones64c", [65, 4], BF16)
        ones128c = p.sb("ones128c", [128, 4], BF16)
        ones33 = p.sb("ones33", [33, 32], BF16)
        onesq = p.sb("onesq", [1, 512], BF16)
        cqT = p.sb("cqT", [32, T], BF16)
        qc1 = [p.sb("qc1%d" % i, [128, T], BF16) for i in range(2)]
        qh = p.sb("qh", [65, H * T], BF16)
        woe = p.sb("woe", [32, H * D], BF16)
        beffb = p.sb("beffb", [1, D], BF16)
        beff_f = p.sb("beff_f", [1, D], F32)
        sqt = p.sb("sqt", [65, 512], BF16)
        kn2m = p.sb("kn2m", [1, 16], F32)
        qm2 = p.sb("qm2", [1, 32], F32)
        kmax = p.sb("kmax", [1, 1], F32)
        negu = p.sb("negu", [1, 32], F32)
        negub = p.sb("negub", [1, 32], BF16)
        Ubc = p.sb("Ubc", [128, 32], F32)
        et = [p.sb("et%d" % i, [128, 1024], BF16) for i in range(3)]
        rs = p.sb("rs", [33, 512], F32)
        rsb = p.sb("rsb", [33, 512], BF16)
        pOs = p.sb("pOs", [32, 512], F32)
        onf = p.sb("onf", [32, 512], F32)
        onrm = p.sb("onrm", [32, H * 512], BF16)
        xf = p.sb("xf", [128, DCH * 512], F32)
        ts = p.sb("ts", [128, DCH * 512], F32)
        tb = p.sb("tb", [128, DCH * 512], BF16)
        sqb = p.sb("sqb", [128, DCH * 512], BF16)
        tmpf = p.sb("tmpf", [128, 512], F32)
        x1c = [p.sb("x1c%d" % i, [128, 512], F32) for i in range(2)]
        mrow = p.sb("mrow", [1, 512], F32)
        m2row = p.sb("m2row", [1, 512], F32)
        vrow = p.sb("vrow", [1, 512], F32)
        rrow = p.sb("rrow", [1, 512], F32)
        brow = p.sb("brow", [1, 512], F32)
        bankA = p.ps("bankA", [128, 1024])
        bankB = p.ps("bankB", [128, 1024])
        bankC = p.ps("bankC", [128, 512])
        bankD = p.ps("bankD", [128, 512])
        bankE = p.ps("bankE", [128, 512])
        bankF = p.ps("bankF", [128, 512])
        for s_ in ("vm", "mm", "xp", "dq", "ms", "xe",
                   "mv", "xo", "d6", "mb", "mp", "dva", "xr", "mn"):
            p.sem(s_)
        p.dma_group("ld", 4)
        p.dma_group("lk", 1)
        p.dma_group("lq", 1)
        p.dma_group("lx", 1)
        p.dma_group("so", 2)
        blk = p.block()

        def W(eng, name, val):
            if val > 0:
                eng.wait_ge(p.sems[name], val)

        @blk.gpsimd
        def _(g):
            for out_ap, in_ap, slow in [
                (vh[:, :], vhat[:, :], False),
                (_chunked_sbuf_ap(wdqb, DCH, 32), _featmaj_ap(W_DQ, D, 32), False),
                (wuqb[:, :], W_UQ[:, :], False),
                (_chunked_sbuf_ap(wuktb, DCH, 32), _featmaj_ap(W_UKt, D, 32), False),
                (wqrb[:, :], W_QR[:, :], False),
                (_chunked_sbuf_ap(wuvtb, DCH, 32), _featmaj_ap(W_UVt, D, 32), False),
                (_chunked_sbuf_ap(wob, DCH, D), _featmaj_ap(W_O, D, D), False),
                (bdq[:, :], b_DQ[:, None], False),
                (buq[:, :], b_UQ.rearrange("(c p) -> p c", p=128), True),
                (qbias[32:64, :], b_QR.rearrange("(h r) -> r h", r=32), True),
                (bo_row[:, :], b_O[None, :], False),
            ]:
                p.dma(g, "ld", out_ap, in_ap, allow_slow_non_contiguous=slow)

        @blk.gpsimd
        def _(g):
            g.wait_ge(p.sems["vm"], 1)
            p.dma(g, "ld", bass.AP(buvb, 0, [[4 * DCH, 128], [4, DCH], [1, 1]]),
                  b_UV.rearrange("(c p) -> p c", p=128)[:, :, None],
                  allow_slow_non_contiguous=True)
            p.dma(g, "ld", bass.AP(bukb, 0, [[4 * DCH, 128], [4, DCH], [1, 1]]),
                  b_UK.rearrange("(c p) -> p c", p=128)[:, :, None],
                  allow_slow_non_contiguous=True)

        @blk.sync
        def _(sy):
            p.gwait(sy, "ld")
            if not (sk & 1):
                sy.wait_ge(p.sems["lq0"], 16 * NTT)
            p.dma(sy, "lk", khat[:, :], ckdup[:, :], transpose=True)

        @blk.vector
        def _(v):
            for t_, val in [(onesr128, 1.0), (ones1x128f, 1.0), (ones64c, 1.0),
                            (ones128c, 1.0), (ones33, 1.0), (onesq, 1.0)]:
                v.memset(t_[:, :], val)
            v.memset(qbias[0:32, :], 0.0)
            v.memset(qbias[64:65, :], 0.0)
            v.memset(buvb[:, :], 0.0)
            v.memset(bukb[:, :], 0.0)
            p.inc(v.engine_nop(), "vm", 1)

        NTT_P1 = NTT if (upto >= 2 and not (sk & 1)) else 0
        pc32 = [bankC[0:32, :], bankD[0:32, :]]

        @blk.gpsimd
        def _(g):
            for tt in range(NTT_P1):
                if tt > 0:
                    g.wait_ge(p.sems["mm"], tt)
                p.dma(g, "lq",
                      _chunked_sbuf_ap(xq, DCH, 512),
                      xT.rearrange("(c p) t -> p c t", p=128)[:, :, tt * 512:(tt + 1) * 512])

        @blk.tensor
        def _(t):
            p.gwait(t, "ld")
            t.wait_ge(p.sems["vm"], 1)
            for tt in range(NTT_P1):
                t.wait_ge(p.sems["lq0"], 16 * (tt + 1))
                W(t, "xp", tt - 1)
                for c in range(DCH):
                    ins = t.matmul(pc32[tt % 2], wdqb[:, c * 32:(c + 1) * 32],
                                   xq[:, c * 512:(c + 1) * 512],
                                   start=(c == 0), stop=(c == DCH - 1))
                p.inc(ins, "mm", 1)

        @blk.scalar
        def _(s):
            for tt in range(NTT_P1):
                s.wait_ge(p.sems["mm"], tt + 1)
                ins = s.activation(cqT[:, tt * 512:(tt + 1) * 512], pc32[tt % 2],
                                   AF.Identity, bias=bdq[:, :])
                p.inc(ins, "xp", 1)
        mm_p1 = p.c["mm"]; xp_p1 = p.c["xp"]

        pq = [bankA[:, 0:512], bankB[:, 0:512]]
        pq64 = [(bankA[0:32, 0:512], bankA[32:64, 0:512], bankA[0:64, 0:512]),
                (bankB[0:32, 0:512], bankB[32:64, 0:512], bankB[0:64, 0:512])]
        for h in range(H if (upto >= 3 and not (sk & 2)) else 0):
            mmb = p.c["mm"]; xpb = p.c["xp"]

            @blk.tensor
            def _(t, h=h, mmb=mmb, xpb=xpb):
                for tt in range(NTT):
                    W(t, "xp", max(tt + 1, xpb + tt - 1))
                    ins = t.matmul(pq[tt % 2], wuqb[:, h * 128:(h + 1) * 128],
                                   cqT[:, tt * 512:(tt + 1) * 512], start=True, stop=True)
                    p.inc(ins, "mm", 1)
                for tt in range(NTT):
                    W(t, "xp", max(xpb + tt + 1, xpb + NTT + tt - 1))
                    lo, hi, _full = pq64[tt % 2]
                    bank = bankA if tt % 2 == 0 else bankB
                    t.matmul(lo, wuktb[:, h * 32:(h + 1) * 32],
                             qc1[h % 2][:, tt * 512:(tt + 1) * 512],
                             start=True, stop=True, tile_position=(0, 0))
                    t.matmul(hi, wqrb[:, h * 32:(h + 1) * 32],
                             cqT[:, tt * 512:(tt + 1) * 512],
                             start=True, stop=True, tile_position=(0, 32))
                    ins = t.matmul(bank[64:68, 0:512], bukb[:, h * 4:(h + 1) * 4],
                                   qc1[h % 2][:, tt * 512:(tt + 1) * 512],
                                   start=True, stop=True, tile_position=(0, 64))
                    p.inc(ins, "mm", 1)

            @blk.scalar
            def _(s_, h=h, mmb=mmb, xpb=xpb):
                for tt in range(NTT):
                    s_.wait_ge(p.sems["mm"], mmb + tt + 1)
                    if h >= 2:
                        pass
                    ins = s_.activation(qc1[h % 2][:, tt * 512:(tt + 1) * 512],
                                        pq[tt % 2], AF.Identity, bias=buq[:, h:h + 1])
                    p.inc(ins, "xp", 1)
                for tt in range(NTT):
                    s_.wait_ge(p.sems["mm"], mmb + NTT + tt + 1)
                    bank = bankA if tt % 2 == 0 else bankB
                    ins = s_.activation(qh[:, h * T + tt * 512:h * T + (tt + 1) * 512],
                                        bank[0:65, 0:512], AF.Identity,
                                        bias=qbias[:, h:h + 1])
                    p.inc(ins, "xp", 1)
        mm_p3 = p.c["mm"]; xp_p3 = p.c["xp"]

        @blk.tensor
        def _(t):
            if upto < 4 or (sk & 4):
                return
            t.wait_ge(p.sems["xp"], xp_p3)
            i = 0
            for h in range(H):
                for half in range(2):
                    W(t, "xp", xp_p3 + i - 1)
                    ins = t.matmul(pq64[i % 2][0], wuvtb[:, h * 32:(h + 1) * 32],
                                   wob[:, h * D + half * 512:h * D + (half + 1) * 512],
                                   start=True, stop=True)
                    p.inc(ins, "mm", 1)
                    i += 1
            if sk & 8:
                for half in range(2):
                    p.inc(t.nop(), "mm", 1)
            else:
                for half in range(2):
                    for c in range(DCH):
                        ins = t.matmul(bankC[0:4, :] if half == 0 else bankD[0:4, :],
                                       buvb[:, 4 * c:4 * c + 4],
                                       wob[:, c * D + half * 512:c * D + (half + 1) * 512],
                                       start=(c == 0), stop=(c == DCH - 1))
                    p.inc(ins, "mm", 1)

        @blk.scalar
        def _(s):
            if upto < 4 or (sk & 4):
                return
            i = 0
            for h in range(H):
                for half in range(2):
                    s.wait_ge(p.sems["mm"], mm_p3 + i + 1)
                    ins = s.copy(woe[:, h * D + half * 512:h * D + (half + 1) * 512],
                                 pq64[i % 2][0])
                    p.inc(ins, "xp", 1)
                    i += 1

        @blk.vector
        def _(v):
            if upto < 4 or (sk & 4):
                return
            p.gwait(v, "ld")
            for half in range(2):
                if sk & 24:
                    p.inc(v.engine_nop(), "dq", 1)
                    continue
                v.wait_ge(p.sems["mm"], mm_p3 + 16 + half + 1)
                v.tensor_add(beff_f[:, half * 512:(half + 1) * 512],
                             bankC[0:1, :] if half == 0 else bankD[0:1, :],
                             bo_row[:, half * 512:(half + 1) * 512])
                v.drain()
                ins = v.tensor_copy(beffb[:, half * 512:(half + 1) * 512],
                                    beff_f[:, half * 512:(half + 1) * 512])
                p.inc(ins, "dq", 1)
        mm_p4 = p.c["mm"]; xp_p4 = p.c["xp"]; dq_p4 = p.c["dq"]

        @blk.tensor
        def _(t):
            if upto < 4.05:
                return
            t.wait_ge(p.sems["dq"], dq_p4)
            if abs(upto - 4.13) < 1e-9:
                for kt in range(16):
                    ins = t.matmul(bankE[0:4, :] if kt % 2 == 0 else bankF[0:4, :],
                                   ones64c[:, :], sqt[:, :], start=True, stop=True)
                    p.inc(ins, "mm", 1)
                return
            if upto < 4.15:
                for kt in range(16):
                    p.inc(t.nop(), "mm", 1)
                return
            for kt in range(16):
                t.wait_ge(p.sems["dq"], dq_p4 + 2 * kt + 1)
                ins = t.matmul(bankE[0:4, :] if kt % 2 == 0 else bankF[0:4, :],
                               ones64c[:, :], sqt[:, :], start=True, stop=True)
                p.inc(ins, "mm", 1)

        @blk.vector
        def _(v):
            if abs(upto - 4.12) < 1e-9:
                v.wait_ge(p.sems["lk0"], 16)
                for kt in range(16):
                    ins = v.tensor_mul(sqt[:, :], khat[0:64, kt * 512:(kt + 1) * 512],
                                       khat[0:64, kt * 512:(kt + 1) * 512])
                    p.inc(ins, "dq", 2)
                p.inc(v.engine_nop(), "dq", 1)
                return
            if abs(upto - 4.13) < 1e-9:
                for kt in range(16):
                    p.inc(v.engine_nop(), "dq", 2)
                p.inc(v.engine_nop(), "dq", 1)
                return
            if upto < 4.1:
                if upto >= 4.07:
                    v.wait_ge(p.sems["lk0"], 16)
                if upto >= 4.05:
                    for kt in range(16):
                        p.inc(v.engine_nop(), "dq", 2)
                    p.inc(v.engine_nop(), "dq", 1)
                return
            v.wait_ge(p.sems["lk0"], 16)
            for kt in range(16):
                W(v, "mm", mm_p4 + kt)
                ins = v.tensor_mul(sqt[:, :], khat[0:65, kt * 512:(kt + 1) * 512],
                                   khat[0:65, kt * 512:(kt + 1) * 512])
                p.inc(ins, "dq", 1)
                if upto >= 4.2:
                    v.wait_ge(p.sems["mm"], mm_p4 + kt + 1)
                    ins = v.reduce_max(kn2m[:, kt:kt + 1],
                                       bankE[0:1, :] if kt % 2 == 0 else bankF[0:1, :], axis=AX)
                    p.inc(ins, "dq", 1)
                else:
                    p.inc(v.engine_nop(), "dq", 1)
            v.drain()
            ins = v.reduce_max(kmax[:, :], kn2m[:, :], axis=AX) if upto >= 4.2 else v.engine_nop()
            p.inc(ins, "dq", 1)
        mm_p5a = p.c["mm"]; dq_p5a = p.c["dq"]

        @blk.scalar
        def _(s):
            if upto < 4.4:
                return
            s.wait_ge(p.sems["dq"], dq_p5a)
            s.activation(kmax[:, :], kmax[:, :], AF.Ln)
            s.drain()
            ins = s.activation(kmax[:, :], kmax[:, :], AF.Exp, scale=0.5)
            p.inc(ins, "xp", 1)
        xp_p5a = p.c["xp"]

        @blk.tensor
        def _(t):
            if upto < 4.6:
                return
            for i in range(32):
                t.wait_ge(p.sems["dq"], dq_p5a + 2 * i + 1)
                ins = t.matmul(bankE[0:4, :] if i % 2 == 0 else bankF[0:4, :],
                               ones64c[:, :], sqt[:, :], start=True, stop=True)
                p.inc(ins, "mm", 1)

        @blk.vector
        def _(v):
            if upto < 4.6:
                return
            for i in range(32):
                h, tt = divmod(i, NTT)
                W(v, "mm", mm_p5a + i)
                ins = v.tensor_mul(sqt[:, :], qh[0:65, h * T + tt * 512:h * T + (tt + 1) * 512],
                                   qh[0:65, h * T + tt * 512:h * T + (tt + 1) * 512])
                p.inc(ins, "dq", 1)
                v.wait_ge(p.sems["mm"], mm_p5a + i + 1)
                ins = v.reduce_max(qm2[:, i:i + 1],
                                   bankE[0:1, :] if i % 2 == 0 else bankF[0:1, :], axis=AX)
                p.inc(ins, "dq", 1)
        mm_p5b = p.c["mm"]; dq_p5b = p.c["dq"]

        @blk.scalar
        def _(s):
            if upto < 4.7:
                return
            s.wait_ge(p.sems["dq"], dq_p5b)
            s.activation(qm2[:, :], qm2[:, :], AF.Ln)
            s.drain()
            ins = s.activation(qm2[:, :], qm2[:, :], AF.Exp, scale=0.5)
            p.inc(ins, "xp", 1)
        xp_p5b = p.c["xp"]

        @blk.vector
        def _(v):
            if upto < 4.8:
                return
            v.wait_ge(p.sems["xp"], xp_p5b)
            v.tensor_single_scalar(negu[:, :], qm2[:, :], kmax[0:1, 0:1], ALU.mult)
            v.drain()
            v.tensor_scalar_mul(negu[:, :], negu[:, :], -SCALE0)
            v.drain()
            ins = v.tensor_copy(negub[:, :], negu[:, :])
            p.inc(ins, "dq", 1)
        dq_p5c = p.c["dq"]

        @blk.tensor
        def _(t):
            if upto < 4.9:
                return
            t.wait_ge(p.sems["dq"], dq_p5c)
            ins = t.matmul(bankE[:, 0:32], onesr128[:, :], negub[:, :], start=True, stop=True)
            p.inc(ins, "mm", 1)
        mm_p5c = p.c["mm"]

        @blk.vector
        def _(v):
            if upto < 4.9:
                return
            v.wait_ge(p.sems["mm"], mm_p5c)
            ins = v.tensor_copy(Ubc[:, :], bankE[:, 0:32])
            p.inc(ins, "dq", 1)
        dq_ubc = p.c["dq"]

        if upto < 6:
            NTT_RUN = 0
        elif upto == 6:
            NTT_RUN = 1
        else:
            NTT_RUN = NTT
        pS = [bankA, bankB]
        pO = bankC[0:33, :]
        pB = bankD[0:32, :]
        dva_prev_end = p.c["dva"]

        for tt in range(NTT_RUN):

            @blk.gpsimd
            def _(g, tt=tt, dva_prev_end=dva_prev_end):
                if tt == 0:
                    g.wait_ge(p.sems["lk0"], 16)
                if tt > 0:
                    g.wait_ge(p.sems["dva"], dva_prev_end)
                p.dma(g, "lx",
                      _chunked_sbuf_ap(xf, DCH, 512),
                      xT.rearrange("(c p) t -> p c t", p=128)[:, :, tt * 512:(tt + 1) * 512])
            ld_xf = p.c["lx0"]

            for h in range(H if upto > 6 else 2):
                U = tt * H + h
                msb = p.c["ms"]; xeb = p.c["xe"]; mvb = p.c["mv"]
                xob = p.c["xo"]; d6b = p.c["d6"]

                @blk.tensor
                def _(t, tt=tt, h=h, U=U, msb=msb, xeb=xeb, mvb=mvb, xob=xob, d6b=d6b):
                    def value_mms(cb):
                        t.wait_ge(p.sems["xe"], xeb + cb + 1)
                        for j in range(2):
                            kc = 2 * cb + j
                            ins = t.matmul(pO, vh[:, kc * 33:(kc + 1) * 33],
                                           et[(xeb + cb) % 3][:, j * 512:(j + 1) * 512],
                                           start=(cb == 0 and j == 0),
                                           stop=(cb == NKB - 1 and j == 1))
                        p.inc(ins, "mv", 1)

                    for cb in range(NKB):
                        W(t, "xe", xeb + cb - 1)
                        if cb == 0:
                            if U > 0:
                                t.wait_ge(p.sems["xo"], xob)
                                W(t, "d6", d6b - 2)
                            else:
                                t.wait_ge(p.sems["xp"], xp_p4)
                                t.wait_ge(p.sems["dq"], dq_ubc)
                                t.wait_ge(p.sems["lk0"], 16)
                        for j in range(2):
                            kc = 2 * cb + j
                            ins = t.matmul(pS[(xeb + cb) % 2][:, j * 512:(j + 1) * 512],
                                           khat[0:65, kc * 128:(kc + 1) * 128],
                                           qh[0:65, h * T + tt * 512:h * T + (tt + 1) * 512],
                                           start=True, stop=True)
                        p.inc(ins, "ms", 1)
                        if cb >= 2:
                            value_mms(cb - 2)
                    value_mms(NKB - 2)
                    value_mms(NKB - 1)
                    t.wait_ge(p.sems["d6"], d6b + 2)
                    ins = t.matmul(pB, ones33[32:33, :], rsb[32:33, :],
                                   start=True, stop=True, tile_position=(32, 0))
                    p.inc(ins, "mb", 1)

                @blk.scalar
                def _(s, tt=tt, h=h, U=U, msb=msb, xeb=xeb, mvb=mvb, d6b=d6b):
                    for cb in range(NKB):
                        s.wait_ge(p.sems["ms"], msb + cb + 1)
                        W(s, "mv", mvb + cb - 2)
                        ins = s.activation(et[(xeb + cb) % 3][:, :], pS[(xeb + cb) % 2][:, :],
                                           AF.Exp, bias=Ubc[:, h * NTT + tt:h * NTT + tt + 1],
                                           scale=SCALE0)
                        p.inc(ins, "xe", 1)
                    s.wait_ge(p.sems["mv"], mvb + NKB)
                    W(s, "d6", d6b)
                    ins = s.copy(pOs[:, :], pO[0:32, :])
                    p.inc(ins, "xo", 1)

                @blk.vector
                def _(v, tt=tt, h=h, U=U, mvb=mvb, xob=xob):
                    v.wait_ge(p.sems["mv"], mvb + NKB)
                    ins = v.reciprocal(rs[32:33, :], pO[32:33, :])
                    p.inc(ins, "d6", 1)
                    v.drain()
                    ins = v.tensor_copy(rsb[32:33, :], rs[32:33, :])
                    p.inc(ins, "d6", 1)
                    v.wait_ge(p.sems["xo"], xob + 1)
                    v.wait_ge(p.sems["mb"], U + 1)
                    v.tensor_mul(onf[:, :], pOs[:, :], pB)
                    v.drain()
                    ins = v.tensor_copy(onrm[:, h * 512:(h + 1) * 512], onf[:, :])
                    p.inc(ins, "d6", 1)

            if upto == 6:
                continue
            d6_done = p.c["d6"]
            xe_done = p.c["xe"]
            mpb = p.c["mp"]
            dvab = p.c["dva"]
            pA = [bankA[:, 0:512], bankB[:, 0:512]]
            pSum = bankE[0:4, :]
            pSq = bankF[0:4, :]

            @blk.tensor
            def _(t, tt=tt, mpb=mpb, dvab=dvab, d6_done=d6_done, xe_done=xe_done,
                  ld_xf=ld_xf, dva_prev_end=dva_prev_end):
                t.wait_ge(p.sems["d6"], d6_done)
                t.wait_ge(p.sems["xe"], xe_done)
                t.wait_ge(p.sems["lx0"], ld_xf)
                for c in range(DCH):
                    W(t, "dva", dvab + 3 * (c - 2) + 1 if c >= 2 else 0)
                    for h in range(H):
                        t.matmul(pA[c % 2], woe[:, h * D + c * 128:h * D + (c + 1) * 128],
                                 onrm[:, h * 512:(h + 1) * 512], start=(h == 0), stop=False)
                    ins = t.matmul(pA[c % 2], beffb[:, c * 128:(c + 1) * 128], onesq[:, :],
                                   start=False, stop=True)
                    p.inc(ins, "mp", 1)

            @blk.vector
            def _(v, tt=tt, mpb=mpb):
                for c in range(DCH):
                    v.wait_ge(p.sems["mp"], mpb + c + 1)
                    ins = v.tensor_add(ts[:, c * 512:(c + 1) * 512],
                                       xf[:, c * 512:(c + 1) * 512], pA[c % 2])
                    p.inc(ins, "dva", 1)
                    v.drain()
                    ins = v.tensor_copy(tb[:, c * 512:(c + 1) * 512],
                                        ts[:, c * 512:(c + 1) * 512])
                    p.inc(ins, "dva", 1)
                    v.drain()
                    ins = v.tensor_mul(sqb[:, c * 512:(c + 1) * 512],
                                       tb[:, c * 512:(c + 1) * 512],
                                       tb[:, c * 512:(c + 1) * 512])
                    p.inc(ins, "dva", 1)

            @blk.tensor
            def _(t, tt=tt, mpb=mpb, dvab=dvab, dva_prev_end=dva_prev_end):
                if tt == 0:
                    t.wait_ge(p.sems["dq"], dq_ubc)
                for c in range(DCH):
                    t.wait_ge(p.sems["dva"], dvab + 3 * c + 3)
                    t.matmul(pSum, ones128c[:, :], tb[:, c * 512:(c + 1) * 512],
                             start=(c == 0), stop=(c == DCH - 1), skip_group_check=True)
                    ins = t.matmul(pSq, ones128c[:, :], sqb[:, c * 512:(c + 1) * 512],
                                   start=(c == 0), stop=(c == DCH - 1), skip_group_check=True)
                p.inc(ins, "mp", 1)

            @blk.vector
            def _(v, tt=tt, mpb=mpb):
                v.wait_ge(p.sems["mp"], mpb + DCH + 1)
                v.tensor_scalar_mul(mrow[:, :], pSum[0:1, :], 1.0 / D)
                v.drain()
                v.tensor_mul(m2row[:, :], mrow[:, :], mrow[:, :])
                v.drain()
                v.scalar_tensor_tensor(vrow[:, :], pSq[0:1, :], 1.0 / D, m2row[:, :],
                                       ALU.mult, ALU.subtract)
                v.drain()
                ins = v.tensor_scalar_add(vrow[:, :], vrow[:, :], LN_EPS)
                p.inc(ins, "dva", 1)

            @blk.scalar
            def _(s, tt=tt, dvab=dvab):
                s.wait_ge(p.sems["dva"], dvab + 25)
                s.activation(vrow[:, :], vrow[:, :], AF.Ln)
                s.drain()
                ins = s.activation(rrow[:, :], vrow[:, :], AF.Exp, scale=-0.5)
                p.inc(ins, "xr", 1)

            @blk.vector
            def _(v, tt=tt):
                v.wait_ge(p.sems["xr"], tt + 1)
                v.drain()
                ins = v.tensor_mul(brow[:, :], mrow[:, :], rrow[:, :])
                p.inc(ins, "dva", 1)

            @blk.tensor
            def _(t, tt=tt, dvab=dvab):
                t.wait_ge(p.sems["dva"], dvab + 26)
                t.matmul(bankE[:, :], ones1x128f[:, :], rrow[:, :], start=True, stop=True)
                ins = t.matmul(bankF[:, :], ones1x128f[:, :], brow[:, :], start=True, stop=True)
                p.inc(ins, "mn", 1)

            @blk.vector
            def _(v, tt=tt, dvab=dvab):
                v.wait_ge(p.sems["mn"], tt + 1)
                for c in range(DCH):
                    n_prev = tt * DCH + c - 2
                    if n_prev >= 0:
                        v.wait_ge(p.sems["so%d" % (n_prev % 2)], 16 * (n_prev // 2 + 1))
                    v.tensor_mul(tmpf[:, :], ts[:, c * 512:(c + 1) * 512], bankE[:, 0:512])
                    v.drain()
                    ins = v.tensor_sub(x1c[c % 2][:, :], tmpf[:, :], bankF[:, 0:512])
                    p.inc(ins, "dva", 1)
                    v.drain()

            @blk.sync
            def _(sy, tt=tt, dvab=dvab):
                for c in range(DCH):
                    sy.wait_ge(p.sems["dva"], dvab + 26 + c + 1)
                    p.dma(sy, "so", x1T[c * 128:(c + 1) * 128, tt * 512:(tt + 1) * 512],
                          x1c[c % 2][:, :])
            dva_prev_end = p.c["dva"]

        @blk.gpsimd
        def _(g):
            p.gwait(g, "so")
    return nc


def emit_ln_tail(p, blk, dvab, tt, ts, tb, sqb, tmpf, x1c, rows, ones128c,
                 ones1x128f, bankE, bankF, outT):
    mrow, m2row, vrow, rrow, brow = rows
    mpb = p.c["mp"]
    pSum = bankE[0:4, :]
    pSq = bankF[0:4, :]

    @blk.tensor
    def _(t):
        for c in range(DCH):
            t.wait_ge(p.sems["dva"], dvab + 3 * c + 3)
            t.matmul(pSum, ones128c[:, 0:4], tb[:, c * 512:(c + 1) * 512],
                     start=(c == 0), stop=(c == DCH - 1), skip_group_check=True)
            ins = t.matmul(pSq, ones128c[:, 0:4], sqb[:, c * 512:(c + 1) * 512],
                           start=(c == 0), stop=(c == DCH - 1), skip_group_check=True)
        p.inc(ins, "mp", 1)
    mp_stats = p.c["mp"]

    @blk.vector
    def _(v):
        v.wait_ge(p.sems["mp"], mp_stats)
        v.tensor_scalar_mul(mrow[:, :], pSum[0:1, :], 1.0 / D)
        v.drain()
        v.tensor_mul(m2row[:, :], mrow[:, :], mrow[:, :])
        v.drain()
        v.scalar_tensor_tensor(vrow[:, :], pSq[0:1, :], 1.0 / D, m2row[:, :],
                               ALU.mult, ALU.subtract)
        v.drain()
        ins = v.tensor_scalar_add(vrow[:, :], vrow[:, :], LN_EPS)
        p.inc(ins, "dva", 1)

    @blk.scalar
    def _(s):
        s.wait_ge(p.sems["dva"], dvab + 25)
        s.activation(vrow[:, :], vrow[:, :], AF.Ln)
        s.drain()
        ins = s.activation(rrow[:, :], vrow[:, :], AF.Exp, scale=-0.5)
        p.inc(ins, "xr", 1)
    xr_now = p.c["xr"]

    @blk.vector
    def _(v):
        v.wait_ge(p.sems["xr"], xr_now)
        v.drain()
        ins = v.tensor_mul(brow[:, :], mrow[:, :], rrow[:, :])
        p.inc(ins, "dva", 1)

    @blk.tensor
    def _(t):
        t.wait_ge(p.sems["dva"], dvab + 26)
        t.matmul(bankE[:, 0:512], ones1x128f[:, :], rrow[:, :], start=True, stop=True)
        ins = t.matmul(bankF[:, 0:512], ones1x128f[:, :], brow[:, :], start=True, stop=True)
        p.inc(ins, "mn", 1)
    mn_now = p.c["mn"]
    n_base = p.groups["so"]["i"]

    @blk.vector
    def _(v):
        v.wait_ge(p.sems["mn"], mn_now)
        for c in range(DCH):
            n_prev = n_base + c - 2
            if n_prev >= 0:
                v.wait_ge(p.sems["so%d" % (n_prev % 2)], 16 * (n_prev // 2 + 1))
            v.tensor_mul(tmpf[:, :], ts[:, c * 512:(c + 1) * 512], bankE[:, 0:512])
            v.drain()
            ins = v.tensor_sub(x1c[c % 2][:, :], tmpf[:, :], bankF[:, 0:512])
            p.inc(ins, "dva", 1)
            v.drain()

    @blk.sync
    def _(sy):
        for c in range(DCH):
            sy.wait_ge(p.sems["dva"], dvab + 26 + c + 1)
            p.dma(sy, "so", outT[c * 128:(c + 1) * 128, tt * 512:(tt + 1) * 512],
                  x1c[c % 2][:, :])


def emit_ln_tail2(p, blk, dvab, mpb, tt, ts, tb, sqb, tmpf, x1c, rows, ones128c,
                  ones1x128f, bankE, bankF, outT):
    mrow, m2row, vrow, rrow, brow = rows
    pSum = bankE[0:4, :]
    pSq = bankF[0:4, :]
    mp_stats = p.c["mp"]

    @blk.vector
    def _(v):
        v.wait_ge(p.sems["mp"], mp_stats)
        v.tensor_scalar_mul(mrow[:, :], pSum[0:1, :], 1.0 / D)
        v.drain()
        v.tensor_mul(m2row[:, :], mrow[:, :], mrow[:, :])
        v.drain()
        v.scalar_tensor_tensor(vrow[:, :], pSq[0:1, :], 1.0 / D, m2row[:, :],
                               ALU.mult, ALU.subtract)
        v.drain()
        ins = v.tensor_scalar_add(vrow[:, :], vrow[:, :], LN_EPS)
        p.inc(ins, "dva", 1)

    @blk.scalar
    def _(s):
        s.wait_ge(p.sems["dva"], dvab + 25)
        s.activation(vrow[:, :], vrow[:, :], AF.Ln)
        s.drain()
        ins = s.activation(rrow[:, :], vrow[:, :], AF.Exp, scale=-0.5)
        p.inc(ins, "xr", 1)
    xr_now = p.c["xr"]

    @blk.vector
    def _(v):
        v.wait_ge(p.sems["xr"], xr_now)
        v.drain()
        ins = v.tensor_mul(brow[:, :], mrow[:, :], rrow[:, :])
        p.inc(ins, "dva", 1)

    @blk.tensor
    def _(t):
        t.wait_ge(p.sems["dva"], dvab + 26)
        t.matmul(bankE[:, 0:512], ones1x128f[:, :], rrow[:, :], start=True, stop=True)
        ins = t.matmul(bankF[:, 0:512], ones1x128f[:, :], brow[:, :], start=True, stop=True)
        p.inc(ins, "mn", 1)
    mn_now = p.c["mn"]
    n_base = p.groups["so"]["i"]

    @blk.vector
    def _(v):
        v.wait_ge(p.sems["mn"], mn_now)
        for c in range(DCH):
            n_prev = n_base + c - 2
            if n_prev >= 0:
                v.wait_ge(p.sems["so%d" % (n_prev % 2)], 16 * (n_prev // 2 + 1))
            v.tensor_mul(tmpf[:, :], ts[:, c * 512:(c + 1) * 512], bankE[:, 0:512])
            v.drain()
            ins = v.tensor_sub(x1c[c % 2][:, :], tmpf[:, :], bankF[:, 0:512])
            p.inc(ins, "dva", 1)
            v.drain()

    @blk.sync
    def _(sy):
        for c in range(DCH):
            sy.wait_ge(p.sems["dva"], dvab + 26 + c + 1)
            p.dma(sy, "so", outT[c * 128:(c + 1) * 128, tt * 512:(tt + 1) * 512],
                  x1c[c % 2][:, :])


def build_s_l0ffn():
    p = P("s_l0ffn")
    nc = p.nc
    E = 8
    HT = 8
    NT = 16
    NG = 2
    KC = 17
    x1T = p.inp("x1T", [D, T])
    Wgt = p.inp("Wgt", [D, E])
    W1t = p.inp("W1t", [E, D, 4096])
    b1 = p.inp("b1", [E, 4096])
    WotP = p.inp("WotP", [KC * 128, D])
    x2T = p.out("x2T", [D, T])
    with p.es:
        x1Tb = p.sb("x1Tb", [128, DCH * T], BF16)
        wgtf = p.sb("wgtf", [128, DCH * E], F32)
        xrc = [p.sb("xrc0", [128, DCH * 128], F32)] * 2
        b1buf = [p.sb("b1buf%d" % i, [1, 4096], BF16) for i in range(2)]
        w1buf = [p.sb("w1buf%d" % i, [128, DCH * 512], BF16) for i in range(2)]
        wotbuf = [p.sb("wotbuf%d" % i, [128, KC * 128], BF16) for i in range(2)]
        sgb = p.sb("sgb", [128, 8 * 2048], BF16)
        gsum = p.sb("gsum", [128, 8 * (2048 + 128)], BF16)
        gsumT = p.sb("gsumT", [128, KC * 512], BF16)
        uf = p.sb("uf", [128, 512], F32)
        identb = p.sb("identb", [128, 128], BF16)
        onesr = p.sb("onesr", [1, 128], BF16)
        ones128c = p.sb("ones128c", [128, 4], BF16)
        ones1x128f = p.sb("ones1x128f", [1, 128], F32)
        nm = p.sb("nm", [128, 1], F32)
        e8 = p.sb("e8", [128, E], F32)
        se = p.sb("se", [128, 1], F32)
        rcp = p.sb("rcp", [128, 1], F32)
        gs = p.sb("gs", [128, E], F32)
        m1v = p.sb("m1v", [128, 1], F32)
        m2v = p.sb("m2v", [128, 1], F32)
        msk1 = p.sb("msk1", [128, E], F32)
        msk2 = p.sb("msk2", [128, E], F32)
        g2 = p.sb("g2", [128, E], F32)
        den = p.sb("den", [128, 1], F32)
        w1v = p.sb("w1v", [128, 1], F32)
        w2v = p.sb("w2v", [128, 1], F32)
        t1m = p.sb("t1m", [128, E], F32)
        comb = p.sb("comb", [128, NT * E], F32)
        csum = p.sb("csum", [128, NT], F32)
        xf = p.sb("xf", [128, DCH * 512], F32)
        ts = p.sb("ts", [128, DCH * 512], F32)
        tb = p.sb("tb", [128, 512], BF16)
        sqb = p.sb("sqb", [128, 512], BF16)
        tmpf = uf
        x1c = [p.sb("x1c%d" % i, [128, 512], F32) for i in range(2)]
        _r = {n: p.sb(n, [1, 512], F32) for n in ("mrow", "m2row", "vrow")}
        rows = (_r["mrow"], _r["m2row"], _r["vrow"], _r["vrow"], _r["m2row"])
        bankA = p.ps("bankA", [128, 512])
        bankB = p.ps("bankB", [128, 512])
        bankC = p.ps("bankC", [128, 512])
        bankD = p.ps("bankD", [128, 512])
        bankE = p.ps("bankE", [128, 512])
        bankF = p.ps("bankF", [128, 512])
        bankG = p.ps("bankG", [128, 128], BF16)
        bankH = p.ps("bankH", [128, 128], BF16)
        for s_ in ("vm", "mm", "xp", "dq", "m1", "sg", "da", "mt", "xt", "m2",
                   "mp", "dva", "xr", "mn"):
            p.sem(s_)
        p.dma_group("ld", 4)
        p.dma_group("lr", 1)
        p.dma_group("lb", 2)
        p.dma_group("lw", 2)
        p.dma_group("lo", 2)
        p.dma_group("lx", 1)
        p.dma_group("so", 2)
        blk = p.block()

        def W(eng, name, val):
            if val > 0:
                eng.wait_ge(p.sems[name], val)

        @blk.gpsimd
        def _(g):
            p.dma(g, "ld", _chunked_sbuf_ap(x1Tb, DCH, T), _featmaj_ap(x1T, D, T))
            p.dma(g, "ld", _chunked_sbuf_ap(wgtf, DCH, E), _featmaj_ap(Wgt, D, E))


        @blk.vector
        def _(v):
            v.memset(onesr[:, :], 1.0)
            v.memset(ones128c[:, :], 1.0)
            v.memset(ones1x128f[:, :], 1.0)
            v.memset(identb[:, :], 1.0)
            v.memset(bass.AP(gsum, 2049, [[gsum.shape[1], 128], [2176, 8], [1, 127]]), 0.0)
            p.inc(v.engine_nop(), "vm", 1)

        @blk.gpsimd
        def _(g):
            g.wait_ge(p.sems["vm"], 1)
            ins = g.affine_select(identb[:, :], identb[:, :], [[-1, 128]],
                                  ALU.is_equal, 0.0, channel_multiplier=1)
            p.inc(ins, "vm", 1)

        pR = [bankC[:, 0:E], bankD[:, 0:E]]

        @blk.gpsimd
        def _(g):
            for tc in range(NT):
                if tc >= 1:
                    g.wait_ge(p.sems["mm"], tc)
                p.dma(g, "lr", _chunked_sbuf_ap(xrc[tc % 2], DCH, 128),
                      x1T.rearrange("(c p) t -> p c t", p=128)[:, :, tc * 128:(tc + 1) * 128])

        @blk.tensor
        def _(t):
            p.gwait(t, "ld")
            t.wait_ge(p.sems["vm"], 1)
            for tc in range(NT):
                t.wait_ge(p.sems["lr0"], 16 * (tc + 1))
                W(t, "xp", tc - 1)
                for dc in range(DCH):
                    ins = t.matmul(pR[tc % 2], xrc[tc % 2][:, dc * 128:(dc + 1) * 128],
                                   wgtf[:, dc * E:(dc + 1) * E],
                                   start=(dc == 0), stop=(dc == DCH - 1))
                p.inc(ins, "mm", 1)

        @blk.scalar
        def _(s):
            for tc in range(NT):
                s.wait_ge(p.sems["dq"], 18 * tc + 1)
                ins = s.activation(e8[:, :], pR[tc % 2], AF.Exp, bias=nm[:, :],
                                   accum_out=se[:, :])
                p.inc(ins, "xp", 1)

        @blk.vector
        def _(v):
            for tc in range(NT):
                v.wait_ge(p.sems["mm"], tc + 1)
                W(v, "xp", tc)
                ins = v.reduce_max(nm[:, :], pR[tc % 2], axis=AX, negate=True)
                p.inc(ins, "dq", 1)
                v.wait_ge(p.sems["xp"], tc + 1)
                v.reciprocal(rcp[:, :], se[:, :])
                v.drain()
                v.tensor_single_scalar(gs[:, :], e8[:, :], rcp[0:128, 0:1], ALU.mult)
                v.drain()
                v.reduce_max(m1v[:, :], gs[:, :], axis=AX)
                v.drain()
                v.tensor_single_scalar(msk1[:, :], gs[:, :], m1v[0:128, 0:1], ALU.is_ge)
                v.drain()
                v.scalar_tensor_tensor(g2[:, :], msk1[:, :], -1e30, gs[:, :],
                                       ALU.mult, ALU.add)
                v.drain()
                v.reduce_max(m2v[:, :], g2[:, :], axis=AX)
                v.drain()
                v.tensor_single_scalar(msk2[:, :], g2[:, :], m2v[0:128, 0:1], ALU.is_ge)
                v.tensor_add(den[:, :], m1v[:, :], m2v[:, :])
                v.drain()
                v.tensor_scalar_add(den[:, :], den[:, :], 1e-9)
                v.drain()
                v.reciprocal(den[:, :], den[:, :])
                v.drain()
                v.tensor_single_scalar(w1v[:, :], m1v[:, :], den[0:128, 0:1], ALU.mult)
                v.tensor_single_scalar(w2v[:, :], m2v[:, :], den[0:128, 0:1], ALU.mult)
                v.drain()
                v.tensor_single_scalar(t1m[:, :], msk1[:, :], w1v[0:128, 0:1], ALU.mult)
                v.tensor_add(csum[:, tc:tc + 1], w1v[:, :], w2v[:, :])
                v.drain()
                ins = v.scalar_tensor_tensor(comb[:, tc * E:(tc + 1) * E], msk2[:, :],
                                             w2v[0:128, 0:1], t1m[:, :], ALU.mult, ALU.add)
                p.inc(ins, "dq", 17)
        mm_rt = p.c["mm"]; dq_rt = p.c["dq"]; xp_rt = p.c["xp"]

        ht_seq = [4, 5, 6, 7, 0, 1, 2, 3]
        pH = [bankA, bankB]
        hfree = {0: None, 1: None}
        for g_i in range(NG):
            for e in range(E):
                @blk.gpsimd
                def _(g, g_i=g_i, e=e, m1b_=None):
                    if g_i == 0 and e == 0:
                        g.wait_ge(p.sems["vm"], 2)
                    m1_now = p.c["m1"]
                    ei = g_i * E + e
                    if ei >= 2:
                        g.wait_ge(p.sems["m1"], m1_now - 64)
                    p.dma(g, "lb", b1buf[ei % 2][:, :], b1[e, :][None, :])
                    for i, ht in enumerate(ht_seq):
                        gi = (g_i * E + e) * HT + i
                        if gi >= 2:
                            g.wait_ge(p.sems["m1"], (gi - 1) * 8)
                        p.dma(g, "lw", _chunked_sbuf_ap(w1buf[gi % 2], DCH, 512),
                              _featmaj_ap(W1t[e, :, :], D, 4096)[:, :, ht * 512:(ht + 1) * 512])

                m1b = p.c["m1"]; sgb_ = p.c["sg"]; dab = p.c["da"]

                @blk.tensor
                def _(t, g_i=g_i, e=e, m1b=m1b, sgb_=sgb_, dab=dab):
                    ei = g_i * E + e
                    t.wait_ge(p.sems["lb%d" % (ei % 2)], 16 * (ei // 2 + 1))
                    for i, ht in enumerate(ht_seq):
                        gi = ei * HT + i
                        t.wait_ge(p.sems["lw%d" % (gi % 2)], 16 * (gi // 2 + 1))
                        for tcg in range(8):
                            tc = g_i * 8 + tcg
                            k = i * 8 + tcg
                            par = k % 2
                            tok = hfree[par]
                            if tok is not None:
                                t.wait_ge(p.sems[tok[0]], tok[1])
                            for dc in range(DCH):
                                t.matmul(pH[par][:, :],
                                         x1Tb[:, dc * T + tc * 128:dc * T + (tc + 1) * 128],
                                         w1buf[gi % 2][:, dc * 512:(dc + 1) * 512],
                                         start=(dc == 0), stop=False)
                            ins = t.matmul(pH[par][:, :], onesr[:, :],
                                           b1buf[(g_i * E + e) % 2][:, ht * 512:(ht + 1) * 512],
                                           start=False, stop=True)
                            p.inc(ins, "m1", 1)
                            if i < 4:
                                hfree[par] = ("sg", sgb_ + i * 8 + tcg + 1)
                            else:
                                hfree[par] = ("da", dab + (i - 4) * 8 + tcg + 1)

                @blk.scalar
                def _(s, g_i=g_i, e=e, m1b=m1b, dab=dab):
                    W(s, "da", dab)
                    for i in range(4):
                        for tcg in range(8):
                            s.wait_ge(p.sems["m1"], m1b + i * 8 + tcg + 1)
                            par = (i * 8 + tcg) % 2
                            ins = s.activation(
                                sgb[:, tcg * 2048 + i * 512:tcg * 2048 + (i + 1) * 512],
                                pH[par][:, :], AF.Sigmoid)
                            p.inc(ins, "sg", 1)

                @blk.vector
                def _(v, g_i=g_i, e=e, m1b=m1b, dab=dab, sgb_=sgb_):
                    if g_i > 0 and e == 0:
                        v.wait_ge(p.sems["mt"], 136 * g_i)
                    for i in range(4, 8):
                        ht = ht_seq[i]
                        for tcg in range(8):
                            tc = g_i * 8 + tcg
                            v.wait_ge(p.sems["m1"], m1b + i * 8 + tcg + 1)
                            v.wait_ge(p.sems["sg"], sgb_ + ht * 8 + tcg + 1)
                            par = (i * 8 + tcg) % 2
                            v.tensor_mul(uf[:, :], pH[par][:, :],
                                         sgb[:, tcg * 2048 + ht * 512:tcg * 2048 + (ht + 1) * 512])
                            v.drain()
                            dst = gsum[:, tcg * 2176 + ht * 512:tcg * 2176 + (ht + 1) * 512]
                            ce = comb[:, tc * E + e:tc * E + e + 1]
                            if e == 0:
                                ins = v.tensor_single_scalar(dst, uf[:, :], ce, ALU.mult)
                            else:
                                ins = v.scalar_tensor_tensor(dst, uf[:, :], ce, dst,
                                                             ALU.mult, ALU.add)
                            p.inc(ins, "da", 1)
                            v.drain()

            da_exp = p.c["da"]

            @blk.vector
            def _(v, g_i=g_i, da_exp=da_exp):
                for tcg in range(8):
                    tc = g_i * 8 + tcg
                    ins = v.tensor_copy(gsum[:, tcg * 2176 + 2048:tcg * 2176 + 2049],
                                        csum[:, tc:tc + 1])
                    v.drain()
                p.inc(ins, "da", 1)
            da_cs = p.c["da"]

            pT = [bankG, bankH]
            for tloc in range(2):
                tt = g_i * 2 + tloc
                mtb = p.c["mt"]; xtb = p.c["xt"]

                @blk.gpsimd
                def _(g, tt=tt):
                    if tt > 0:
                        g.wait_ge(p.sems["dva"], tt * 34)
                    p.dma(g, "lx", _chunked_sbuf_ap(xf, DCH, 512),
                          x1T.rearrange("(c p) t -> p c t", p=128)[:, :, tt * 512:(tt + 1) * 512])
                    for oc in range(DCH):
                        n = tt * DCH + oc
                        if n >= 2:
                            g.wait_ge(p.sems["m2"], n - 1)
                        p.dma(g, "lo", _chunked_sbuf_ap(wotbuf[n % 2], KC, 128),
                              _featmaj_ap(WotP[:, oc * 128:(oc + 1) * 128], KC * 128, 128))

                @blk.tensor
                def _(t, g_i=g_i, tloc=tloc, da_cs=da_cs, mtb=mtb, xtb=xtb):
                    for tq in range(4):
                        tcg = tloc * 4 + tq
                        for ch in range(KC):
                            k = tq * KC + ch
                            t.wait_ge(p.sems["da"], da_cs)
                            W(t, "xt", xtb + k - 1)
                            ins = t.transpose(pT[k % 2][:, :],
                                              gsum[:, tcg * 2176 + ch * 128:tcg * 2176 + (ch + 1) * 128],
                                              identb[:, :])
                            p.inc(ins, "mt", 1)

                @blk.scalar
                def _(s, tloc=tloc, mtb=mtb):
                    for tq in range(4):
                        for ch in range(KC):
                            k = tq * KC + ch
                            s.wait_ge(p.sems["mt"], mtb + k + 1)
                            ins = s.copy(gsumT[:, ch * 512 + tq * 128:ch * 512 + (tq + 1) * 128],
                                         pT[k % 2][:, :])
                            p.inc(ins, "xt", 1)
                xt_done = p.c["xt"]

                pX = [bankC, bankD]
                m2b = p.c["m2"]
                dvab = p.c["dva"]
                mpb = p.c["mp"]

                @blk.tensor
                def _(t, tt=tt, xt_done=xt_done, m2b=m2b, dvab=dvab):
                    t.wait_ge(p.sems["xt"], xt_done)
                    if tt == 0:
                        t.wait_ge(p.sems["xp"], NT)

                    def stats(c):
                        t.wait_ge(p.sems["dva"], dvab + 3 * c + 3)
                        t.matmul(bankE[0:4, :], ones128c[:, 0:4], tb[:, :],
                                 start=(c == 0), stop=(c == DCH - 1), skip_group_check=True)
                        ins = t.matmul(bankF[0:4, :], ones128c[:, 0:4], sqb[:, :],
                                       start=(c == 0), stop=(c == DCH - 1), skip_group_check=True)
                        p.inc(ins, "mp", 1)

                    for oc in range(DCH):
                        n = tt * DCH + oc
                        t.wait_ge(p.sems["lo%d" % (n % 2)], 16 * (n // 2 + 1))
                        if oc >= 2:
                            t.wait_ge(p.sems["dva"], dvab + 3 * (oc - 2) + 1)
                        elif tt > 0:
                            t.wait_ge(p.sems["dva"], dvab - 10)
                        for ch in range(KC):
                            ins = t.matmul(pX[oc % 2][:, :],
                                           wotbuf[n % 2][:, ch * 128:(ch + 1) * 128],
                                           gsumT[:, ch * 512:(ch + 1) * 512],
                                           start=(ch == 0), stop=(ch == KC - 1))
                        p.inc(ins, "m2", 1)
                        if oc >= 2:
                            stats(oc - 2)
                    stats(DCH - 2)
                    stats(DCH - 1)

                @blk.vector
                def _(v, tt=tt, m2b=m2b, mpb=mpb):
                    v.wait_ge(p.sems["lx0"], 16 * (tt + 1))
                    for oc in range(DCH):
                        v.wait_ge(p.sems["m2"], m2b + oc + 1)
                        W(v, "mp", mpb + oc)
                        ins = v.tensor_add(ts[:, oc * 512:(oc + 1) * 512],
                                           xf[:, oc * 512:(oc + 1) * 512], pX[oc % 2][:, :])
                        p.inc(ins, "dva", 1)
                        v.drain()
                        ins = v.tensor_copy(tb[:, :], ts[:, oc * 512:(oc + 1) * 512])
                        p.inc(ins, "dva", 1)
                        v.drain()
                        ins = v.tensor_mul(sqb[:, :], tb[:, :], tb[:, :])
                        p.inc(ins, "dva", 1)

                emit_ln_tail2(p, blk, dvab, mpb, tt, ts, tb, sqb, tmpf, x1c, rows,
                              ones128c, ones1x128f, bankE, bankF, x2T)

        @blk.gpsimd
        def _(g):
            p.gwait(g, "so")
    return nc


def build_s_l1qkv():
    p = P("s_l1qkv")
    nc = p.nc
    x2T = p.inp("x2T", [D, T])
    W_qkv = p.inp("W_qkv", [D, D])
    qkv = p.out("qkv", [T, D], BF16)
    kno = p.out("kno", [128, (T // 128) * 16])
    NT = T // 128
    with p.es:
        x2Tb = p.sb("x2Tb", [128, DCH * T], BF16)
        wqb = p.sb("wqb", [128, DCH * D], BF16)
        qst = p.sb("qst", [128, NT * D], BF16)
        sq = p.sb("sq", [128, 512], F32)
        knq = p.sb("knq", [128, NT * 16], F32)
        pq = [p.ps("pq%d" % i, [128, 512]) for i in range(2)]
        for s_ in ("mm", "xp", "dv"):
            p.sem(s_)
        p.dma_group("ld", 2)
        p.dma_group("so", 1)
        blk = p.block()

        @blk.gpsimd
        def _(g):
            p.dma(g, "ld", _chunked_sbuf_ap(x2Tb, DCH, T), _featmaj_ap(x2T, D, T))
            p.dma(g, "ld", _chunked_sbuf_ap(wqb, DCH, D), _featmaj_ap(W_qkv, D, D))

        @blk.tensor
        def _(t):
            p.gwait(t, "ld")
            i = 0
            for tc in range(NT):
                for half in range(2):
                    if i >= 2:
                        t.wait_ge(p.sems["xp"], i - 1)
                        t.wait_ge(p.sems["dv"], i - 1)
                    for dc in range(DCH):
                        ins = t.matmul(pq[i % 2][:, :],
                                       x2Tb[:, dc * T + tc * 128:dc * T + (tc + 1) * 128],
                                       wqb[:, dc * D + half * 512:dc * D + (half + 1) * 512],
                                       start=(dc == 0), stop=(dc == DCH - 1))
                    p.inc(ins, "mm", 1)
                    i += 1

        @blk.scalar
        def _(s):
            i = 0
            for tc in range(NT):
                for half in range(2):
                    s.wait_ge(p.sems["mm"], i + 1)
                    ins = s.copy(qst[:, tc * D + half * 512:tc * D + (half + 1) * 512],
                                 pq[i % 2][:, :])
                    p.inc(ins, "xp", 1)
                    i += 1

        @blk.vector
        def _(v):
            i = 0
            for tc in range(NT):
                for half in range(2):
                    v.wait_ge(p.sems["mm"], i + 1)
                    v.wait_ge(p.sems["xp"], i + 1)
                    v.tensor_copy(sq[:, :], pq[i % 2][:, :])
                    v.drain()
                    v.tensor_mul(sq[:, :], sq[:, :], sq[:, :])
                    v.drain()
                    for h in range(8):
                        v.reduce_sum(knq[:, tc * 16 + half * 8 + h:tc * 16 + half * 8 + h + 1],
                                     sq[:, h * 64:(h + 1) * 64], axis=AX)
                    ins = v.engine_nop()
                    p.inc(ins, "dv", 1)
                    v.drain()
                    i += 1

        @blk.sync
        def _(sy):
            sy.wait_ge(p.sems["xp"], 2 * NT)
            p.dma(sy, "so", qkv.rearrange("(k p) c -> p k c", p=128),
                  _chunked_sbuf_ap(qst, NT, D))
            sy.wait_ge(p.sems["dv"], 2 * NT)
            p.dma(sy, "so", kno[:, :], knq[:, :])

        @blk.gpsimd
        def _(g):
            p.gwait(g, "so")
    return nc


def build_s_l1rout():
    p = P("s_l1rout")
    nc = p.nc
    E = 16
    NT = T // 128
    x3T = p.inp("x3T", [D, T])
    router_w = p.inp("router_w", [D, E])
    router_b = p.inp("router_b", [E])
    m1o = p.out("m1o", [128, (T // 128) * E])
    m2o = p.out("m2o", [128, (T // 128) * E])
    w1o = p.out("w1o", [128, T // 128])
    w2o = p.out("w2o", [128, T // 128])
    counts = p.out("counts", [1, E])
    with p.es:
        xrc = p.sb("xrc", [128, DCH * 128], F32)
        wrf = p.sb("wrf", [128, DCH * E], F32)
        rbf = p.sb("rbf", [1, E], F32)
        onesf = p.sb("onesf", [1, 128], F32)
        ones128c = p.sb("ones128c", [128, 4], BF16)
        nm = p.sb("nm", [128, 1], F32)
        e16 = p.sb("e16", [128, E], F32)
        se = p.sb("se", [128, 1], F32)
        gs = p.sb("gs", [128, E], F32)
        m1v = p.sb("m1v", [128, 1], F32)
        m2v = p.sb("m2v", [128, 1], F32)
        msk1 = p.sb("msk1", [128, NT * E], F32)
        msk2 = p.sb("msk2", [128, NT * E], F32)
        g2 = p.sb("g2", [128, E], F32)
        den = p.sb("den", [128, 1], F32)
        w1v = p.sb("w1v", [128, NT], F32)
        w2v = p.sb("w2v", [128, NT], F32)
        cntb = p.sb("cntb", [128, E], BF16)
        crow = p.sb("crow", [1, E], F32)
        pR = None
        bankA = p.ps("bankA", [128, 512])
        bankB = p.ps("bankB", [128, 512])
        bankC = p.ps("bankC", [128, 512])
        for s_ in ("vm", "mm", "xp", "dq", "mc", "dc"):
            p.sem(s_)
        p.dma_group("ld", 2)
        p.dma_group("lr", 1)
        p.dma_group("so", 2)
        blk = p.block()

        def W(eng, name, val):
            if val > 0:
                eng.wait_ge(p.sems[name], val)

        @blk.gpsimd
        def _(g):
            p.dma(g, "ld", _chunked_sbuf_ap(wrf, DCH, E), _featmaj_ap(router_w, D, E))
            p.dma(g, "ld", rbf[:, :], router_b[None, :])
            for tc in range(NT):
                if tc >= 1:
                    g.wait_ge(p.sems["mm"], tc)
                p.dma(g, "lr", _chunked_sbuf_ap(xrc, DCH, 128),
                      x3T.rearrange("(c p) t -> p c t", p=128)[:, :, tc * 128:(tc + 1) * 128])

        @blk.vector
        def _(v):
            v.memset(onesf[:, :], 1.0)
            v.memset(ones128c[:, :], 1.0)
            p.inc(v.engine_nop(), "vm", 1)

        pRb = [bankA[:, 0:E], bankB[:, 0:E]]

        @blk.tensor
        def _(t):
            p.gwait(t, "ld")
            t.wait_ge(p.sems["vm"], 1)

            def cnt_mm(j):
                t.wait_ge(p.sems["dq"], 18 * (j + 1))
                ins = t.matmul(bankC[0:4, :E], ones128c[:, :], cntb[:, :],
                               start=(j == 0), stop=(j == NT - 1), skip_group_check=True)
                p.inc(ins, "mc", 1)

            for tc in range(NT):
                t.wait_ge(p.sems["lr0"], 16 * (tc + 1))
                W(t, "xp", tc - 1)
                for dc in range(DCH):
                    t.matmul(pRb[tc % 2], xrc[:, dc * 128:(dc + 1) * 128],
                             wrf[:, dc * E:(dc + 1) * E],
                             start=(dc == 0), stop=False, skip_group_check=True)
                ins = t.matmul(pRb[tc % 2], onesf[:, :], rbf[:, :], start=False, stop=True,
                               skip_group_check=True)
                p.inc(ins, "mm", 1)
                if tc >= 2:
                    cnt_mm(tc - 2)
            cnt_mm(NT - 2)
            cnt_mm(NT - 1)

        @blk.scalar
        def _(s):
            for tc in range(NT):
                s.wait_ge(p.sems["dq"], 18 * tc + 1)
                ins = s.activation(e16[:, :], pRb[tc % 2], AF.Exp, bias=nm[:, :],
                                   accum_out=se[:, :])
                p.inc(ins, "xp", 1)

        @blk.vector
        def _(v):
            for tc in range(NT):
                v.wait_ge(p.sems["mm"], tc + 1)
                W(v, "xp", tc)
                ins = v.reduce_max(nm[:, :], pRb[tc % 2], axis=AX, negate=True)
                p.inc(ins, "dq", 1)
                v.wait_ge(p.sems["xp"], tc + 1)
                v.reciprocal(den[:, :], se[:, :])
                v.drain()
                v.tensor_single_scalar(gs[:, :], e16[:, :], den[0:128, 0:1], ALU.mult)
                v.drain()
                v.reduce_max(m1v[:, :], gs[:, :], axis=AX)
                v.drain()
                v.tensor_single_scalar(msk1[:, tc * E:(tc + 1) * E], gs[:, :],
                                       m1v[0:128, 0:1], ALU.is_ge)
                v.drain()
                v.scalar_tensor_tensor(g2[:, :], msk1[:, tc * E:(tc + 1) * E], -1e30,
                                       gs[:, :], ALU.mult, ALU.add)
                v.drain()
                v.reduce_max(m2v[:, :], g2[:, :], axis=AX)
                v.drain()
                v.tensor_single_scalar(msk2[:, tc * E:(tc + 1) * E], g2[:, :],
                                       m2v[0:128, 0:1], ALU.is_ge)
                v.tensor_add(den[:, :], m1v[:, :], m2v[:, :])
                v.drain()
                v.tensor_scalar_add(den[:, :], den[:, :], 1e-9)
                v.drain()
                v.reciprocal(den[:, :], den[:, :])
                v.drain()
                v.tensor_single_scalar(w1v[:, tc:tc + 1], m1v[:, :], den[0:128, 0:1], ALU.mult)
                v.tensor_single_scalar(w2v[:, tc:tc + 1], m2v[:, :], den[0:128, 0:1], ALU.mult)
                v.drain()
                W(v, "mc", tc)
                ins = v.tensor_add(cntb[:, :], msk1[:, tc * E:(tc + 1) * E],
                                   msk2[:, tc * E:(tc + 1) * E])
                p.inc(ins, "dq", 17)

        @blk.vector
        def _(v):
            v.wait_ge(p.sems["mc"], NT)
            ins = v.tensor_copy(crow[:, :], bankC[0:1, 0:E])
            p.inc(ins, "dc", 1)

        @blk.sync
        def _(sy):
            sy.wait_ge(p.sems["dq"], 18 * NT)
            p.dma(sy, "so", m1o[:, :], msk1[:, :])
            p.dma(sy, "so", m2o[:, :], msk2[:, :])
            p.dma(sy, "so", w1o[:, :], w1v[:, :])
            p.dma(sy, "so", w2o[:, :], w2v[:, :])
            sy.wait_ge(p.sems["dc"], 1)
            p.dma(sy, "so", counts[:, :], crow[:, :])

        @blk.gpsimd
        def _(g):
            p.gwait(g, "so")
    return nc


def build_s_l1attn():
    p = P("s_l1attn")
    nc = p.nc
    H = 16
    NP = 8
    NKC = S // 128
    x2T = p.inp("x2T", [D, T])
    khat_in = p.inp("khat_in", [NP * 128, S], BF16)
    qhpT_in = p.inp("qhpT_in", [NP * 128, T], BF16)
    vsw_in = p.inp("vsw_in", [NP * 128, NKC * 130], BF16)
    knTo = p.inp("knTo", [H, T])
    knTb = p.inp("knTb", [H, S])
    W_out = p.inp("W_out", [D, D])
    b_out = p.inp("b_out", [D])
    x3T = p.out("x3T", [D, T])
    with p.es:
        khat = [p.sb("khat%d" % i, [128, S], BF16) for i in range(2)]
        qhp = [p.sb("qhp%d" % i, [128, T], BF16) for i in range(2)]
        vsw = [p.sb("vsw%d" % i, [128, NKC * 130], BF16) for i in range(2)]
        onrm = p.sb("onrm", [64, H * 2 * 512], BF16)
        woutb = p.sb("woutb", [64, H * D], BF16)
        b_outb = p.sb("b_outb", [1, D], BF16)
        onesq = p.sb("onesq", [1, 512], BF16)
        ones65 = p.sb("ones65", [65, 64], BF16)
        onesr128 = p.sb("onesr128", [1, 128], BF16)
        ones128c = p.sb("ones128c", [128, 4], BF16)
        ones1x128f = p.sb("ones1x128f", [1, 128], F32)
        identf = p.sb("identf", [16, 16], F32)
        knb_p = [p.sb("knb_p%d" % i, [H, T], BF16) for i in range(2)]
        kno_s = knb_p[0]
        qk5 = p.sb("qk5", [H, 5], F32)
        km16 = p.sb("km16", [H, 16], F32)
        t5 = p.sb("t5", [5, 16], F32)
        m5 = p.sb("m5", [5, 1], F32)
        r5 = p.sb("r5", [1, 5], F32)
        u4 = p.sb("u4", [1, 4], F32)
        negub = p.sb("negub", [1, 4], BF16)
        Ubc = p.sb("Ubc", [128, 4], F32)
        et = [p.sb("et%d" % i, [128, 1024], BF16) for i in range(3)]
        rs = p.sb("rs", [65, 512], F32)
        rsb = p.sb("rsb", [65, 512], BF16)
        pOs = p.sb("pOs", [64, 512], F32)
        onf = p.sb("onf", [64, 512], F32)
        xf = p.sb("xf", [128, DCH * 512], F32)
        ts = p.sb("ts", [128, DCH * 512], F32)
        tb = p.sb("tb", [128, 512], BF16)
        sqb = p.sb("sqb", [128, 512], BF16)
        x1c = [p.sb("x1c%d" % i, [128, 512], F32) for i in range(2)]
        _r = {n: p.sb(n, [1, 512], F32) for n in ("mrow", "m2row", "vrow")}
        rows = (_r["mrow"], _r["m2row"], _r["vrow"], _r["vrow"], _r["m2row"])
        tmpf = p.sb("tmpf", [128, 512], F32)
        bankA = p.ps("bankA", [128, 1024])
        bankB = p.ps("bankB", [128, 1024])
        bankC = p.ps("bankC", [128, 512])
        bankD = p.ps("bankD", [128, 512])
        bankE = p.ps("bankE", [128, 512])
        bankF = p.ps("bankF", [128, 512])
        for s_ in ("vm", "mm", "xp", "dq", "ms", "xe", "mv", "xo", "d6", "mb",
                   "mp", "dva", "xr", "mn"):
            p.sem(s_)
        p.dma_group("ld", 2)
        p.dma_group("lk", 2)
        p.dma_group("lt", 1)
        p.dma_group("lx", 1)
        p.dma_group("so", 2)
        blk = p.block()

        def W(eng, name, val):
            if val > 0:
                eng.wait_ge(p.sems[name], val)

        @blk.gpsimd
        def _(g):
            p.dma(g, "ld", woutb[:, :],
                  W_out.rearrange("(h p) n -> p h n", p=64))
            p.dma(g, "ld", b_outb[:, :], b_out[None, :])


        @blk.vector
        def _(v):
            for t_, val in [(onesq, 1.0), (ones65, 1.0), (identf, 1.0),
                            (onesr128, 1.0), (ones128c, 1.0), (ones1x128f, 1.0)]:
                v.memset(t_[:, :], val)
            p.inc(v.engine_nop(), "vm", 1)

        @blk.gpsimd
        def _(g):
            g.wait_ge(p.sems["vm"], 1)
            ins = g.affine_select(identf[:, :], identf[:, :], [[-1, 16]],
                                  ALU.is_equal, 0.0, channel_multiplier=1)
            p.inc(ins, "vm", 1)

        kn_toks = []

        @blk.gpsimd
        def _(g):
            kn_toks.append(p.dma(g, "lk", kno_s[:, :], knTo[:, :]))
            for r in range(4):
                g.wait_ge(p.sems["dq"], max(1, r))
                kn_toks.append(p.dma(g, "lk", knb_p[r % 2][:, :],
                                     knTb[:, r * T:(r + 1) * T]))

        @blk.vector
        def _(v):
            p.gwait(v, "ld")
            v.wait_ge(p.sems["vm"], 2)
            tok = kn_toks[0]
            v.wait_ge(p.sems[tok[0]], tok[1])
            for tt in range(NTT):
                v.reduce_max(qk5[:, tt:tt + 1], kno_s[:, tt * 512:(tt + 1) * 512],
                             axis=AX)
            ins = v.engine_nop()
            p.inc(ins, "dq", 1)
            for r in range(4):
                tok = kn_toks[r + 1]
                v.wait_ge(p.sems[tok[0]], tok[1])
                for i in range(4):
                    v.reduce_max(km16[:, 4 * r + i:4 * r + i + 1],
                                 knb_p[r % 2][:, i * 512:(i + 1) * 512], axis=AX)
                ins = v.engine_nop()
                p.inc(ins, "dq", 1)
            v.drain()
            ins = v.reduce_max(qk5[:, 4:5], km16[:, :], axis=AX)
            p.inc(ins, "dq", 1)

        @blk.tensor
        def _(t):
            t.wait_ge(p.sems["dq"], 6)
            ins = t.transpose(bankE[0:5, 0:16], qk5[:, :], identf[:, :])
            p.inc(ins, "mm", 1)

        @blk.vector
        def _(v):
            v.wait_ge(p.sems["mm"], 1)
            v.tensor_copy(t5[:, :], bankE[0:5, 0:16])
            v.drain()
            ins = v.reduce_max(m5[:, :], t5[:, :], axis=AX)
            p.inc(ins, "dq", 1)

        @blk.tensor
        def _(t):
            t.wait_ge(p.sems["dq"], 7)
            ins = t.transpose(bankF[0:1, 0:5], m5[:, :], identf[0:5, 0:5])
            p.inc(ins, "mm", 1)

        @blk.vector
        def _(v):
            v.wait_ge(p.sems["mm"], 2)
            ins = v.tensor_copy(r5[:, :], bankF[0:1, 0:5])
            p.inc(ins, "dq", 1)

        @blk.scalar
        def _(s):
            s.wait_ge(p.sems["dq"], 8)
            ins = s.activation(r5[:, :], r5[:, :], AF.Ln)
            p.inc(ins, "xp", 1)

        @blk.vector
        def _(v):
            v.wait_ge(p.sems["xp"], 1)
            ins = v.tensor_single_scalar(u4[:, :], r5[0:1, 0:4], r5[0:1, 4:5],
                                         ALU.add)
            p.inc(ins, "dq", 1)

        @blk.scalar
        def _(s):
            s.wait_ge(p.sems["dq"], 9)
            ins = s.activation(u4[:, :], u4[:, :], AF.Exp, scale=0.5)
            p.inc(ins, "xp", 1)

        @blk.vector
        def _(v):
            v.wait_ge(p.sems["xp"], 2)
            ins = v.tensor_scalar_mul(negub[:, :], u4[:, :], -SCALE1)
            p.inc(ins, "dq", 1)

        @blk.tensor
        def _(t):
            t.wait_ge(p.sems["dq"], 10)
            ins = t.matmul(bankE[:, 0:4], onesr128[:, :], negub[:, :],
                           start=True, stop=True)
            p.inc(ins, "mm", 1)

        @blk.vector
        def _(v):
            v.wait_ge(p.sems["mm"], 3)
            ins = v.tensor_copy(Ubc[:, :], bankE[:, 0:4])
            p.inc(ins, "dq", 1)
        dq_ub0 = p.c["dq"]

        pS = [bankA, bankB]
        pOA = bankC[0:65, :]
        pOB = bankD[0:65, :]

        for th in range(2):
            for ph in range(NP):
                u = th * NP + ph
                buf = u % 2

                @blk.sync
                def _(sy, th=th, ph=ph, u=u, buf=buf):
                    if u >= 2:
                        sy.wait_ge(p.sems["mv"], (u - 1) * 2 * NKC)
                    p.dma(sy, "lt", khat[buf][:, :],
                          khat_in[ph * 128:(ph + 1) * 128, :])
                    p.dma(sy, "lt", qhp[buf][:, :],
                          qhpT_in[ph * 128:(ph + 1) * 128, :])
                    p.dma(sy, "lt", vsw[buf][:, :],
                          vsw_in[ph * 128:(ph + 1) * 128, :])
                lt_done = p.c["lt0"]
                d6_snap = p.c["d6"]
                dva_snap = p.c["dva"]
                mp_snap = p.c["mp"]

                for tq in range(2):
                    tt = th * 2 + tq
                    msb = p.c["ms"]; xeb = p.c["xe"]; mvb = p.c["mv"]
                    xob = p.c["xo"]; d6b = p.c["d6"]; mbq = p.c["mb"]

                    @blk.tensor
                    def _(t, buf=buf, tt=tt, ph=ph, tq=tq, u=u, msb=msb, xeb=xeb,
                          mvb=mvb, xob=xob, d6b=d6b, lt_done=lt_done,
                          d6_snap=d6_snap, dva_snap=dva_snap, mp_snap=mp_snap):
                        t.wait_ge(p.sems["dq"], dq_ub0)
                        t.wait_ge(p.sems["lt0"], lt_done)
                        if tq == 0 and u > 0:
                            W(t, "d6", d6_snap)
                            W(t, "dva", dva_snap)
                            W(t, "mp", mp_snap)

                        def value_mms(kc):
                            t.wait_ge(p.sems["xe"], xeb + kc + 1)
                            t.matmul(pOA, vsw[buf][:, kc * 130:kc * 130 + 65],
                                     et[(xeb + kc) % 3][:, 0:512],
                                     start=(kc == 0), stop=(kc == NKC - 1))
                            ins = t.matmul(pOB, vsw[buf][:, kc * 130 + 65:kc * 130 + 130],
                                           et[(xeb + kc) % 3][:, 512:1024],
                                           start=(kc == 0), stop=(kc == NKC - 1))
                            p.inc(ins, "mv", 1)

                        for kc in range(NKC):
                            W(t, "xe", xeb + kc - 1)
                            if kc == 0:
                                W(t, "xo", xob)
                                W(t, "d6", d6b - 1)
                            t.matmul(pS[(xeb + kc) % 2][:, 0:512],
                                     khat[buf][0:64, kc * 128:(kc + 1) * 128],
                                     qhp[buf][0:64, tt * 512:(tt + 1) * 512],
                                     start=True, stop=True, tile_position=(0, 0))
                            ins = t.matmul(pS[(xeb + kc) % 2][:, 512:1024],
                                           khat[buf][64:128, kc * 128:(kc + 1) * 128],
                                           qhp[buf][64:128, tt * 512:(tt + 1) * 512],
                                           start=True, stop=True, tile_position=(64, 0))
                            p.inc(ins, "ms", 1)
                            if kc >= 2:
                                value_mms(kc - 2)
                        value_mms(NKC - 2)
                        value_mms(NKC - 1)
                        t.wait_ge(p.sems["d6"], d6b + 2)
                        ins = t.matmul(bankE[0:64, :], ones65[64:65, :], rsb[64:65, :],
                                       start=True, stop=True, tile_position=(64, 0))
                        p.inc(ins, "mb", 1)
                        t.wait_ge(p.sems["d6"], d6b + 4)
                        ins = t.matmul(bankF[0:64, :], ones65[64:65, :], rsb[64:65, :],
                                       start=True, stop=True, tile_position=(64, 0))
                        p.inc(ins, "mb", 1)

                    @blk.scalar
                    def _(s, tt=tt, ph=ph, tq=tq, msb=msb, xeb=xeb, mvb=mvb,
                          d6b=d6b, u=u):
                        for kc in range(NKC):
                            s.wait_ge(p.sems["ms"], msb + kc + 1)
                            W(s, "mv", mvb + kc - 2)
                            ins = s.activation(et[(xeb + kc) % 3][:, :],
                                               pS[(xeb + kc) % 2][:, :], AF.Exp,
                                               bias=Ubc[:, tt:tt + 1],
                                               scale=SCALE1)
                            p.inc(ins, "xe", 1)
                        s.wait_ge(p.sems["mv"], mvb + NKC)
                        W(s, "d6", d6b)
                        ins = s.copy(pOs[:, :], pOA[0:64, :])
                        p.inc(ins, "xo", 1)
                        s.wait_ge(p.sems["d6"], d6b + 3)
                        ins = s.copy(pOs[:, :], pOB[0:64, :])
                        p.inc(ins, "xo", 1)

                    @blk.vector
                    def _(v, tt=tt, ph=ph, tq=tq, mvb=mvb, xob=xob, d6b=d6b, mbq=mbq):
                        h0 = ph * 2
                        v.wait_ge(p.sems["mv"], mvb + NKC)
                        ins = v.reciprocal(rs[64:65, :], pOA[64:65, :])
                        p.inc(ins, "d6", 1)
                        v.drain()
                        ins = v.tensor_copy(rsb[64:65, :], rs[64:65, :])
                        p.inc(ins, "d6", 1)
                        v.wait_ge(p.sems["xo"], xob + 1)
                        v.wait_ge(p.sems["mb"], mbq + 1)
                        v.tensor_mul(onf[:, :], pOs[:, :], bankE[0:64, :])
                        v.drain()
                        ins = v.tensor_copy(onrm[:, (h0 * 2 + tq) * 512:(h0 * 2 + tq + 1) * 512],
                                            onf[:, :])
                        p.inc(ins, "d6", 1)
                        v.reciprocal(rs[64:65, :], pOB[64:65, :])
                        v.drain()
                        ins = v.tensor_copy(rsb[64:65, :], rs[64:65, :])
                        p.inc(ins, "d6", 1)
                        v.wait_ge(p.sems["xo"], xob + 2)
                        v.wait_ge(p.sems["mb"], mbq + 2)
                        v.tensor_mul(onf[:, :], pOs[:, :], bankF[0:64, :])
                        v.drain()
                        ins = v.tensor_copy(onrm[:, ((h0 + 1) * 2 + tq) * 512:((h0 + 1) * 2 + tq + 1) * 512],
                                            onf[:, :])
                        p.inc(ins, "d6", 1)

            d6_half = p.c["d6"]
            for tq in range(2):
                tt = th * 2 + tq

                @blk.sync
                def _(sy, tt=tt):
                    if tt > 0:
                        sy.wait_ge(p.sems["dva"], tt * 34)
                    p.dma(sy, "lx", _chunked_sbuf_ap(xf, DCH, 512),
                          x2T.rearrange("(c p) t -> p c t", p=128)[:, :, tt * 512:(tt + 1) * 512])
                lx_now = p.c["lx0"]

                pA = [bankA[:, 0:512], bankB[:, 0:512]]
                mpb = p.c["mp"]
                dvab = p.c["dva"]
                xe_now = p.c["xe"]

                @blk.tensor
                def _(t, th=th, tq=tq, tt=tt, mpb=mpb, dvab=dvab, d6_half=d6_half,
                      xe_now=xe_now, lx_now=lx_now):
                    if tt == 0:
                        p.gwait(t, "ld")
                    t.wait_ge(p.sems["d6"], d6_half)
                    t.wait_ge(p.sems["xe"], xe_now)
                    t.wait_ge(p.sems["lx0"], lx_now)

                    def stats(c):
                        t.wait_ge(p.sems["dva"], dvab + 3 * c + 3)
                        t.matmul(bankE[0:4, :], ones128c[:, :], tb[:, :],
                                 start=(c == 0), stop=(c == DCH - 1), skip_group_check=True)
                        ins = t.matmul(bankF[0:4, :], ones128c[:, :], sqb[:, :],
                                       start=(c == 0), stop=(c == DCH - 1), skip_group_check=True)
                        p.inc(ins, "mp", 1)

                    for oc in range(DCH):
                        if oc >= 2:
                            t.wait_ge(p.sems["dva"], dvab + 3 * (oc - 2) + 1)
                        for h in range(H):
                            t.matmul(pA[oc % 2], woutb[:, h * D + oc * 128:h * D + (oc + 1) * 128],
                                     onrm[:, (h * 2 + tq) * 512:(h * 2 + tq + 1) * 512],
                                     start=(h == 0), stop=False, skip_group_check=True)
                        ins = t.matmul(pA[oc % 2], b_outb[:, oc * 128:(oc + 1) * 128],
                                       onesq[:, :], start=False, stop=True,
                                       skip_group_check=True)
                        p.inc(ins, "mn", 1)
                        if oc >= 2:
                            stats(oc - 2)
                    stats(DCH - 2)
                    stats(DCH - 1)
                mn_prj = p.c["mn"]

                @blk.vector
                def _(v, tt=tt, mpb=mpb, mn_prj=mn_prj):
                    for oc in range(DCH):
                        v.wait_ge(p.sems["mn"], mn_prj - DCH + oc + 1)
                        W(v, "mp", mpb + oc)
                        ins = v.tensor_add(ts[:, oc * 512:(oc + 1) * 512],
                                           xf[:, oc * 512:(oc + 1) * 512], pA[oc % 2])
                        p.inc(ins, "dva", 1)
                        v.drain()
                        ins = v.tensor_copy(tb[:, :], ts[:, oc * 512:(oc + 1) * 512])
                        p.inc(ins, "dva", 1)
                        v.drain()
                        ins = v.tensor_mul(sqb[:, :], tb[:, :], tb[:, :])
                        p.inc(ins, "dva", 1)

                emit_ln_tail2(p, blk, dvab, mpb, tt, ts, tb, sqb, tmpf, x1c, rows,
                              ones128c, ones1x128f, bankE, bankF, x3T)

        @blk.gpsimd
        def _(g):
            p.gwait(g, "so")
    return nc


def build_s_l1ffn():
    p = P("s_l1ffn")
    nc = p.nc
    E = 16
    FC = 11
    NT = 16
    CAP = 2560.0
    x3T = p.inp("x3T", [D, T])
    m1i = p.inp("m1i", [128, NT * E])
    m2i = p.inp("m2i", [128, NT * E])
    w1i = p.inp("w1i", [128, NT])
    w2i = p.inp("w2i", [128, NT])
    counts_all = p.inp("counts_all", [8, E])
    core_sel = p.inp("core_sel", [8, 1])
    w_gate = p.inp("w_gate", [E, D, 1408])
    w_up = p.inp("w_up", [E, D, 1408])
    w_down = p.inp("w_down", [E, 1408, D])
    sw_gate = p.inp("sw_gate", [D, 1408])
    sw_up = p.inp("sw_up", [D, 1408])
    sw_down = p.inp("sw_down", [1408, D])
    yT = p.out("yT", [D, T])
    with p.es:
        x3Tb = p.sb("x3Tb", [128, DCH * T], BF16)
        m1s = p.sb("m1s", [128, NT * E], F32)
        m2s = p.sb("m2s", [128, NT * E], F32)
        w1s = p.sb("w1s", [128, NT], F32)
        w2s = p.sb("w2s", [128, NT], F32)
        ca = p.sb("ca", [8, E], F32)
        cse = p.sb("cse", [8, 1], F32)
        B8 = p.sb("B8", [8, 8], F32)
        m8 = p.sb("m8", [8, 1], F32)
        offsf = p.sb("offsf", [16, 1], F32)
        acc = p.sb("acc", [16, 1], F32)
        triL = p.sb("triL", [128, 128], BF16)
        identf = p.sb("identf", [128, 128], F32)
        identb = p.sb("identb", [128, 128], BF16)
        ri128 = p.sb("ri128", [128, 128], F32)
        ci128 = p.sb("ci128", [128, 128], F32)
        cnt2 = p.sb("cnt2", [128, E], BF16)
        grT = p.sb("grT", [16, 128], F32)
        t16 = p.sb("t16", [128, E], F32)
        win1 = p.sb("win1", [128, 1], F32)
        win2 = p.sb("win2", [128, 1], F32)
        gw1 = p.sb("gw1", [128, 1], F32)
        gw2 = p.sb("gw2", [128, 1], F32)
        G = p.sb("G", [128, NT * E], F32)
        Gb = p.sb("Gb", [128, E], BF16)
        GT = p.sb("GT", [16, NT * 128], BF16)
        grows = p.sb("grows", [1, T], BF16)
        onesr128 = p.sb("onesr128", [1, 128], BF16)
        ones128c = p.sb("ones128c", [128, 4], BF16)
        ones1x128f = p.sb("ones1x128f", [1, 128], F32)
        wgub = p.sb("wgub", [128, 2 * DCH * 1408], BF16)
        wdb = p.sb("wdb", [128, FC * D], BF16)
        sgE = p.sb("sgE", [128, 512], BF16)
        t1f = p.sb("t1f", [128, 512], F32)
        gateS = p.sb("gateS", [128, 512], F32)
        glg = p.sb("glg", [128, FC * 512], BF16)
        yaccT = p.sb("yaccT", [128, DCH * 1024], F32)
        xf = p.sb("xf", [128, DCH * 512], F32)
        ts = p.sb("ts", [128, DCH * 512], F32)
        tb = p.sb("tb", [128, 512], BF16)
        sqb = p.sb("sqb", [128, 512], BF16)
        x1c = [p.sb("x1c%d" % i, [128, 512], F32) for i in range(2)]
        _r = {n: p.sb(n, [1, 512], F32) for n in ("mrow", "m2row", "vrow")}
        rows = (_r["mrow"], _r["m2row"], _r["vrow"], _r["vrow"], _r["m2row"])
        tmpf = t1f
        bankA = p.ps("bankA", [128, 512])
        bankB = p.ps("bankB", [128, 512])
        bankC = p.ps("bankC", [128, 512])
        bankD = p.ps("bankD", [128, 512])
        bankE = p.ps("bankE", [128, 512])
        bankF = p.ps("bankF", [128, 512])
        bankG = p.ps("bankG", [128, 512])
        bankH = p.ps("bankH", [128, 128], BF16)
        for s_ in ("vm", "mm", "xp", "dq", "m1", "sg", "da", "m2", "dy",
                   "mp", "dva", "xr", "mn"):
            p.sem(s_)
        p.dma_group("ld", 4)
        p.dma_group("lw", 2)
        p.dma_group("lg", 1)
        p.dma_group("lx", 1)
        p.dma_group("so", 2)
        blk = p.block()

        def W(eng, name, val):
            if val > 0:
                eng.wait_ge(p.sems[name], val)

        @blk.gpsimd
        def _(g):
            p.dma(g, "ld", _chunked_sbuf_ap(x3Tb, DCH, T), _featmaj_ap(x3T, D, T))
            p.dma(g, "ld", m1s[:, :], m1i[:, :])
            p.dma(g, "ld", m2s[:, :], m2i[:, :])
            p.dma(g, "ld", w1s[:, :], w1i[:, :])
            p.dma(g, "ld", w2s[:, :], w2i[:, :])
            p.dma(g, "ld", ca[:, :], counts_all[:, :])
            p.dma(g, "ld", cse[:, :], core_sel[:, :])

        @blk.vector
        def _(v):
            v.memset(onesr128[:, :], 1.0)
            v.memset(ones128c[:, :], 1.0)
            v.memset(ones1x128f[:, :], 1.0)
            v.memset(triL[:, :], 1.0)
            v.memset(identf[:, :], 1.0)
            v.memset(identb[:, :], 1.0)
            v.memset(B8[:, :], 1.0)
            p.inc(v.engine_nop(), "vm", 1)

        @blk.gpsimd
        def _(g):
            g.wait_ge(p.sems["vm"], 1)
            g.iota(ri128[:, :], [[0, 128]], channel_multiplier=1,
                   allow_small_or_imprecise_dtypes=True)
            g.iota(ci128[:, :], [[1, 128]], channel_multiplier=0,
                   allow_small_or_imprecise_dtypes=True)
            g.affine_select(identf[:, :], identf[:, :], [[-1, 128]], ALU.is_equal, 0.0,
                            channel_multiplier=1)
            ins = g.affine_select(identb[:, :], identb[:, :], [[-1, 128]], ALU.is_equal, 0.0,
                                  channel_multiplier=1)
            p.inc(ins, "vm", 1)

        @blk.vector
        def _(v):
            v.wait_ge(p.sems["vm"], 2)
            v.tensor_tensor(triL[:, :], ri128[:, :], ci128[:, :], ALU.is_lt)
            ins = v.tensor_tensor(B8[:, :], ri128[0:8, 0:8], ci128[0:8, 0:8], ALU.is_gt)
            p.inc(ins, "vm", 1)

        @blk.tensor
        def _(t):
            p.gwait(t, "ld")
            t.wait_ge(p.sems["vm"], 3)
            ins = t.matmul(bankC[0:8, 0:1], B8[:, :], cse[:, :], start=True, stop=True)
            p.inc(ins, "mm", 1)

        @blk.vector
        def _(v):
            v.wait_ge(p.sems["mm"], 1)
            ins = v.tensor_copy(m8[:, :], bankC[0:8, 0:1])
            p.inc(ins, "dq", 1)

        @blk.tensor
        def _(t):
            t.wait_ge(p.sems["dq"], 1)
            ins = t.matmul(bankC[0:16, 0:1], ca[:, :], m8[:, :], start=True, stop=True)
            p.inc(ins, "mm", 1)

        @blk.vector
        def _(v):
            v.wait_ge(p.sems["mm"], 2)
            v.tensor_copy(offsf[:, :], bankC[0:16, 0:1])
            v.drain()
            ins = v.tensor_copy(acc[:, :], offsf[:, :])
            p.inc(ins, "dq", 1)

        for tc in range(NT):
            dqb = p.c["dq"]; mmb = p.c["mm"]

            @blk.vector
            def _(v, tc=tc, dqb=dqb, mmb=mmb):
                W(v, "mm", mmb - 2)
                ins = v.tensor_add(cnt2[:, :], m1s[:, tc * E:(tc + 1) * E],
                                   m2s[:, tc * E:(tc + 1) * E])
                p.inc(ins, "dq", 1)

            @blk.tensor
            def _(t, tc=tc, dqb=dqb, mmb=mmb):
                t.wait_ge(p.sems["dq"], dqb + 1)
                t.matmul(bankC[0:16, 0:128], cnt2[:, :], triL[:, :], start=True,
                         stop=True, skip_group_check=True)
                ins = t.matmul(bankD[0:16, 0:4], cnt2[:, :], ones128c[:, :],
                               start=True, stop=True, skip_group_check=True)
                p.inc(ins, "mm", 2)

            @blk.vector
            def _(v, tc=tc, dqb=dqb, mmb=mmb):
                v.wait_ge(p.sems["mm"], mmb + 2)
                v.tensor_single_scalar(grT[:, :], bankC[0:16, 0:128], acc[0:16, 0:1],
                                       ALU.add)
                v.drain()
                v.tensor_single_scalar(grT[:, :], grT[:, :], CAP, ALU.is_lt)
                v.drain()
                ins = v.tensor_add(acc[:, :], acc[:, :], bankD[0:16, 0:1])
                p.inc(ins, "dq", 1)

            @blk.tensor
            def _(t, tc=tc, dqb=dqb):
                t.wait_ge(p.sems["dq"], dqb + 2)
                ins = t.transpose(bankE[:, 0:16], grT[:, :], identf[0:16, 0:16])
                p.inc(ins, "mm", 1)

            @blk.vector
            def _(v, tc=tc, dqb=dqb, mmb=mmb):
                v.wait_ge(p.sems["mm"], mmb + 3)
                v.tensor_mul(t16[:, :], m1s[:, tc * E:(tc + 1) * E], bankE[:, 0:16])
                v.drain()
                v.reduce_sum(win1[:, :], t16[:, :], axis=AX)
                v.drain()
                v.tensor_mul(t16[:, :], m2s[:, tc * E:(tc + 1) * E], bankE[:, 0:16])
                v.drain()
                v.reduce_sum(win2[:, :], t16[:, :], axis=AX)
                v.drain()
                v.tensor_mul(gw1[:, :], w1s[:, tc:tc + 1], win1[:, :])
                v.tensor_mul(gw2[:, :], w2s[:, tc:tc + 1], win2[:, :])
                v.drain()
                v.tensor_single_scalar(G[:, tc * E:(tc + 1) * E],
                                       m1s[:, tc * E:(tc + 1) * E], gw1[0:128, 0:1],
                                       ALU.mult)
                v.drain()
                v.scalar_tensor_tensor(G[:, tc * E:(tc + 1) * E],
                                       m2s[:, tc * E:(tc + 1) * E], gw2[0:128, 0:1],
                                       G[:, tc * E:(tc + 1) * E], ALU.mult, ALU.add)
                v.drain()
                ins = v.tensor_copy(Gb[:, :], G[:, tc * E:(tc + 1) * E])
                p.inc(ins, "dq", 1)

            @blk.tensor
            def _(t, tc=tc, dqb=dqb, mmb=mmb):
                t.wait_ge(p.sems["dq"], dqb + 3)
                W(t, "xp", tc)
                ins = t.transpose(bankH[0:16, :], Gb[:, :], identb[:, :])
                p.inc(ins, "mm", 1)

            @blk.scalar
            def _(s, tc=tc, mmb=mmb):
                s.wait_ge(p.sems["mm"], mmb + 4)
                ins = s.copy(GT[:, tc * 128:(tc + 1) * 128], bankH[0:16, :])
                p.inc(ins, "xp", 1)
        dq_gates = p.c["dq"]; xp_gates = p.c["xp"]; mm_gates = p.c["mm"]

        for hf in range(2):
            for ei in range(17):
                e = ei - 1
                li = hf * 17 + ei
                m1b = p.c["m1"]; sgb_ = p.c["sg"]; dab = p.c["da"]
                m2b = p.c["m2"]; dyb = p.c["dy"]

                @blk.gpsimd
                def _(g, hf=hf, ei=ei, e=e, li=li, m1b=m1b, m2b=m2b):
                    if li == 0:
                        g.wait_ge(p.sems["vm"], 3)
                    if li >= 1:
                        g.wait_ge(p.sems["m1"], m1b)
                    if ei == 0:
                        p.dma(g, "lw", _chunked_sbuf_ap(wgub, 2 * DCH, 1408)[:, 0:DCH, :],
                              _featmaj_ap(sw_gate, D, 1408))
                        p.dma(g, "lw", _chunked_sbuf_ap(wgub, 2 * DCH, 1408)[:, DCH:2 * DCH, :],
                              _featmaj_ap(sw_up, D, 1408))
                    else:
                        p.dma(g, "lw", _chunked_sbuf_ap(wgub, 2 * DCH, 1408)[:, 0:DCH, :],
                              _featmaj_ap(w_gate[e, :, :], D, 1408))
                        p.dma(g, "lw", _chunked_sbuf_ap(wgub, 2 * DCH, 1408)[:, DCH:2 * DCH, :],
                              _featmaj_ap(w_up[e, :, :], D, 1408))
                        g.wait_ge(p.sems["xp"], xp_gates)
                        p.dma(g, "lg", grows[:, :], GT[e:e + 1, :])
                    if li >= 1:
                        g.wait_ge(p.sems["m2"], m2b)
                    if ei == 0:
                        p.dma(g, "lw", _chunked_sbuf_ap(wdb, FC, D),
                              _featmaj_ap(sw_down, 1408, D))
                    else:
                        p.dma(g, "lw", _chunked_sbuf_ap(wdb, FC, D),
                              _featmaj_ap(w_down[e, :, :], 1408, D))
                lw_done = p.c["lw0"] + p.c["lw1"]
                lw0_now = p.c["lw0"]; lw1_now = p.c["lw1"]; lg_now = p.c["lg0"]

                for tq in range(2):
                    tt = hf * 2 + tq
                    m1c = p.c["m1"]; sgc = p.c["sg"]; dac = p.c["da"]
                    m2q = p.c["m2"]
                    dva_snap = p.c["dva"]

                    @blk.tensor
                    def _(t, hf=hf, ei=ei, tq=tq, tt=tt, li=li, m1c=m1c, sgc=sgc,
                          dac=dac, lw0_now=lw0_now, lw1_now=lw1_now, lg_now=lg_now):
                        off = 1 if ei > 0 else 0
                        t.wait_ge(p.sems["lw0"], lw0_now)
                        t.wait_ge(p.sems["lw1"], lw1_now)
                        if li == 0 and tq == 0:
                            p.gwait(t, "ld")
                            W(t, "dq", p.c["dq"])
                        if ei > 0:
                            t.wait_ge(p.sems["lg0"], lg_now)
                            W(t, "da", dac)
                            ins = t.matmul(bankC[:, :], onesr128[:, :],
                                           grows[:, tt * 512:(tt + 1) * 512],
                                           start=True, stop=True)
                            p.inc(ins, "m1", 1)
                        for fc in range(FC):
                            W(t, "sg", sgc + fc)
                            for dc in range(DCH):
                                ins = t.matmul(bankA[:, :],
                                               wgub[:, dc * 1408 + fc * 128:dc * 1408 + (fc + 1) * 128],
                                               x3Tb[:, dc * T + tt * 512:dc * T + (tt + 1) * 512],
                                               start=(dc == 0), stop=(dc == DCH - 1))
                            p.inc(ins, "m1", 1)
                            W(t, "da", dac + off + 2 * fc - 1)
                            for dc in range(DCH):
                                ins = t.matmul(bankB[:, :],
                                               wgub[:, (DCH + dc) * 1408 + fc * 128:(DCH + dc) * 1408 + (fc + 1) * 128],
                                               x3Tb[:, dc * T + tt * 512:dc * T + (tt + 1) * 512],
                                               start=(dc == 0), stop=(dc == DCH - 1))
                            p.inc(ins, "m1", 1)

                    @blk.scalar
                    def _(s, ei=ei, li=li, m1c=m1c, sgc=sgc, dac=dac):
                        off = 1 if ei > 0 else 0
                        for fc in range(FC):
                            s.wait_ge(p.sems["m1"], m1c + off + 2 * fc + 1)
                            W(s, "da", dac + off + 2 * fc - 1)
                            ins = s.activation(sgE[:, :], bankA[:, :], AF.Silu)
                            p.inc(ins, "sg", 1)

                    @blk.vector
                    def _(v, ei=ei, tq=tq, li=li, m1c=m1c, sgc=sgc, dac=dac, m2q=m2q):
                        off = 1 if ei > 0 else 0
                        W(v, "m2", m2q)
                        if ei > 0:
                            v.wait_ge(p.sems["m1"], m1c + 1)
                            ins = v.tensor_copy(gateS[:, :], bankC[:, :])
                            p.inc(ins, "da", 1)
                        for fc in range(FC):
                            v.wait_ge(p.sems["m1"], m1c + off + 2 * fc + 2)
                            v.wait_ge(p.sems["sg"], sgc + fc + 1)
                            v.tensor_mul(t1f[:, :], bankB[:, :], sgE[:, :])
                            p.inc(v.engine_nop(), "da", 1)
                            v.drain()
                            if ei > 0:
                                v.tensor_mul(t1f[:, :], t1f[:, :], gateS[:, :])
                                v.drain()
                            ins = v.tensor_copy(glg[:, fc * 512:(fc + 1) * 512], t1f[:, :])
                            p.inc(ins, "da", 1)
                            v.drain()

                    da_glg = p.c["da"]
                    m2c = p.c["m2"]; dyc = p.c["dy"]

                    @blk.tensor
                    def _(t, ei=ei, li=li, tq=tq, da_glg=da_glg, m2c=m2c, dyc=dyc):
                        t.wait_ge(p.sems["da"], da_glg)
                        for oc in range(DCH):
                            W(t, "dy", dyc + oc - 1)
                            for fc in range(FC):
                                ins = t.matmul(bankD[:, :] if oc % 2 == 0 else bankE[:, :],
                                               wdb[:, fc * D + oc * 128:fc * D + (oc + 1) * 128],
                                               glg[:, fc * 512:(fc + 1) * 512],
                                               start=(fc == 0), stop=(fc == FC - 1))
                            p.inc(ins, "m2", 1)

                    @blk.vector
                    def _(v, ei=ei, li=li, tq=tq, m2c=m2c, dyc=dyc, dva_snap=dva_snap):
                        if ei == 0:
                            W(v, "dva", dva_snap)
                        for oc in range(DCH):
                            v.wait_ge(p.sems["m2"], m2c + oc + 1)
                            src = bankD[:, :] if oc % 2 == 0 else bankE[:, :]
                            dst = yaccT[:, oc * 1024 + tq * 512:oc * 1024 + (tq + 1) * 512]
                            if ei == 0:
                                ins = v.tensor_copy(dst, src)
                            else:
                                ins = v.tensor_add(dst, dst, src)
                            p.inc(ins, "dy", 1)
                            v.drain()

            dy_half = p.c["dy"]
            for tq in range(2):
                tt = hf * 2 + tq

                @blk.sync
                def _(sy, tt=tt):
                    if tt > 0:
                        sy.wait_ge(p.sems["dva"], tt * 34)
                    p.dma(sy, "lx", _chunked_sbuf_ap(xf, DCH, 512),
                          x3T.rearrange("(c p) t -> p c t", p=128)[:, :, tt * 512:(tt + 1) * 512])
                lx_now = p.c["lx0"]

                mpb = p.c["mp"]
                dvab = p.c["dva"]

                @blk.vector
                def _(v, hf=hf, tq=tq, tt=tt, mpb=mpb, dvab=dvab, dy_half=dy_half,
                      lx_now=lx_now):
                    v.wait_ge(p.sems["dy"], dy_half)
                    v.wait_ge(p.sems["lx0"], lx_now)
                    for oc in range(DCH):
                        W(v, "mp", mpb + oc)
                        v.scalar_tensor_tensor(ts[:, oc * 512:(oc + 1) * 512],
                                               xf[:, oc * 512:(oc + 1) * 512], 2.0,
                                               yaccT[:, oc * 1024 + tq * 512:oc * 1024 + (tq + 1) * 512],
                                               ALU.mult, ALU.add)
                        p.inc(v.engine_nop(), "dva", 1)
                        v.drain()
                        ins = v.tensor_copy(tb[:, :], ts[:, oc * 512:(oc + 1) * 512])
                        p.inc(ins, "dva", 1)
                        v.drain()
                        ins = v.tensor_mul(sqb[:, :], tb[:, :], tb[:, :])
                        p.inc(ins, "dva", 1)

                @blk.tensor
                def _(t, tt=tt, mpb=mpb, dvab=dvab):
                    for c in range(DCH):
                        t.wait_ge(p.sems["dva"], dvab + 3 * c + 3)
                        t.matmul(bankF[0:4, :], ones128c[:, :], tb[:, :],
                                 start=(c == 0), stop=(c == DCH - 1), skip_group_check=True)
                        ins = t.matmul(bankG[0:4, :], ones128c[:, :], sqb[:, :],
                                       start=(c == 0), stop=(c == DCH - 1), skip_group_check=True)
                        p.inc(ins, "mp", 1)

                emit_ln_tail2(p, blk, dvab, mpb, tt, ts, tb, sqb, tmpf, x1c, rows,
                              ones128c, ones1x128f, bankF, bankG, yT)

        @blk.gpsimd
        def _(g):
            p.gwait(g, "so")
    return nc



L1SEG = 320
L1NSL = 16 * L1SEG


def build_s_l1slot():
    p = P("s_l1slot")
    nc = p.nc
    E = 16
    NT = 16
    CAP = 2560.0
    m1i = p.inp("m1i", [128, NT * E])
    m2i = p.inp("m2i", [128, NT * E])
    counts_all = p.inp("counts_all", [8, E])
    core_sel = p.inp("core_sel", [8, 1])
    dst1o = p.out("dst1o", [128, NT])
    dst2o = p.out("dst2o", [128, NT])
    with p.es:
        m1s = p.sb("m1s", [128, NT * E], F32)
        m2s = p.sb("m2s", [128, NT * E], F32)
        ca = p.sb("ca", [8, E], F32)
        cse = p.sb("cse", [8, 1], F32)
        B8 = p.sb("B8", [8, 8], F32)
        m8 = p.sb("m8", [8, 1], F32)
        offsf = p.sb("offsf", [16, 1], F32)
        acc = p.sb("acc", [16, 1], F32)
        triL = p.sb("triL", [128, 128], BF16)
        identf = p.sb("identf", [128, 128], F32)
        ri128 = p.sb("ri128", [128, 128], F32)
        ci128 = p.sb("ci128", [128, 128], F32)
        ones128c = p.sb("ones128c", [128, 4], BF16)
        cnt2 = p.sb("cnt2", [128, E], BF16)
        grT = p.sb("grT", [16, 128], F32)
        t16 = p.sb("t16", [128, E], F32)
        tm = p.sb("tm", [128, E], F32)
        rka = [p.sb("rka%d" % k, [128, NT], F32) for k in range(2)]
        eka = [p.sb("eka%d" % k, [128, NT], F32) for k in range(2)]
        qk = p.sb("qk", [128, NT], F32)
        q2 = p.sb("q2", [128, NT], F32)
        kp = p.sb("kp", [128, NT], F32)
        d1s = p.sb("d1s", [128, NT], F32)
        d2s = p.sb("d2s", [128, NT], F32)
        bankC = p.ps("bankC", [128, 512])
        bankD = p.ps("bankD", [128, 512])
        bankE = p.ps("bankE", [128, 512])
        for s_ in ("vm", "mm", "dq"):
            p.sem(s_)
        p.dma_group("ld", 2)
        p.dma_group("so", 2)
        blk = p.block()

        def W(eng, name, val):
            if val > 0:
                eng.wait_ge(p.sems[name], val)

        @blk.gpsimd
        def _(g):
            p.dma(g, "ld", m1s[:, :], m1i[:, :])
            p.dma(g, "ld", m2s[:, :], m2i[:, :])
            p.dma(g, "ld", ca[:, :], counts_all[:, :])
            p.dma(g, "ld", cse[:, :], core_sel[:, :])

        @blk.vector
        def _(v):
            v.memset(triL[:, :], 1.0)
            v.memset(identf[:, :], 1.0)
            v.memset(B8[:, :], 1.0)
            v.memset(ones128c[:, :], 1.0)
            p.inc(v.engine_nop(), "vm", 1)

        @blk.gpsimd
        def _(g):
            g.wait_ge(p.sems["vm"], 1)
            g.iota(ri128[:, :], [[0, 128]], channel_multiplier=1,
                   allow_small_or_imprecise_dtypes=True)
            g.iota(ci128[:, :], [[1, 128]], channel_multiplier=0,
                   allow_small_or_imprecise_dtypes=True)
            ins = g.affine_select(identf[:, :], identf[:, :], [[-1, 128]],
                                  ALU.is_equal, 0.0, channel_multiplier=1)
            p.inc(ins, "vm", 1)

        @blk.vector
        def _(v):
            v.wait_ge(p.sems["vm"], 2)
            v.tensor_tensor(triL[:, :], ri128[:, :], ci128[:, :], ALU.is_lt)
            ins = v.tensor_tensor(B8[:, :], ri128[0:8, 0:8], ci128[0:8, 0:8],
                                  ALU.is_gt)
            p.inc(ins, "vm", 1)

        @blk.tensor
        def _(t):
            p.gwait(t, "ld")
            t.wait_ge(p.sems["vm"], 3)
            ins = t.matmul(bankC[0:8, 0:1], B8[:, :], cse[:, :], start=True, stop=True)
            p.inc(ins, "mm", 1)

        @blk.vector
        def _(v):
            v.wait_ge(p.sems["mm"], 1)
            ins = v.tensor_copy(m8[:, :], bankC[0:8, 0:1])
            p.inc(ins, "dq", 1)

        @blk.tensor
        def _(t):
            t.wait_ge(p.sems["dq"], 1)
            ins = t.matmul(bankC[0:16, 0:1], ca[:, :], m8[:, :], start=True, stop=True)
            p.inc(ins, "mm", 1)

        @blk.vector
        def _(v):
            v.wait_ge(p.sems["mm"], 2)
            v.tensor_copy(offsf[:, :], bankC[0:16, 0:1])
            v.drain()
            ins = v.tensor_copy(acc[:, :], offsf[:, :])
            p.inc(ins, "dq", 1)

        for tc in range(NT):
            dqb = p.c["dq"]; mmb = p.c["mm"]

            @blk.vector
            def _(v, tc=tc, mmb=mmb):
                W(v, "mm", mmb - 1)
                ins = v.tensor_add(cnt2[:, :], m1s[:, tc * E:(tc + 1) * E],
                                   m2s[:, tc * E:(tc + 1) * E])
                p.inc(ins, "dq", 1)

            @blk.tensor
            def _(t, dqb=dqb):
                t.wait_ge(p.sems["dq"], dqb + 1)
                t.matmul(bankC[0:16, 0:128], cnt2[:, :], triL[:, :], start=True,
                         stop=True, skip_group_check=True)
                ins = t.matmul(bankD[0:16, 0:4], cnt2[:, :], ones128c[:, :],
                               start=True, stop=True, skip_group_check=True)
                p.inc(ins, "mm", 2)

            @blk.vector
            def _(v, dqb=dqb, mmb=mmb):
                v.wait_ge(p.sems["mm"], mmb + 2)
                v.tensor_single_scalar(grT[:, :], bankC[0:16, 0:128], acc[0:16, 0:1],
                                       ALU.add)
                v.drain()
                ins = v.tensor_add(acc[:, :], acc[:, :], bankD[0:16, 0:1])
                p.inc(ins, "dq", 1)

            @blk.tensor
            def _(t, dqb=dqb):
                t.wait_ge(p.sems["dq"], dqb + 2)
                ins = t.transpose(bankE[:, 0:16], grT[:, :], identf[0:16, 0:16])
                p.inc(ins, "mm", 1)

            @blk.vector
            def _(v, tc=tc, mmb=mmb):
                v.wait_ge(p.sems["mm"], mmb + 3)
                v.tensor_copy(t16[:, :], bankE[:, 0:16])
                v.drain()
                for k, mk in ((0, m1s), (1, m2s)):
                    v.tensor_mul(tm[:, :], mk[:, tc * E:(tc + 1) * E], t16[:, :])
                    v.drain()
                    v.reduce_sum(rka[k][:, tc:tc + 1], tm[:, :], axis=AX)
                    v.drain()
                    v.tensor_mul(tm[:, :], mk[:, tc * E:(tc + 1) * E], ci128[:, 0:E])
                    v.drain()
                    v.reduce_sum(eka[k][:, tc:tc + 1], tm[:, :], axis=AX)
                    v.drain()
                p.inc(v.engine_nop(), "dq", 1)

        @blk.vector
        def _(v):
            v.wait_ge(p.sems["dq"], p.c["dq"])
            for k, dXs in ((0, d1s), (1, d2s)):
                rk, ek = rka[k], eka[k]
                v.tensor_single_scalar(qk[:, :], rk[:, :], L1SEG - 0.5, ALU.is_gt)
                v.tensor_single_scalar(kp[:, :], rk[:, :], CAP - 0.5, ALU.is_lt)
                v.drain()
                for j in range(2, 8):
                    v.tensor_single_scalar(q2[:, :], rk[:, :], j * L1SEG - 0.5,
                                           ALU.is_gt)
                    v.drain()
                    v.tensor_add(qk[:, :], qk[:, :], q2[:, :])
                    v.drain()
                v.scalar_tensor_tensor(ek[:, :], ek[:, :], float(L1SEG), rk[:, :],
                                       ALU.mult, ALU.add)
                v.drain()
                v.scalar_tensor_tensor(ek[:, :], qk[:, :], float(15 * L1SEG),
                                       ek[:, :], ALU.mult, ALU.add)
                v.drain()
                v.tensor_scalar_add(ek[:, :], ek[:, :], 1.0)
                v.drain()
                v.tensor_mul(ek[:, :], ek[:, :], kp[:, :])
                v.drain()
                v.tensor_scalar_add(dXs[:, :], ek[:, :], -1.0)
                v.drain()
            p.inc(v.engine_nop(), "dq", 1)
        dq_all = p.c["dq"]

        @blk.sync
        def _(sy):
            sy.wait_ge(p.sems["dq"], dq_all)
            p.dma(sy, "so", dst1o[:, :], d1s[:, :])
            p.dma(sy, "so", dst2o[:, :], d2s[:, :])

        @blk.gpsimd
        def _(g):
            p.gwait(g, "so")
    return nc


def build_s_l1ffn_r():
    p = P("s_l1ffn_r")
    nc = p.nc
    E = 16
    FC = 11
    x3T = p.inp("x3T", [D, T])
    xgi = p.inp("xgi", [D, L1NSL])
    gsl = p.inp("gsl", [1, L1NSL])
    w_gate = p.inp("w_gate", [E, D, 1408])
    w_up = p.inp("w_up", [E, D, 1408])
    w_down = p.inp("w_down", [E, 1408, D])
    sw_gate = p.inp("sw_gate", [D, 1408])
    sw_up = p.inp("sw_up", [D, 1408])
    sw_down = p.inp("sw_down", [1408, D])
    ygT = p.out("ygT", [D, L1NSL], BF16)
    yshT = p.out("yshT", [D, T], BF16)
    with p.es:
        x3Tb = p.sb("x3Tb", [128, DCH * T], BF16)
        xgb = [p.sb("xgb%d" % i, [128, DCH * L1SEG], BF16) for i in range(2)]
        gsr = [p.sb("gsr%d" % i, [1, L1SEG], BF16) for i in range(2)]
        gateS = p.sb("gateS", [128, L1SEG], F32)
        wgp = [p.sb("wgp%d" % i, [128, 2 * DCH * 3 * 128], BF16) for i in range(2)]
        wdb = p.sb("wdb", [128, FC * D], BF16)
        sgE = p.sb("sgE", [128, 512], BF16)
        t1f = p.sb("t1f", [128, 512], F32)
        glg = p.sb("glg", [128, FC * 512], BF16)
        ygb = [p.sb("ygb%d" % i, [128, DCH * 512], BF16) for i in range(2)]
        onesr128 = p.sb("onesr128", [1, 128], BF16)
        bankA = p.ps("bankA", [128, 512])
        bankB = p.ps("bankB", [128, 512])
        bankD = p.ps("bankD", [128, 512])
        bankE = p.ps("bankE", [128, 512])
        bankG = p.ps("bankG", [128, 512])
        for s_ in ("vm", "m1", "sg", "da", "m2", "dy"):
            p.sem(s_)
        p.dma_group("lw", 2)
        p.dma_group("lt", 2)
        p.dma_group("lg", 2)
        p.dma_group("lx", 1)
        p.dma_group("so", 2)
        blk = p.block()

        def W(eng, name, val):
            if val > 0:
                eng.wait_ge(p.sems[name], val)

        @blk.gpsimd
        def _(g):
            p.dma(g, "lx", _chunked_sbuf_ap(x3Tb, DCH, T), _featmaj_ap(x3T, D, T))

        @blk.vector
        def _(v):
            v.memset(onesr128[:, :], 1.0)
            p.inc(v.engine_nop(), "vm", 1)

        units = [(-1, tt) for tt in range(NTT)] + [(e, 0) for e in range(E)]
        xg_guard = [0, 0]
        yg_tok = [None, None]
        wg_guard = [0, 0]
        piece_lw = {}
        pc_idx = [0]
        wd_lw = [(0, 0)]
        for ui, (e, tt) in enumerate(units):
            buf = ui % 2
            nco = L1SEG if e >= 0 else 512
            m1b = p.c["m1"]; m2b = p.c["m2"]; dab = p.c["da"]
            sgb_ = p.c["sg"]; dyb = p.c["dy"]
            off = 1 if e >= 0 else 0

            wg_src = sw_gate if e < 0 else w_gate[e, :, :]
            wu_src = sw_up if e < 0 else w_up[e, :, :]
            wd_src = sw_down if e < 0 else w_down[e, :, :]
            pieces = [(0, 3), (3, 3), (6, 3), (9, 2)]

            if e >= 0:
                @blk.gpsimd
                def _(g, e=e, buf=buf, xgg=xg_guard[buf]):
                    W(g, "m1", xgg)
                    p.dma(g, "lt", _chunked_sbuf_ap(xgb[buf], DCH, L1SEG),
                          _featmaj_ap(xgi, D, L1NSL)[:, :, e * L1SEG:(e + 1) * L1SEG])
                    p.dma(g, "lg", gsr[buf][:, :], gsl[:, e * L1SEG:(e + 1) * L1SEG])
            lt0_now = p.c["lt0"]; lt1_now = p.c["lt1"]
            lg0_now = p.c["lg0"]; lg1_now = p.c["lg1"]

            if (e < 0 and tt == 0) or e >= 0:
                @blk.gpsimd
                def _(g, m2b=m2b):
                    W(g, "m2", m2b)
                    p.dma(g, "lw", _chunked_sbuf_ap(wdb, FC, D),
                          _featmaj_ap(wd_src, 1408, D))
                wd_lw[0] = (p.c["lw0"], p.c["lw1"])
            if True:
                for pi, (f0, nf) in enumerate(pieces):
                    pb = (pc_idx[0] + pi) % 2
                    if pi < 2:
                        gd = wg_guard[pb]
                    else:
                        gd = p.c["m1"] + off + 6 * (pi - 1)

                    @blk.gpsimd
                    def _(g, pb=pb, f0=f0, nf=nf, gd=gd):
                        W(g, "m1", gd)
                        p.dma(g, "lw",
                              bass.AP(wgp[pb], 0, [[2 * DCH * 384, 128], [384, DCH], [1, nf * 128]]),
                              _featmaj_ap(wg_src, D, 1408)[:, :, f0 * 128:(f0 + nf) * 128])
                        p.dma(g, "lw",
                              bass.AP(wgp[pb], DCH * 384, [[2 * DCH * 384, 128], [384, DCH], [1, nf * 128]]),
                              _featmaj_ap(wu_src, D, 1408)[:, :, f0 * 128:(f0 + nf) * 128])
                    piece_lw[(ui, pi)] = (p.c["lw0"], p.c["lw1"])

            pcb = pc_idx[0]
            wg_done = wg_guard

            @blk.tensor
            def _(t, e=e, tt=tt, buf=buf, ui=ui, nco=nco, m1b=m1b, dab=dab,
                  sgb_=sgb_, off=off, pcb=pcb, wg_done=wg_done,
                  lt0_now=lt0_now, lt1_now=lt1_now, lg0_now=lg0_now,
                  lg1_now=lg1_now):
                if ui == 0:
                    p.gwait(t, "lx")
                    t.wait_ge(p.sems["vm"], 1)
                if e >= 0:
                    W(t, "lt0", lt0_now)
                    W(t, "lt1", lt1_now)
                    W(t, "lg0", lg0_now)
                    W(t, "lg1", lg1_now)
                    W(t, "da", dab)
                    ins = t.matmul(bankG[:, 0:L1SEG], onesr128[:, :], gsr[buf][:, :],
                                   start=True, stop=True)
                    p.inc(ins, "m1", 1)

                def mv(dc):
                    if e >= 0:
                        return xgb[buf][:, dc * L1SEG:(dc + 1) * L1SEG]
                    return x3Tb[:, dc * T + tt * 512:dc * T + (tt + 1) * 512]

                for fc in range(FC):
                    pi = min(fc // 3, 3)
                    pb = (pcb + pi) % 2
                    fo = fc - [0, 3, 6, 9][pi]
                    lw = piece_lw[(ui, pi)]
                    W(t, "lw0", lw[0])
                    W(t, "lw1", lw[1])
                    W(t, "sg", sgb_ + fc)
                    for dc in range(DCH):
                        ins = t.matmul(bankA[:, 0:nco],
                                       wgp[pb][:, dc * 384 + fo * 128:dc * 384 + (fo + 1) * 128],
                                       mv(dc), start=(dc == 0), stop=(dc == DCH - 1))
                    p.inc(ins, "m1", 1)
                    W(t, "da", dab + off + fc)
                    for dc in range(DCH):
                        ins = t.matmul(bankB[:, 0:nco],
                                       wgp[pb][:, (DCH + dc) * 384 + fo * 128:(DCH + dc) * 384 + (fo + 1) * 128],
                                       mv(dc), start=(dc == 0), stop=(dc == DCH - 1))
                    p.inc(ins, "m1", 1)
                    if fc in (2, 5, 8, 10):
                        wg_done[pb] = p.c["m1"]
            xg_guard[buf] = p.c["m1"]
            pc_idx[0] += 4

            @blk.scalar
            def _(s, e=e, nco=nco, m1b=m1b, dab=dab, sgb_=sgb_, off=off):
                for fc in range(FC):
                    s.wait_ge(p.sems["m1"], m1b + off + 2 * fc + 1)
                    W(s, "da", dab + off + fc)
                    ins = s.activation(sgE[:, 0:nco], bankA[:, 0:nco], AF.Silu)
                    p.inc(ins, "sg", 1)

            @blk.vector
            def _(v, e=e, nco=nco, m1b=m1b, dab=dab, sgb_=sgb_, m2b=m2b, off=off):
                W(v, "m2", m2b)
                if e >= 0:
                    v.wait_ge(p.sems["m1"], m1b + 1)
                    ins = v.tensor_copy(gateS[:, :], bankG[:, 0:L1SEG])
                    p.inc(ins, "da", 1)
                    v.drain()
                for fc in range(FC):
                    v.wait_ge(p.sems["m1"], m1b + off + 2 * fc + 2)
                    v.wait_ge(p.sems["sg"], sgb_ + fc + 1)
                    v.tensor_mul(t1f[:, 0:nco], bankB[:, 0:nco], sgE[:, 0:nco])
                    v.drain()
                    if e >= 0:
                        v.tensor_mul(t1f[:, 0:nco], t1f[:, 0:nco], gateS[:, :])
                        v.drain()
                    ins = v.tensor_copy(glg[:, fc * 512:fc * 512 + nco], t1f[:, 0:nco])
                    p.inc(ins, "da", 1)
                    v.drain()
            da_glg = p.c["da"]
            m2c = p.c["m2"]; dyc = p.c["dy"]

            @blk.tensor
            def _(t, e=e, nco=nco, da_glg=da_glg, dyc=dyc, wlw=wd_lw[0]):
                W(t, "lw0", wlw[0])
                W(t, "lw1", wlw[1])
                t.wait_ge(p.sems["da"], da_glg)
                for oc in range(DCH):
                    W(t, "dy", dyc + oc - 1)
                    for fc in range(FC):
                        pa = bankD if oc % 2 == 0 else bankE
                        ins = t.matmul(pa[:, 0:nco],
                                       wdb[:, fc * D + oc * 128:fc * D + (oc + 1) * 128],
                                       glg[:, fc * 512:fc * 512 + nco],
                                       start=(fc == 0), stop=(fc == FC - 1))
                    p.inc(ins, "m2", 1)

            @blk.scalar
            def _(s, e=e, nco=nco, buf=buf, m2c=m2c, dyc=dyc, tok=yg_tok[buf]):
                for oc in range(DCH):
                    s.wait_ge(p.sems["m2"], m2c + oc + 1)
                    if oc == 0 and tok is not None:
                        s.wait_ge(p.sems[tok[0]], tok[1])
                    pa = bankD if oc % 2 == 0 else bankE
                    ins = s.copy(ygb[buf][:, oc * 512:oc * 512 + nco], pa[:, 0:nco])
                    p.inc(ins, "dy", 1)
            dy_done = p.c["dy"]

            @blk.sync
            def _(sy, e=e, tt=tt, buf=buf, dy_done=dy_done):
                sy.wait_ge(p.sems["dy"], dy_done)
                if e >= 0:
                    tok = p.dma(sy, "so",
                                ygT.rearrange("(c p) s -> p c s", p=128)[:, :, e * L1SEG:(e + 1) * L1SEG],
                                bass.AP(ygb[buf], 0, [[DCH * 512, 128], [512, DCH], [1, L1SEG]]))
                else:
                    tok = p.dma(sy, "so",
                                yshT.rearrange("(c p) s -> p c s", p=128)[:, :, tt * 512:(tt + 1) * 512],
                                _chunked_sbuf_ap(ygb[buf], DCH, 512))
                yg_tok[buf] = tok

        @blk.gpsimd
        def _(g):
            p.gwait(g, "so")
    return nc


def build_s_l1comb():
    p = P("s_l1comb")
    nc = p.nc
    x3T = p.inp("x3T", [D, T])
    yshT = p.inp("yshT", [D, T], BF16)
    y1T = p.inp("y1T", [D, T], BF16)
    y2T = p.inp("y2T", [D, T], BF16)
    yT = p.out("yT", [D, T])
    with p.es:
        xf = p.sb("xf", [128, DCH * 512], F32)
        ysc = p.sb("ysc", [128, DCH * 512], BF16)
        y1c = p.sb("y1c", [128, DCH * 512], BF16)
        y2c = p.sb("y2c", [128, DCH * 512], BF16)
        ts = p.sb("ts", [128, DCH * 512], F32)
        tb = p.sb("tb", [128, 512], BF16)
        sqb = p.sb("sqb", [128, 512], BF16)
        tmpf = p.sb("tmpf", [128, 512], F32)
        x1c = [p.sb("x1c%d" % i, [128, 512], F32) for i in range(2)]
        _r = {n: p.sb(n, [1, 512], F32) for n in ("mrow", "m2row", "vrow")}
        rows = (_r["mrow"], _r["m2row"], _r["vrow"], _r["vrow"], _r["m2row"])
        ones128c = p.sb("ones128c", [128, 4], BF16)
        ones1x128f = p.sb("ones1x128f", [1, 128], F32)
        bankE = p.ps("bankE", [128, 512])
        bankF = p.ps("bankF", [128, 512])
        for s_ in ("vm", "mp", "dva", "xr", "mn"):
            p.sem(s_)
        p.dma_group("lx", 2)
        p.dma_group("so", 2)
        blk = p.block()

        def W(eng, name, val):
            if val > 0:
                eng.wait_ge(p.sems[name], val)

        @blk.vector
        def _(v):
            v.memset(ones128c[:, :], 1.0)
            v.memset(ones1x128f[:, :], 1.0)
            p.inc(v.engine_nop(), "vm", 1)

        for tt in range(NTT):
            @blk.sync
            def _(sy, tt=tt):
                if tt > 0:
                    sy.wait_ge(p.sems["dva"], tt * 34)
                p.dma(sy, "lx", _chunked_sbuf_ap(xf, DCH, 512),
                      x3T.rearrange("(c p) t -> p c t", p=128)[:, :, tt * 512:(tt + 1) * 512])
                p.dma(sy, "lx", _chunked_sbuf_ap(ysc, DCH, 512),
                      yshT.rearrange("(c p) t -> p c t", p=128)[:, :, tt * 512:(tt + 1) * 512])
                p.dma(sy, "lx", _chunked_sbuf_ap(y1c, DCH, 512),
                      y1T.rearrange("(c p) t -> p c t", p=128)[:, :, tt * 512:(tt + 1) * 512])
                p.dma(sy, "lx", _chunked_sbuf_ap(y2c, DCH, 512),
                      y2T.rearrange("(c p) t -> p c t", p=128)[:, :, tt * 512:(tt + 1) * 512])
            lx0_now = p.c["lx0"]; lx1_now = p.c["lx1"]
            mpb = p.c["mp"]; dvab = p.c["dva"]

            @blk.vector
            def _(v, tt=tt, mpb=mpb, lx0_now=lx0_now, lx1_now=lx1_now):
                v.wait_ge(p.sems["vm"], 1)
                v.wait_ge(p.sems["lx0"], lx0_now)
                v.wait_ge(p.sems["lx1"], lx1_now)
                for oc in range(DCH):
                    W(v, "mp", mpb + oc)
                    sl = slice(oc * 512, (oc + 1) * 512)
                    v.scalar_tensor_tensor(ts[:, sl], xf[:, sl], 2.0, ysc[:, sl],
                                           ALU.mult, ALU.add)
                    v.drain()
                    v.tensor_add(ts[:, sl], ts[:, sl], y1c[:, sl])
                    v.drain()
                    ins = v.tensor_add(ts[:, sl], ts[:, sl], y2c[:, sl])
                    p.inc(ins, "dva", 1)
                    v.drain()
                    ins = v.tensor_copy(tb[:, :], ts[:, sl])
                    p.inc(ins, "dva", 1)
                    v.drain()
                    ins = v.tensor_mul(sqb[:, :], tb[:, :], tb[:, :])
                    p.inc(ins, "dva", 1)

            @blk.tensor
            def _(t, tt=tt, mpb=mpb, dvab=dvab):
                for c in range(DCH):
                    t.wait_ge(p.sems["dva"], dvab + 3 * c + 3)
                    t.matmul(bankE[0:4, :], ones128c[:, :], tb[:, :],
                             start=(c == 0), stop=(c == DCH - 1), skip_group_check=True)
                    ins = t.matmul(bankF[0:4, :], ones128c[:, :], sqb[:, :],
                                   start=(c == 0), stop=(c == DCH - 1), skip_group_check=True)
                    p.inc(ins, "mp", 1)

            emit_ln_tail2(p, blk, dvab, mpb, tt, ts, tb, sqb, tmpf, x1c, rows,
                          ones128c, ones1x128f, bankE, bankF, yT)

        @blk.gpsimd
        def _(g):
            p.gwait(g, "so")
    return nc



def build_s_l0rout():
    p = P("s_l0rout")
    nc = p.nc
    E = 8
    NT = T // 128
    x1T = p.inp("x1T", [D, T])
    Wgt = p.inp("Wgt", [D, E])
    m1o = p.out("m1o", [128, NT * E])
    m2o = p.out("m2o", [128, NT * E])
    w1o = p.out("w1o", [128, NT])
    w2o = p.out("w2o", [128, NT])
    counts = p.out("counts", [1, E])
    with p.es:
        xrc = p.sb("xrc", [128, DCH * 128], F32)
        wrf = p.sb("wrf", [128, DCH * E], F32)
        ones128c = p.sb("ones128c", [128, 4], BF16)
        nm = p.sb("nm", [128, 1], F32)
        e16 = p.sb("e16", [128, E], F32)
        se = p.sb("se", [128, 1], F32)
        gs = p.sb("gs", [128, E], F32)
        m1v = p.sb("m1v", [128, 1], F32)
        m2v = p.sb("m2v", [128, 1], F32)
        msk1 = p.sb("msk1", [128, NT * E], F32)
        msk2 = p.sb("msk2", [128, NT * E], F32)
        g2 = p.sb("g2", [128, E], F32)
        den = p.sb("den", [128, 1], F32)
        w1v = p.sb("w1v", [128, NT], F32)
        w2v = p.sb("w2v", [128, NT], F32)
        cntb = p.sb("cntb", [128, E], BF16)
        crow = p.sb("crow", [1, E], F32)
        bankA = p.ps("bankA", [128, 512])
        bankB = p.ps("bankB", [128, 512])
        bankC = p.ps("bankC", [128, 512])
        for s_ in ("vm", "mm", "xp", "dq", "mc", "dc"):
            p.sem(s_)
        p.dma_group("ld", 2)
        p.dma_group("lr", 1)
        p.dma_group("so", 2)
        blk = p.block()

        def W(eng, name, val):
            if val > 0:
                eng.wait_ge(p.sems[name], val)

        @blk.gpsimd
        def _(g):
            p.dma(g, "ld", _chunked_sbuf_ap(wrf, DCH, E), _featmaj_ap(Wgt, D, E))
            for tc in range(NT):
                if tc >= 1:
                    g.wait_ge(p.sems["mm"], tc)
                p.dma(g, "lr", _chunked_sbuf_ap(xrc, DCH, 128),
                      x1T.rearrange("(c p) t -> p c t", p=128)[:, :, tc * 128:(tc + 1) * 128])

        @blk.vector
        def _(v):
            v.memset(ones128c[:, :], 1.0)
            p.inc(v.engine_nop(), "vm", 1)

        pRb = [bankA[:, 0:E], bankB[:, 0:E]]

        @blk.tensor
        def _(t):
            p.gwait(t, "ld")
            t.wait_ge(p.sems["vm"], 1)

            def cnt_mm(j):
                t.wait_ge(p.sems["dq"], 18 * (j + 1))
                ins = t.matmul(bankC[0:4, :E], ones128c[:, :], cntb[:, :],
                               start=(j == 0), stop=(j == NT - 1), skip_group_check=True)
                p.inc(ins, "mc", 1)

            for tc in range(NT):
                t.wait_ge(p.sems["lr0"], 16 * (tc + 1))
                W(t, "xp", tc - 1)
                for dc in range(DCH):
                    ins = t.matmul(pRb[tc % 2], xrc[:, dc * 128:(dc + 1) * 128],
                                   wrf[:, dc * E:(dc + 1) * E],
                                   start=(dc == 0), stop=(dc == DCH - 1),
                                   skip_group_check=True)
                p.inc(ins, "mm", 1)
                if tc >= 2:
                    cnt_mm(tc - 2)
            cnt_mm(NT - 2)
            cnt_mm(NT - 1)

        @blk.scalar
        def _(s):
            for tc in range(NT):
                s.wait_ge(p.sems["dq"], 18 * tc + 1)
                ins = s.activation(e16[:, :], pRb[tc % 2], AF.Exp, bias=nm[:, :],
                                   accum_out=se[:, :])
                p.inc(ins, "xp", 1)

        @blk.vector
        def _(v):
            for tc in range(NT):
                v.wait_ge(p.sems["mm"], tc + 1)
                W(v, "xp", tc)
                ins = v.reduce_max(nm[:, :], pRb[tc % 2], axis=AX, negate=True)
                p.inc(ins, "dq", 1)
                v.wait_ge(p.sems["xp"], tc + 1)
                v.reciprocal(den[:, :], se[:, :])
                v.drain()
                v.tensor_single_scalar(gs[:, :], e16[:, :], den[0:128, 0:1], ALU.mult)
                v.drain()
                v.reduce_max(m1v[:, :], gs[:, :], axis=AX)
                v.drain()
                v.tensor_single_scalar(msk1[:, tc * E:(tc + 1) * E], gs[:, :],
                                       m1v[0:128, 0:1], ALU.is_ge)
                v.drain()
                v.scalar_tensor_tensor(g2[:, :], msk1[:, tc * E:(tc + 1) * E], -1e30,
                                       gs[:, :], ALU.mult, ALU.add)
                v.drain()
                v.reduce_max(m2v[:, :], g2[:, :], axis=AX)
                v.drain()
                v.tensor_single_scalar(msk2[:, tc * E:(tc + 1) * E], g2[:, :],
                                       m2v[0:128, 0:1], ALU.is_ge)
                v.tensor_add(den[:, :], m1v[:, :], m2v[:, :])
                v.drain()
                v.tensor_scalar_add(den[:, :], den[:, :], 1e-9)
                v.drain()
                v.reciprocal(den[:, :], den[:, :])
                v.drain()
                v.tensor_single_scalar(w1v[:, tc:tc + 1], m1v[:, :], den[0:128, 0:1], ALU.mult)
                v.tensor_single_scalar(w2v[:, tc:tc + 1], m2v[:, :], den[0:128, 0:1], ALU.mult)
                v.drain()
                W(v, "mc", tc)
                ins = v.tensor_add(cntb[:, :], msk1[:, tc * E:(tc + 1) * E],
                                   msk2[:, tc * E:(tc + 1) * E])
                p.inc(ins, "dq", 17)

        @blk.vector
        def _(v):
            v.wait_ge(p.sems["mc"], NT)
            ins = v.tensor_copy(crow[:, :], bankC[0:1, 0:E])
            p.inc(ins, "dc", 1)

        @blk.sync
        def _(sy):
            sy.wait_ge(p.sems["dq"], 18 * NT)
            p.dma(sy, "so", m1o[:, :], msk1[:, :])
            p.dma(sy, "so", m2o[:, :], msk2[:, :])
            p.dma(sy, "so", w1o[:, :], w1v[:, :])
            p.dma(sy, "so", w2o[:, :], w2v[:, :])
            sy.wait_ge(p.sems["dc"], 1)
            p.dma(sy, "so", counts[:, :], crow[:, :])

        @blk.gpsimd
        def _(g):
            p.gwait(g, "so")
    return nc


def build_s_l0slot(segs):
    p = P("s_l0slot")
    nc = p.nc
    E = 8
    NT = 16
    bases = [sum(segs[:e]) for e in range(E)]
    nsl = sum(segs)
    m1i = p.inp("m1i", [128, NT * E])
    m2i = p.inp("m2i", [128, NT * E])
    counts_all = p.inp("counts_all", [8, E])
    core_sel = p.inp("core_sel", [8, 1])
    dst1o = p.out("dst1o", [128, NT])
    dst2o = p.out("dst2o", [128, NT])
    with p.es:
        m1s = p.sb("m1s", [128, NT * E], F32)
        m2s = p.sb("m2s", [128, NT * E], F32)
        ca = p.sb("ca", [8, E], F32)
        cse = p.sb("cse", [8, 1], F32)
        B8 = p.sb("B8", [8, 8], F32)
        m8 = p.sb("m8", [8, 1], F32)
        offsf = p.sb("offsf", [E, 1], F32)
        acc = p.sb("acc", [E, 1], F32)
        triL = p.sb("triL", [128, 128], BF16)
        identf = p.sb("identf", [128, 128], F32)
        ri128 = p.sb("ri128", [128, 128], F32)
        ci128 = p.sb("ci128", [128, 128], F32)
        ones128c = p.sb("ones128c", [128, 4], BF16)
        segbc = p.sb("segbc", [128, E], F32)
        basebc = p.sb("basebc", [128, E], F32)
        cnt2 = p.sb("cnt2", [128, E], BF16)
        grT = p.sb("grT", [E, 128], F32)
        t16 = p.sb("t16", [128, E], F32)
        tm = p.sb("tm", [128, E], F32)
        rka = [p.sb("rka%d" % k, [128, NT], F32) for k in range(2)]
        bka = [p.sb("bka%d" % k, [128, NT], F32) for k in range(2)]
        ska = [p.sb("ska%d" % k, [128, NT], F32) for k in range(2)]
        qk = p.sb("qk", [128, NT], F32)
        q2 = p.sb("q2", [128, NT], F32)
        d1s = p.sb("d1s", [128, NT], F32)
        d2s = p.sb("d2s", [128, NT], F32)
        bankC = p.ps("bankC", [128, 512])
        bankD = p.ps("bankD", [128, 512])
        bankE = p.ps("bankE", [128, 512])
        for s_ in ("vm", "mm", "dq"):
            p.sem(s_)
        p.dma_group("ld", 2)
        p.dma_group("so", 2)
        blk = p.block()

        def W(eng, name, val):
            if val > 0:
                eng.wait_ge(p.sems[name], val)

        @blk.gpsimd
        def _(g):
            p.dma(g, "ld", m1s[:, :], m1i[:, :])
            p.dma(g, "ld", m2s[:, :], m2i[:, :])
            p.dma(g, "ld", ca[:, :], counts_all[:, :])
            p.dma(g, "ld", cse[:, :], core_sel[:, :])

        @blk.vector
        def _(v):
            v.memset(triL[:, :], 1.0)
            v.memset(identf[:, :], 1.0)
            v.memset(B8[:, :], 1.0)
            v.memset(ones128c[:, :], 1.0)
            for e in range(E):
                v.memset(segbc[:, e:e + 1], float(segs[e]))
                v.memset(basebc[:, e:e + 1], float(bases[e]))
            p.inc(v.engine_nop(), "vm", 1)

        @blk.gpsimd
        def _(g):
            g.wait_ge(p.sems["vm"], 1)
            g.iota(ri128[:, :], [[0, 128]], channel_multiplier=1,
                   allow_small_or_imprecise_dtypes=True)
            g.iota(ci128[:, :], [[1, 128]], channel_multiplier=0,
                   allow_small_or_imprecise_dtypes=True)
            ins = g.affine_select(identf[:, :], identf[:, :], [[-1, 128]],
                                  ALU.is_equal, 0.0, channel_multiplier=1)
            p.inc(ins, "vm", 1)

        @blk.vector
        def _(v):
            v.wait_ge(p.sems["vm"], 2)
            v.tensor_tensor(triL[:, :], ri128[:, :], ci128[:, :], ALU.is_lt)
            ins = v.tensor_tensor(B8[:, :], ri128[0:8, 0:8], ci128[0:8, 0:8],
                                  ALU.is_gt)
            p.inc(ins, "vm", 1)

        @blk.tensor
        def _(t):
            p.gwait(t, "ld")
            t.wait_ge(p.sems["vm"], 3)
            ins = t.matmul(bankC[0:8, 0:1], B8[:, :], cse[:, :], start=True, stop=True)
            p.inc(ins, "mm", 1)

        @blk.vector
        def _(v):
            v.wait_ge(p.sems["mm"], 1)
            ins = v.tensor_copy(m8[:, :], bankC[0:8, 0:1])
            p.inc(ins, "dq", 1)

        @blk.tensor
        def _(t):
            t.wait_ge(p.sems["dq"], 1)
            ins = t.matmul(bankC[0:E, 0:1], ca[:, :], m8[:, :], start=True, stop=True)
            p.inc(ins, "mm", 1)

        @blk.vector
        def _(v):
            v.wait_ge(p.sems["mm"], 2)
            v.tensor_copy(offsf[:, :], bankC[0:E, 0:1])
            v.drain()
            ins = v.tensor_copy(acc[:, :], offsf[:, :])
            p.inc(ins, "dq", 1)

        for tc in range(NT):
            dqb = p.c["dq"]; mmb = p.c["mm"]

            @blk.vector
            def _(v, tc=tc, mmb=mmb):
                W(v, "mm", mmb - 1)
                ins = v.tensor_add(cnt2[:, :], m1s[:, tc * E:(tc + 1) * E],
                                   m2s[:, tc * E:(tc + 1) * E])
                p.inc(ins, "dq", 1)

            @blk.tensor
            def _(t, dqb=dqb):
                t.wait_ge(p.sems["dq"], dqb + 1)
                t.matmul(bankC[0:E, 0:128], cnt2[:, :], triL[:, :], start=True,
                         stop=True, skip_group_check=True)
                ins = t.matmul(bankD[0:E, 0:4], cnt2[:, :], ones128c[:, :],
                               start=True, stop=True, skip_group_check=True)
                p.inc(ins, "mm", 2)

            @blk.vector
            def _(v, dqb=dqb, mmb=mmb):
                v.wait_ge(p.sems["mm"], mmb + 2)
                v.tensor_single_scalar(grT[:, :], bankC[0:E, 0:128], acc[0:E, 0:1],
                                       ALU.add)
                v.drain()
                ins = v.tensor_add(acc[:, :], acc[:, :], bankD[0:E, 0:1])
                p.inc(ins, "dq", 1)

            @blk.tensor
            def _(t, dqb=dqb):
                t.wait_ge(p.sems["dq"], dqb + 2)
                ins = t.transpose(bankE[:, 0:E], grT[:, :], identf[0:E, 0:E])
                p.inc(ins, "mm", 1)

            @blk.vector
            def _(v, tc=tc, mmb=mmb):
                v.wait_ge(p.sems["mm"], mmb + 3)
                v.tensor_copy(t16[:, :], bankE[:, 0:E])
                v.drain()
                for k, mk in ((0, m1s), (1, m2s)):
                    v.tensor_mul(tm[:, :], mk[:, tc * E:(tc + 1) * E], t16[:, :])
                    v.drain()
                    v.reduce_sum(rka[k][:, tc:tc + 1], tm[:, :], axis=AX)
                    v.drain()
                    v.tensor_mul(tm[:, :], mk[:, tc * E:(tc + 1) * E], segbc[:, :])
                    v.drain()
                    v.reduce_sum(ska[k][:, tc:tc + 1], tm[:, :], axis=AX)
                    v.drain()
                    v.tensor_mul(tm[:, :], mk[:, tc * E:(tc + 1) * E], basebc[:, :])
                    v.drain()
                    v.reduce_sum(bka[k][:, tc:tc + 1], tm[:, :], axis=AX)
                    v.drain()
                p.inc(v.engine_nop(), "dq", 1)

        @blk.vector
        def _(v):
            v.wait_ge(p.sems["dq"], p.c["dq"])
            for k, dXs in ((0, d1s), (1, d2s)):
                rk, sk, bk = rka[k], ska[k], bka[k]
                v.tensor_scalar_add(q2[:, :], sk[:, :], -0.5)
                v.drain()
                v.tensor_tensor(qk[:, :], rk[:, :], q2[:, :], ALU.is_gt)
                v.drain()
                for j in range(2, 8):
                    v.tensor_scalar_mul(q2[:, :], sk[:, :], float(j))
                    v.drain()
                    v.tensor_scalar_add(q2[:, :], q2[:, :], -0.5)
                    v.drain()
                    v.tensor_tensor(q2[:, :], rk[:, :], q2[:, :], ALU.is_gt)
                    v.drain()
                    v.tensor_add(qk[:, :], qk[:, :], q2[:, :])
                    v.drain()
                v.tensor_mul(q2[:, :], qk[:, :], sk[:, :])
                v.drain()
                v.scalar_tensor_tensor(qk[:, :], qk[:, :], float(nsl), rk[:, :],
                                       ALU.mult, ALU.add)
                v.drain()
                v.tensor_add(qk[:, :], qk[:, :], bk[:, :])
                v.drain()
                v.tensor_sub(dXs[:, :], qk[:, :], q2[:, :])
                v.drain()
            p.inc(v.engine_nop(), "dq", 1)
        dq_all = p.c["dq"]

        @blk.sync
        def _(sy):
            sy.wait_ge(p.sems["dq"], dq_all)
            p.dma(sy, "so", dst1o[:, :], d1s[:, :])
            p.dma(sy, "so", dst2o[:, :], d2s[:, :])

        @blk.gpsimd
        def _(g):
            p.gwait(g, "so")
    return nc



def build_s_l0ffn_r(segs):
    p = P("s_l0ffn_r")
    nc = p.nc
    E = 8
    bases = [sum(segs[:e]) for e in range(E)]
    nsl = sum(segs)
    xgi = p.inp("xgi", [D, nsl])
    gsl = p.inp("gsl", [1, nsl])
    W1t = p.inp("W1t", [E, D, 4096])
    b1 = p.inp("b1", [E, 4096])
    hgT = p.out("hgT", [17 * 128, nsl], BF16)
    with p.es:
        whb = [p.sb("whb%d" % i, [128, DCH * 2048], BF16) for i in range(2)]
        b1h = [p.sb("b1h%d" % i, [1, 2048], BF16) for i in range(2)]
        xgb = [p.sb("xgb%d" % i, [128, DCH * 512], BF16) for i in range(2)]
        gsr = [p.sb("gsr%d" % i, [1, 512], BF16) for i in range(2)]
        gateS = p.sb("gateS", [128, 512], F32)
        sgE = p.sb("sgE", [128, 512], BF16)
        t1f = p.sb("t1f", [128, 512], F32)
        hgb = [p.sb("hgb%d" % i, [128, 9 * 512], BF16) for i in range(2)]
        onesr = p.sb("onesr", [1, 512], BF16)
        bankA = p.ps("bankA", [128, 512])
        bankB = p.ps("bankB", [128, 512])
        bankC = p.ps("bankC", [128, 512])
        bankD = p.ps("bankD", [128, 512])
        bankE = p.ps("bankE", [128, 512])
        for s_ in ("vm", "m1", "sg", "da", "mg"):
            p.sem(s_)
        p.dma_group("lw", 2)
        p.dma_group("lt", 2)
        p.dma_group("lg", 2)
        p.dma_group("so", 2)
        blk = p.block()

        def W(eng, name, val):
            if val > 0:
                eng.wait_ge(p.sems[name], val)

        @blk.vector
        def _(v):
            v.memset(onesr[:, :], 1.0)
            p.inc(v.engine_nop(), "vm", 1)

        units = []
        for e in range(E):
            for h in range(2):
                units.append((e, h))
        xg_guard = [0, 0]
        wh_guard = [0, 0]
        hg_tok = [None, None]
        bi_global = 0
        for ui, (e, h) in enumerate(units):
            wbuf = ui % 2
            nb = (segs[e] + 511) // 512

            @blk.gpsimd
            def _(g, e=e, h=h, wbuf=wbuf, m1g=wh_guard[wbuf]):
                W(g, "m1", m1g)
                ap = bass.AP(whb[wbuf], 0, [[DCH * 2048, 128], [2048, DCH], [1, 1024]])
                p.dma(g, "lw", ap,
                      _featmaj_ap(W1t[e, :, :], D, 4096)[:, :, h * 1024:(h + 1) * 1024])
                ap = bass.AP(whb[wbuf], 1024, [[DCH * 2048, 128], [2048, DCH], [1, 1024]])
                p.dma(g, "lw", ap,
                      _featmaj_ap(W1t[e, :, :], D, 4096)[:, :, 2048 + h * 1024:2048 + (h + 1) * 1024])
                p.dma(g, "lg", b1h[wbuf][:, 0:1024], b1[e:e + 1, h * 1024:(h + 1) * 1024])
                p.dma(g, "lg", b1h[wbuf][:, 1024:2048],
                      b1[e:e + 1, 2048 + h * 1024:2048 + (h + 1) * 1024])
            lw0_now = p.c["lw0"]; lw1_now = p.c["lw1"]

            for bi in range(nb):
                bbuf = bi_global % 2
                bi_global += 1
                col0 = bases[e] + bi * 512
                ncols = min(512, segs[e] - bi * 512)
                m1b = p.c["m1"]; sgb_ = p.c["sg"]; dab = p.c["da"]

                @blk.gpsimd
                def _(g, col0=col0, ncols=ncols, bbuf=bbuf, xgg=xg_guard[bbuf]):
                    W(g, "mg", xgg)
                    p.dma(g, "lt",
                          bass.AP(xgb[bbuf], 0, [[DCH * 512, 128], [512, DCH], [1, ncols]]),
                          _featmaj_ap(xgi, D, nsl)[:, :, col0:col0 + ncols])
                    p.dma(g, "lg", gsr[bbuf][:, 0:ncols], gsl[:, col0:col0 + ncols])
                lt0_now = p.c["lt0"]; lt1_now = p.c["lt1"]
                lg0_now = p.c["lg0"]; lg1_now = p.c["lg1"]

                @blk.tensor
                def _(t, e=e, h=h, wbuf=wbuf, bbuf=bbuf, ncols=ncols, ui=ui, bi=bi,
                      m1b=m1b, dab=dab, sgb_=sgb_, lw0_now=lw0_now, lw1_now=lw1_now,
                      lt0_now=lt0_now, lt1_now=lt1_now, lg0_now=lg0_now,
                      lg1_now=lg1_now):
                    W(t, "lw0", lw0_now)
                    W(t, "lw1", lw1_now)
                    W(t, "lt0", lt0_now)
                    W(t, "lt1", lt1_now)
                    W(t, "lg0", lg0_now)
                    W(t, "lg1", lg1_now)
                    if ui == 0 and bi == 0:
                        t.wait_ge(p.sems["vm"], 1)
                    W(t, "da", dab)
                    ins = t.matmul(bankE[:, 0:ncols], onesr[:, 0:128],
                                   gsr[bbuf][:, 0:ncols], start=True, stop=True)
                    p.inc(ins, "m1", 1)
                    for hc in range(8):
                        pA = bankA if hc % 2 == 0 else bankB
                        pB = bankC if hc % 2 == 0 else bankD
                        W(t, "da", dab + 1 + hc - 1)
                        W(t, "sg", sgb_ + hc - 1)
                        for dc in range(DCH):
                            t.matmul(pA[:, 0:ncols],
                                     whb[wbuf][:, dc * 2048 + hc * 128:dc * 2048 + (hc + 1) * 128],
                                     xgb[bbuf][:, dc * 512:dc * 512 + ncols],
                                     start=(dc == 0), stop=False, skip_group_check=True)
                        t.matmul(pA[:, 0:ncols], b1h[wbuf][:, hc * 128:(hc + 1) * 128],
                                 onesr[:, 0:ncols], start=False, stop=True,
                                 skip_group_check=True)
                        for dc in range(DCH):
                            t.matmul(pB[:, 0:ncols],
                                     whb[wbuf][:, dc * 2048 + 1024 + hc * 128:dc * 2048 + 1024 + (hc + 1) * 128],
                                     xgb[bbuf][:, dc * 512:dc * 512 + ncols],
                                     start=(dc == 0), stop=False, skip_group_check=True)
                        ins = t.matmul(pB[:, 0:ncols],
                                       b1h[wbuf][:, 1024 + hc * 128:1024 + (hc + 1) * 128],
                                       onesr[:, 0:ncols], start=False, stop=True,
                                       skip_group_check=True)
                        p.inc(ins, "m1", 1)

                @blk.scalar
                def _(s, ncols=ncols, m1b=m1b, dab=dab, sgb_=sgb_):
                    for hc in range(8):
                        s.wait_ge(p.sems["m1"], m1b + hc + 2)
                        W(s, "da", dab + 1 + hc)
                        pB = bankC if hc % 2 == 0 else bankD
                        ins = s.activation(sgE[:, 0:ncols], pB[:, 0:ncols], AF.Sigmoid)
                        p.inc(ins, "sg", 1)

                @blk.vector
                def _(v, e=e, h=h, bbuf=bbuf, ncols=ncols, m1b=m1b, dab=dab,
                      sgb_=sgb_, tok=hg_tok[bbuf]):
                    v.wait_ge(p.sems["m1"], m1b + 1)
                    if tok is not None:
                        v.wait_ge(p.sems[tok[0]], tok[1])
                    ins = v.tensor_copy(gateS[:, 0:ncols], bankE[:, 0:ncols])
                    p.inc(ins, "da", 1)
                    v.drain()
                    for hc in range(8):
                        v.wait_ge(p.sems["m1"], m1b + hc + 2)
                        v.wait_ge(p.sems["sg"], sgb_ + hc + 1)
                        pA = bankA if hc % 2 == 0 else bankB
                        v.tensor_mul(t1f[:, 0:ncols], pA[:, 0:ncols], sgE[:, 0:ncols])
                        v.drain()
                        v.tensor_mul(t1f[:, 0:ncols], t1f[:, 0:ncols], gateS[:, 0:ncols])
                        v.drain()
                        ins = v.tensor_copy(hgb[bbuf][:, hc * 512:hc * 512 + ncols],
                                            t1f[:, 0:ncols])
                        p.inc(ins, "da", 1)
                        v.drain()
                    if h == 1:
                        v.memset(hgb[bbuf][:, 8 * 512:8 * 512 + ncols], 0.0)
                        v.drain()
                        v.tensor_copy(hgb[bbuf][0:1, 8 * 512:8 * 512 + ncols],
                                      gsr[bbuf][:, 0:ncols])
                        v.drain()
                    p.inc(v.engine_nop(), "mg", 1)
                mg_now = p.c["mg"]

                @blk.sync
                def _(sy, e=e, h=h, bbuf=bbuf, col0=col0, ncols=ncols, mg_now=mg_now):
                    sy.wait_ge(p.sems["mg"], mg_now)
                    nch = 9 if h == 1 else 8
                    tok = p.dma(sy, "so",
                                hgT.rearrange("(c p) s -> p c s", p=128)[:, h * 8:h * 8 + nch, col0:col0 + ncols],
                                bass.AP(hgb[bbuf], 0, [[9 * 512, 128], [512, nch], [1, ncols]]))
                    hg_tok[bbuf] = tok
                xg_guard[bbuf] = p.c["mg"]
            wh_guard[wbuf] = p.c["m1"]

        @blk.gpsimd
        def _(g):
            p.gwait(g, "so")
    return nc


def build_s_l0comb():
    p = P("s_l0comb")
    nc = p.nc
    KC = 17
    x1T = p.inp("x1T", [D, T])
    h1T = p.inp("h1T", [KC * 128, T], BF16)
    h2T = p.inp("h2T", [KC * 128, T], BF16)
    WotP = p.inp("WotP", [KC * 128, D])
    x2T = p.out("x2T", [D, T])
    with p.es:
        wotb = p.sb("wotb", [128, KC * D], BF16)
        h1c = p.sb("h1c", [128, KC * 512], BF16)
        h2c = p.sb("h2c", [128, KC * 512], BF16)
        gsb = p.sb("gsb", [128, KC * 512], BF16)
        xf = p.sb("xf", [128, DCH * 512], F32)
        ts = p.sb("ts", [128, DCH * 512], F32)
        tb = p.sb("tb", [128, 512], BF16)
        sqb = p.sb("sqb", [128, 512], BF16)
        tmpf = p.sb("tmpf", [128, 512], F32)
        x1c = [p.sb("x1c%d" % i, [128, 512], F32) for i in range(2)]
        _r = {n: p.sb(n, [1, 512], F32) for n in ("mrow", "m2row", "vrow")}
        rows = (_r["mrow"], _r["m2row"], _r["vrow"], _r["vrow"], _r["m2row"])
        ones128c = p.sb("ones128c", [128, 4], BF16)
        ones1x128f = p.sb("ones1x128f", [1, 128], F32)
        bankA = p.ps("bankA", [128, 512])
        bankB = p.ps("bankB", [128, 512])
        bankE = p.ps("bankE", [128, 512])
        bankF = p.ps("bankF", [128, 512])
        for s_ in ("vm", "dg", "mn", "mp", "dva", "xr"):
            p.sem(s_)
        p.dma_group("ld", 2)
        p.dma_group("lx", 2)
        p.dma_group("so", 2)
        blk = p.block()

        def W(eng, name, val):
            if val > 0:
                eng.wait_ge(p.sems[name], val)

        @blk.gpsimd
        def _(g):
            p.dma(g, "ld", _chunked_sbuf_ap(wotb, KC, D), _featmaj_ap(WotP, KC * 128, D))

        @blk.vector
        def _(v):
            v.memset(ones128c[:, :], 1.0)
            v.memset(ones1x128f[:, :], 1.0)
            p.inc(v.engine_nop(), "vm", 1)

        pA = [bankA[:, :], bankB[:, :]]
        for tt in range(NTT):
            @blk.sync
            def _(sy, tt=tt):
                if tt > 0:
                    sy.wait_ge(p.sems["dva"], tt * 34)
                    sy.wait_ge(p.sems["dg"], tt * KC)
                p.dma(sy, "lx", _chunked_sbuf_ap(xf, DCH, 512),
                      x1T.rearrange("(c p) t -> p c t", p=128)[:, :, tt * 512:(tt + 1) * 512])
                p.dma(sy, "lx", _chunked_sbuf_ap(h1c, KC, 512),
                      h1T.rearrange("(c p) t -> p c t", p=128)[:, :, tt * 512:(tt + 1) * 512])
                p.dma(sy, "lx", _chunked_sbuf_ap(h2c, KC, 512),
                      h2T.rearrange("(c p) t -> p c t", p=128)[:, :, tt * 512:(tt + 1) * 512])
            lx0_now = p.c["lx0"]; lx1_now = p.c["lx1"]
            mn_prev = p.c["mn"]
            dgb = p.c["dg"]

            @blk.vector
            def _(v, tt=tt, lx0_now=lx0_now, lx1_now=lx1_now, mn_prev=mn_prev):
                v.wait_ge(p.sems["vm"], 1)
                v.wait_ge(p.sems["lx0"], lx0_now)
                v.wait_ge(p.sems["lx1"], lx1_now)
                W(v, "mn", mn_prev)
                for kc in range(KC):
                    sl = slice(kc * 512, (kc + 1) * 512)
                    ins = v.tensor_add(gsb[:, sl], h1c[:, sl], h2c[:, sl])
                    p.inc(ins, "dg", 1)
                    v.drain()

            mpb = p.c["mp"]
            dvab = p.c["dva"]

            @blk.tensor
            def _(t, tt=tt, dgb=dgb, mpb=mpb, dvab=dvab):
                if tt == 0:
                    p.gwait(t, "ld")
                t.wait_ge(p.sems["dg"], dgb + KC)

                def stats(c):
                    t.wait_ge(p.sems["dva"], dvab + 3 * c + 3)
                    t.matmul(bankE[0:4, :], ones128c[:, :], tb[:, :],
                             start=(c == 0), stop=(c == DCH - 1), skip_group_check=True)
                    ins = t.matmul(bankF[0:4, :], ones128c[:, :], sqb[:, :],
                                   start=(c == 0), stop=(c == DCH - 1), skip_group_check=True)
                    p.inc(ins, "mp", 1)

                for oc in range(DCH):
                    if oc >= 2:
                        t.wait_ge(p.sems["dva"], dvab + 3 * (oc - 2) + 1)
                    for kc in range(KC):
                        ins = t.matmul(pA[oc % 2], wotb[:, kc * D + oc * 128:kc * D + (oc + 1) * 128],
                                       gsb[:, kc * 512:(kc + 1) * 512],
                                       start=(kc == 0), stop=(kc == KC - 1),
                                       skip_group_check=True)
                    p.inc(ins, "mn", 1)
                    if oc >= 2:
                        stats(oc - 2)
                stats(DCH - 2)
                stats(DCH - 1)
            mn_prj = p.c["mn"]

            @blk.vector
            def _(v, tt=tt, mpb=mpb, mn_prj=mn_prj):
                for oc in range(DCH):
                    v.wait_ge(p.sems["mn"], mn_prj - DCH + oc + 1)
                    W(v, "mp", mpb + oc)
                    ins = v.tensor_add(ts[:, oc * 512:(oc + 1) * 512],
                                       xf[:, oc * 512:(oc + 1) * 512], pA[oc % 2])
                    p.inc(ins, "dva", 1)
                    v.drain()
                    ins = v.tensor_copy(tb[:, :], ts[:, oc * 512:(oc + 1) * 512])
                    p.inc(ins, "dva", 1)
                    v.drain()
                    ins = v.tensor_mul(sqb[:, :], tb[:, :], tb[:, :])
                    p.inc(ins, "dva", 1)

            emit_ln_tail2(p, blk, dvab, mpb, tt, ts, tb, sqb, tmpf, x1c, rows,
                          ones128c, ones1x128f, bankE, bankF, x2T)

        @blk.gpsimd
        def _(g):
            p.gwait(g, "so")
    return nc


def _run(nc, in_maps):
    return run_bass_kernel_spmd(nc, in_maps, core_ids=list(range(NCORES))).results


def kernel(**inputs):
    f32 = np.float32
    g = inputs
    x = np.ascontiguousarray(g["L0_attn_h_t"].reshape(2 * S, D))
    xT = [np.ascontiguousarray(x[T * c:T * (c + 1)].T) for c in range(NCORES)]

    Wck = np.ascontiguousarray(np.concatenate([g["L0_attn_W_DKV"], g["L0_attn_W_KR"]], axis=1))
    bck = np.ascontiguousarray(np.concatenate([g["L0_attn_b_DKV"], g["L0_attn_b_KR"]]))
    r1 = _run(build_s_ck(), [{"xT": xT[c], "Wck": Wck, "bck": bck} for c in range(NCORES)])
    ckd_b = [np.concatenate([r1[4 * b + i]["ck_dup"] for i in range(4)], axis=0)
             for b in range(2)]
    vh_b = [np.concatenate([r1[4 * b + i]["vhat"] for i in range(4)], axis=1)
            for b in range(2)]

    W_UKt = np.ascontiguousarray(g["L0_attn_W_UK"].T)
    W_UVt = np.ascontiguousarray(g["L0_attn_W_UV"].T)
    r2 = _run(build_s_l0attn(), [{
        "xT": xT[c],
        "ckdup": np.ascontiguousarray(ckd_b[c // 4]),
        "vhat": np.ascontiguousarray(vh_b[c // 4]),
        "W_DQ": g["L0_attn_W_DQ"], "b_DQ": g["L0_attn_b_DQ"],
        "W_UQ": g["L0_attn_W_UQ"], "b_UQ": g["L0_attn_b_UQ"],
        "W_UKt": W_UKt,
        "W_QR": g["L0_attn_W_QR"], "b_QR": g["L0_attn_b_QR"],
        "W_UVt": W_UVt,
        "W_O": g["L0_attn_W_O"], "b_O": g["L0_attn_b_O"],
        "b_UV": g["L0_attn_b_UV"], "b_UK": g["L0_attn_b_UK"],
    } for c in range(NCORES)])

    Wgt = np.ascontiguousarray(g["L0_ffn_Wg"].T)
    W1t = np.ascontiguousarray(g["L0_ffn_W1"].transpose(0, 2, 1))
    WotP = np.ascontiguousarray(np.concatenate(
        [g["L0_ffn_Wo"].T, g["L0_ffn_bo"][None, :], np.zeros((127, D), f32)], axis=0))
    r3r = _run(build_s_l0rout(), [{
        "x1T": r2[c]["x1T"], "Wgt": Wgt,
    } for c in range(NCORES)])
    counts0 = np.ascontiguousarray(
        np.concatenate([r3r[c]["counts"] for c in range(NCORES)], axis=0))
    tot0 = counts0.sum(axis=0)
    segs0 = [max(128, int(128 * math.ceil(t / 8.0 / 128.0))) for t in tot0]
    nsl0 = sum(segs0)
    r3b_in = []
    for c in range(NCORES):
        sel = np.zeros((NCORES, 1), f32)
        sel[c, 0] = 1.0
        r3b_in.append({
            "m1i": r3r[c]["m1o"], "m2i": r3r[c]["m2o"],
            "counts_all": counts0, "core_sel": sel,
        })
    r3b = _run(build_s_l0slot(segs0), r3b_in)
    dst0 = {}
    for c in range(NCORES):
        dst0[(c, 1)] = r3b[c]["dst1o"].T.reshape(T).astype(np.int64)
        dst0[(c, 2)] = r3b[c]["dst2o"].T.reshape(T).astype(np.int64)
    x1T_all = np.concatenate([r2[c]["x1T"] for c in range(NCORES)], axis=1)
    perm0 = np.full(NCORES * nsl0, -1, np.int64)
    gv0 = np.zeros(NCORES * nsl0, f32)
    for c in range(NCORES):
        for k, w in ((1, r3r[c]["w1o"]), (2, r3r[c]["w2o"])):
            d = dst0[(c, k)]
            m = d >= 0
            toks = c * T + np.arange(T)
            perm0[d[m]] = toks[m]
            gv0[d[m]] = w.T.reshape(T)[m]
    r3c_in = []
    for c in range(NCORES):
        pq = perm0[c * nsl0:(c + 1) * nsl0]
        xg = np.zeros((D, nsl0), f32)
        mq = pq >= 0
        xg[:, mq] = x1T_all[:, pq[mq]]
        r3c_in.append({
            "xgi": xg,
            "gsl": np.ascontiguousarray(gv0[c * nsl0:(c + 1) * nsl0][None, :]),
            "W1t": W1t, "b1": g["L0_ffn_b1"],
        })
    r3c = _run(build_s_l0ffn_r(segs0), r3c_in)
    hgT_all = np.concatenate([r3c[c]["hgT"] for c in range(NCORES)], axis=1)
    r3d_in = []
    for c in range(NCORES):
        h1 = np.zeros((17 * 128, T), hgT_all.dtype)
        h2 = np.zeros((17 * 128, T), hgT_all.dtype)
        d1 = dst0[(c, 1)]; m1_ = d1 >= 0; h1[:, m1_] = hgT_all[:, d1[m1_]]
        d2 = dst0[(c, 2)]; m2_ = d2 >= 0; h2[:, m2_] = hgT_all[:, d2[m2_]]
        r3d_in.append({
            "x1T": r2[c]["x1T"], "h1T": h1, "h2T": h2, "WotP": WotP,
        })
    r3 = _run(build_s_l0comb(), r3d_in)

    r4 = _run(build_s_l1qkv(), [{
        "x2T": r3[c]["x2T"], "W_qkv": g["L1_attn_W_qkv"],
    } for c in range(NCORES)])

    kn_t = [r4[c]["kno"].reshape(128, 16, 16).transpose(1, 0, 2).reshape(T, 16)
            for c in range(NCORES)]
    knT_b = [np.ascontiguousarray(np.concatenate(kn_t[4 * b:4 * b + 4], axis=0).T)
             for b in range(2)]
    qkv_b, vsw_b = [], []
    for b in range(2):
        qb = np.concatenate([r4[4 * b + i]["qkv"] for i in range(4)], axis=0)
        qkv_b.append(np.ascontiguousarray(qb.T))
        v130 = np.ones((8, S, 130), qb.dtype)
        qh2 = qb.reshape(S, 8, 2, 64).transpose(1, 0, 2, 3)
        v130[:, :, 0:64] = qh2[:, :, 0, :]
        v130[:, :, 65:129] = qh2[:, :, 1, :]
        vsw_b.append(np.ascontiguousarray(
            v130.reshape(8, 64, 128, 130).transpose(0, 2, 1, 3)).reshape(8 * 128, 64 * 130))
    r5_in = []
    for c in range(NCORES):
        b = c // 4
        qo = r4[c]["qkv"]
        r5_in.append({
            "x2T": r3[c]["x2T"],
            "khat_in": qkv_b[b],
            "qhpT_in": np.ascontiguousarray(qo.T),
            "vsw_in": vsw_b[b],
            "knTo": np.ascontiguousarray(kn_t[c].T),
            "knTb": knT_b[b],
            "W_out": g["L1_attn_W_out"], "b_out": g["L1_attn_b_out"],
        })
    r5 = _run(build_s_l1attn(), r5_in)

    r6 = _run(build_s_l1rout(), [{
        "x3T": r5[c]["x3T"], "router_w": g["L1_ffn_router_w"],
        "router_b": g["L1_ffn_router_b"],
    } for c in range(NCORES)])
    counts_all = np.ascontiguousarray(
        np.concatenate([r6[c]["counts"] for c in range(NCORES)], axis=0))

    r6b_in = []
    for c in range(NCORES):
        sel = np.zeros((NCORES, 1), f32)
        sel[c, 0] = 1.0
        r6b_in.append({
            "m1i": r6[c]["m1o"], "m2i": r6[c]["m2o"],
            "counts_all": counts_all, "core_sel": sel,
        })
    r6b = _run(build_s_l1slot(), r6b_in)
    NSL = L1NSL
    dstt = {}
    for c in range(NCORES):
        dstt[(c, 1)] = r6b[c]["dst1o"].T.reshape(T).astype(np.int64)
        dstt[(c, 2)] = r6b[c]["dst2o"].T.reshape(T).astype(np.int64)

    x3T_all = np.concatenate([r5[c]["x3T"] for c in range(NCORES)], axis=1)
    perm = np.full(NCORES * NSL, -1, np.int64)
    gv = np.zeros(NCORES * NSL, f32)
    for c in range(NCORES):
        for k, w in ((1, r6[c]["w1o"]), (2, r6[c]["w2o"])):
            d = dstt[(c, k)]
            m = d >= 0
            toks = c * T + np.arange(T)
            perm[d[m]] = toks[m]
            gv[d[m]] = w.T.reshape(T)[m]
    r7a_in = []
    for c in range(NCORES):
        pq = perm[c * NSL:(c + 1) * NSL]
        xg = np.zeros((D, NSL), f32)
        mq = pq >= 0
        xg[:, mq] = x3T_all[:, pq[mq]]
        r7a_in.append({
            "x3T": r5[c]["x3T"], "xgi": xg,
            "gsl": np.ascontiguousarray(gv[c * NSL:(c + 1) * NSL][None, :]),
            "w_gate": g["L1_ffn_w_gate"], "w_up": g["L1_ffn_w_up"],
            "w_down": g["L1_ffn_w_down"],
            "sw_gate": g["L1_ffn_sw_gate"], "sw_up": g["L1_ffn_sw_up"],
            "sw_down": g["L1_ffn_sw_down"],
        })
    r7a = _run(build_s_l1ffn_r(), r7a_in)

    ygT_all = np.concatenate([r7a[c]["ygT"] for c in range(NCORES)], axis=1)
    r7b_in = []
    for c in range(NCORES):
        y1 = np.zeros((D, T), ygT_all.dtype)
        y2 = np.zeros((D, T), ygT_all.dtype)
        d1 = dstt[(c, 1)]; m1_ = d1 >= 0; y1[:, m1_] = ygT_all[:, d1[m1_]]
        d2 = dstt[(c, 2)]; m2_ = d2 >= 0; y2[:, m2_] = ygT_all[:, d2[m2_]]
        r7b_in.append({
            "x3T": r5[c]["x3T"], "yshT": r7a[c]["yshT"],
            "y1T": y1, "y2T": y2,
        })
    r7b = _run(build_s_l1comb(), r7b_in)

    y = np.concatenate([r7b[c]["yT"].T for c in range(NCORES)], axis=0)
    return np.ascontiguousarray(y.reshape(2, S, D)).astype(np.float32)


# version 12
# speedup vs baseline: 2.6632x; 1.0977x over previous
import math
from contextlib import ExitStack

import numpy as np
import ml_dtypes

import concourse.bass as bass
import concourse.mybir as mybir
from concourse.bass_utils import run_bass_kernel_spmd

F32 = mybir.dt.float32
BF16 = mybir.dt.bfloat16
AF = mybir.ActivationFunctionType
ALU = mybir.AluOpType
AX = mybir.AxisListType.X
BF = ml_dtypes.bfloat16

NCORES = 8
T = 2048
S = 8192
D = 1024
DCH = D // 128
NTT = T // 512
SCALE0 = 1.0 / math.sqrt(160.0)
SCALE1 = 64.0 ** -0.5
LN_EPS = 1e-5


class P:

    def __init__(self, name):
        self.nc = bass.Bass("TRN2", target_bir_lowering=False)
        self.es = ExitStack()
        self.name = name
        self.c = {}
        self.sems = {}
        self.groups = {}

    def inp(self, name, shape, dt=F32):
        return self.nc.dram_tensor(name, shape, dt, kind="ExternalInput").ap()

    def out(self, name, shape, dt=F32):
        return self.nc.dram_tensor(name, shape, dt, kind="ExternalOutput").ap()

    def sb(self, name, shape, dt):
        return self.es.enter_context(self.nc.sbuf_tensor(name, shape, dt))

    def ps(self, name, shape, dt=F32):
        return self.es.enter_context(self.nc.psum_tensor(name, shape, dt))

    def sem(self, name):
        h = self.es.enter_context(self.nc.semaphore(name))
        self.sems[name] = h
        self.c[name] = 0
        return h

    def inc(self, instr, name, k):
        instr.then_inc(self.sems[name], k)
        self.c[name] += k
        return self.c[name]

    def block(self):
        return self.es.enter_context(self.nc.Block())

    def dma_group(self, name, k):
        names = ["%s%d" % (name, i) for i in range(k)]
        for n in names:
            self.sem(n)
        self.groups[name] = {"names": names, "i": 0}

    def dma(self, eng, name, out, in_, **kw):
        g = self.groups[name]
        sn = g["names"][g["i"] % len(g["names"])]
        if self.c[sn] > 0:
            eng.wait_ge(self.sems[sn], self.c[sn])
        ins = eng.dma_start(out=out, in_=in_, **kw)
        self.inc(ins, sn, 16)
        g["i"] += 1
        return (sn, self.c[sn])

    def gwait(self, eng, name):
        for sn in self.groups[name]["names"]:
            if self.c[sn] > 0:
                eng.wait_ge(self.sems[sn], self.c[sn])


def _featmaj_ap(dram_ap, d_in, cols):
    return dram_ap.rearrange("(c p) t -> p c t", p=128)


def _chunked_sbuf_ap(t, nchunk, cols):
    return bass.AP(t, 0, [[nchunk * cols, 128], [cols, nchunk], [1, cols]])


def build_s_ck():
    p = P("s_ck")
    nc = p.nc
    xT = p.inp("xT", [D, T])
    Wck = p.inp("Wck", [D, 64])
    bck = p.inp("bck", [64])
    ck_dup = p.out("ck_dup", [T, 128], BF16)
    vhat = p.out("vhat", [128, (T // 128) * 33], BF16)
    NT = T // 128
    with p.es:
        xTb = p.sb("xTb", [128, DCH * T], BF16)
        wckb = p.sb("wckb", [128, DCH * 64], BF16)
        bckb = p.sb("bckb", [1, 64], BF16)
        onesr = p.sb("onesr", [1, 128], BF16)
        ckd = p.sb("ckd", [128, NT * 128], BF16)
        vh = p.sb("vh", [128, NT * 33], BF16)
        pck = [p.ps("pck%d" % i, [128, 64]) for i in range(2)]
        ld = p.sem("ld")
        vm = p.sem("vm")
        mm = p.sem("mm")
        ev = p.sem("ev")
        so = p.sem("so")
        blk = p.block()

        @blk.gpsimd
        def _(g):
            g.dma_start(out=_chunked_sbuf_ap(xTb, DCH, T), in_=_featmaj_ap(xT, D, T)).then_inc(p.sems["ld"], 16)
            g.dma_start(out=_chunked_sbuf_ap(wckb, DCH, 64), in_=_featmaj_ap(Wck, D, 64)).then_inc(p.sems["ld"], 16)
            g.dma_start(out=bckb[:, :], in_=bck[None, :]).then_inc(p.sems["ld"], 16)
        p.c["ld"] = 48

        @blk.vector
        def _(v):
            v.memset(onesr[:, :], 1.0)
            v.memset(bass.AP(vh, 32, [[vh.shape[1], 128], [33, NT], [1, 1]]), 1.0)
            v.memset(bass.AP(ckd, 64, [[ckd.shape[1], 128], [128, NT], [1, 1]]), 1.0)
            v.memset(bass.AP(ckd, 65, [[ckd.shape[1], 128], [128, NT], [1, 63]]), 0.0)
            p.inc(v.engine_nop(), "vm", 1)

        @blk.tensor
        def _(t):
            t.wait_ge(p.sems["ld"], 48)
            t.wait_ge(p.sems["vm"], 1)
            for tk in range(NT):
                if tk >= 2:
                    t.wait_ge(p.sems["ev"], 2 * (tk - 1))
                for c in range(DCH):
                    t.matmul(pck[tk % 2][:, :], xTb[:, c * T + tk * 128: c * T + (tk + 1) * 128],
                             wckb[:, c * 64:(c + 1) * 64], start=(c == 0), stop=False)
                ins = t.matmul(pck[tk % 2][:, :], onesr[:, :], bckb[:, :], start=False, stop=True)
                p.inc(ins, "mm", 1)

        @blk.scalar
        def _(s):
            for tk in range(NT):
                s.wait_ge(p.sems["mm"], tk + 1)
                src = pck[tk % 2]
                s.copy(ckd[:, tk * 128: tk * 128 + 64], src[:, :])
                ins = s.copy(vh[:, tk * 33: tk * 33 + 32], src[:, 0:32])
                p.inc(ins, "ev", 2)

        @blk.sync
        def _(sy):
            sy.wait_ge(p.sems["ev"], 2 * NT)
            sy.wait_ge(p.sems["vm"], 1)
            sy.dma_start(out=ck_dup.rearrange("(k p) c -> p k c", p=128), in_=_chunked_sbuf_ap(ckd, T // 128, 128)).then_inc(p.sems["so"], 16)
            sy.dma_start(out=vhat[:, :], in_=vh[:, :]).then_inc(p.sems["so"], 16)
            sy.wait_ge(p.sems["so"], 32)
    return nc


def build_s_l0attn(upto=99, skip_mid=False):
    p = P("s_l0attn")
    nc = p.nc
    sk = int(skip_mid) if skip_mid else 0
    NKB = S // 256
    H = 8
    xT = p.inp("xT", [D, T])
    ckdup = p.inp("ckdup", [S, 128], BF16)
    vhat = p.inp("vhat", [128, (S // 128) * 33], BF16)
    W_DQ = p.inp("W_DQ", [D, 32]); b_DQ = p.inp("b_DQ", [32])
    W_UQ = p.inp("W_UQ", [32, D]); b_UQ = p.inp("b_UQ", [D])
    W_UKt = p.inp("W_UKt", [D, 32])
    W_QR = p.inp("W_QR", [32, 256]); b_QR = p.inp("b_QR", [256])
    W_UVt = p.inp("W_UVt", [D, 32])
    W_O = p.inp("W_O", [D, D]); b_O = p.inp("b_O", [D]); b_UV = p.inp("b_UV", [D])
    b_UK = p.inp("b_UK", [D])
    x1T = p.out("x1T", [D, T])
    with p.es:
        xq = p.sb("xq", [128, DCH * 512], BF16)
        khat = p.sb("khat", [128, S], BF16)
        vh = p.sb("vh", [128, (S // 128) * 33], BF16)
        wdqb = p.sb("wdqb", [128, DCH * 32], BF16)
        wuqb = p.sb("wuqb", [32, D], BF16)
        wuktb = p.sb("wuktb", [128, DCH * 32], BF16)
        wqrb = p.sb("wqrb", [32, 256], BF16)
        wuvtb = p.sb("wuvtb", [128, DCH * 32], BF16)
        wob = p.sb("wob", [128, DCH * D], BF16)
        bdq = p.sb("bdq", [32, 1], F32)
        buq = p.sb("buq", [128, DCH], F32)
        qbias = p.sb("qbias", [65, H], F32)
        bukb = p.sb("bukb", [128, 4 * DCH], BF16)
        buvb = p.sb("buvb", [128, 4 * DCH], BF16)
        bo_row = p.sb("bo_row", [1, D], F32)
        onesr128 = p.sb("onesr128", [1, 128], BF16)
        ones1x128f = p.sb("ones1x128f", [1, 128], F32)
        ones64c = p.sb("ones64c", [65, 4], BF16)
        ones128c = p.sb("ones128c", [128, 4], BF16)
        identf2 = p.sb("identf2", [128, 128], F32)
        onesq = p.sb("onesq", [1, 512], BF16)
        cqT = p.sb("cqT", [32, T], BF16)
        qc1 = [p.sb("qc1%d" % i, [128, T], BF16) for i in range(2)]
        qh = p.sb("qh", [128, H * T], BF16)
        woe = p.sb("woe", [32, H * D], BF16)
        beffb = p.sb("beffb", [1, D], BF16)
        beff_f = p.sb("beff_f", [1, D], F32)
        sqt = p.sb("sqt", [65, 512], BF16)
        kn2m = p.sb("kn2m", [1, 16], F32)
        qm2 = p.sb("qm2", [1, 32], F32)
        kmax = p.sb("kmax", [1, 1], F32)
        negu = p.sb("negu", [1, 32], F32)
        negub = p.sb("negub", [1, 32], BF16)
        Ubc = p.sb("Ubc", [128, 32], F32)
        et = [p.sb("et%d" % i, [128, 1024], BF16) for i in range(3)]
        rq = p.sb("rq", [128, 1], F32)
        otmp = p.sb("otmp", [128, 128], F32)
        onrm = p.sb("onrm", [32, H * 512], BF16)
        xf = p.sb("xf", [128, DCH * 512], F32)
        ts = p.sb("ts", [128, DCH * 512], F32)
        tb = p.sb("tb", [128, DCH * 512], BF16)
        sqb = p.sb("sqb", [128, DCH * 512], BF16)
        tmpf = p.sb("tmpf", [128, 512], F32)
        x1c = [p.sb("x1c%d" % i, [128, 512], F32) for i in range(2)]
        mrow = p.sb("mrow", [1, 512], F32)
        m2row = p.sb("m2row", [1, 512], F32)
        vrow = p.sb("vrow", [1, 512], F32)
        rrow = p.sb("rrow", [1, 512], F32)
        brow = p.sb("brow", [1, 512], F32)
        bankA = p.ps("bankA", [128, 1024])
        bankB = p.ps("bankB", [128, 1024])
        bankC = p.ps("bankC", [128, 512])
        bankD = p.ps("bankD", [128, 512])
        bankE = p.ps("bankE", [128, 512])
        bankF = p.ps("bankF", [128, 512])
        for s_ in ("vm", "mm", "xp", "dq", "ms", "xe",
                   "mv", "xo", "d6", "mb", "mp", "dva", "xr", "mn"):
            p.sem(s_)
        p.dma_group("ld", 4)
        p.dma_group("lk", 1)
        p.dma_group("lq", 1)
        p.dma_group("lx", 1)
        p.dma_group("so", 2)
        blk = p.block()

        def W(eng, name, val):
            if val > 0:
                eng.wait_ge(p.sems[name], val)

        @blk.gpsimd
        def _(g):
            for out_ap, in_ap, slow in [
                (vh[:, :], vhat[:, :], False),
                (_chunked_sbuf_ap(wdqb, DCH, 32), _featmaj_ap(W_DQ, D, 32), False),
                (wuqb[:, :], W_UQ[:, :], False),
                (_chunked_sbuf_ap(wuktb, DCH, 32), _featmaj_ap(W_UKt, D, 32), False),
                (wqrb[:, :], W_QR[:, :], False),
                (_chunked_sbuf_ap(wuvtb, DCH, 32), _featmaj_ap(W_UVt, D, 32), False),
                (_chunked_sbuf_ap(wob, DCH, D), _featmaj_ap(W_O, D, D), False),
                (bdq[:, :], b_DQ[:, None], False),
                (buq[:, :], b_UQ.rearrange("(c p) -> p c", p=128), True),
                (qbias[32:64, :], b_QR.rearrange("(h r) -> r h", r=32), True),
                (bo_row[:, :], b_O[None, :], False),
            ]:
                p.dma(g, "ld", out_ap, in_ap, allow_slow_non_contiguous=slow)

        @blk.gpsimd
        def _(g):
            g.wait_ge(p.sems["vm"], 1)
            g.affine_select(identf2[:, :], identf2[:, :], [[-1, 128]],
                            ALU.is_equal, 0.0, channel_multiplier=1)
            p.dma(g, "ld", bass.AP(buvb, 0, [[4 * DCH, 128], [4, DCH], [1, 1]]),
                  b_UV.rearrange("(c p) -> p c", p=128)[:, :, None],
                  allow_slow_non_contiguous=True)
            p.dma(g, "ld", bass.AP(bukb, 0, [[4 * DCH, 128], [4, DCH], [1, 1]]),
                  b_UK.rearrange("(c p) -> p c", p=128)[:, :, None],
                  allow_slow_non_contiguous=True)

        @blk.sync
        def _(sy):
            p.gwait(sy, "ld")
            if not (sk & 1):
                sy.wait_ge(p.sems["lq0"], 16 * NTT)
            p.dma(sy, "lk", khat[:, :], ckdup[:, :], transpose=True)

        @blk.vector
        def _(v):
            for t_, val in [(onesr128, 1.0), (ones1x128f, 1.0), (ones64c, 1.0),
                            (ones128c, 1.0), (identf2, 1.0), (onesq, 1.0)]:
                v.memset(t_[:, :], val)
            v.memset(qbias[0:32, :], 0.0)
            v.memset(qbias[64:65, :], 0.0)
            v.memset(qh[64:128, :], 0.0)
            v.memset(buvb[:, :], 0.0)
            v.memset(bukb[:, :], 0.0)
            p.inc(v.engine_nop(), "vm", 1)

        NTT_P1 = NTT if (upto >= 2 and not (sk & 1)) else 0
        pc32 = [bankC[0:32, :], bankD[0:32, :]]

        @blk.gpsimd
        def _(g):
            for tt in range(NTT_P1):
                if tt > 0:
                    g.wait_ge(p.sems["mm"], tt)
                p.dma(g, "lq",
                      _chunked_sbuf_ap(xq, DCH, 512),
                      xT.rearrange("(c p) t -> p c t", p=128)[:, :, tt * 512:(tt + 1) * 512])

        @blk.tensor
        def _(t):
            p.gwait(t, "ld")
            t.wait_ge(p.sems["vm"], 1)
            for tt in range(NTT_P1):
                t.wait_ge(p.sems["lq0"], 16 * (tt + 1))
                W(t, "xp", tt - 1)
                for c in range(DCH):
                    ins = t.matmul(pc32[tt % 2], wdqb[:, c * 32:(c + 1) * 32],
                                   xq[:, c * 512:(c + 1) * 512],
                                   start=(c == 0), stop=(c == DCH - 1))
                p.inc(ins, "mm", 1)

        @blk.scalar
        def _(s):
            for tt in range(NTT_P1):
                s.wait_ge(p.sems["mm"], tt + 1)
                ins = s.activation(cqT[:, tt * 512:(tt + 1) * 512], pc32[tt % 2],
                                   AF.Identity, bias=bdq[:, :])
                p.inc(ins, "xp", 1)
        mm_p1 = p.c["mm"]; xp_p1 = p.c["xp"]

        pq = [bankA[:, 0:512], bankB[:, 0:512]]
        pq64 = [(bankA[0:32, 0:512], bankA[32:64, 0:512], bankA[0:64, 0:512]),
                (bankB[0:32, 0:512], bankB[32:64, 0:512], bankB[0:64, 0:512])]
        for h in range(H if (upto >= 3 and not (sk & 2)) else 0):
            mmb = p.c["mm"]; xpb = p.c["xp"]

            @blk.tensor
            def _(t, h=h, mmb=mmb, xpb=xpb):
                for tt in range(NTT):
                    W(t, "xp", max(tt + 1, xpb + tt - 1))
                    ins = t.matmul(pq[tt % 2], wuqb[:, h * 128:(h + 1) * 128],
                                   cqT[:, tt * 512:(tt + 1) * 512], start=True, stop=True)
                    p.inc(ins, "mm", 1)
                for tt in range(NTT):
                    W(t, "xp", max(xpb + tt + 1, xpb + NTT + tt - 1))
                    lo, hi, _full = pq64[tt % 2]
                    bank = bankA if tt % 2 == 0 else bankB
                    t.matmul(lo, wuktb[:, h * 32:(h + 1) * 32],
                             qc1[h % 2][:, tt * 512:(tt + 1) * 512],
                             start=True, stop=True, tile_position=(0, 0))
                    t.matmul(hi, wqrb[:, h * 32:(h + 1) * 32],
                             cqT[:, tt * 512:(tt + 1) * 512],
                             start=True, stop=True, tile_position=(0, 32))
                    ins = t.matmul(bank[64:68, 0:512], bukb[:, h * 4:(h + 1) * 4],
                                   qc1[h % 2][:, tt * 512:(tt + 1) * 512],
                                   start=True, stop=True, tile_position=(0, 64))
                    p.inc(ins, "mm", 1)

            @blk.scalar
            def _(s_, h=h, mmb=mmb, xpb=xpb):
                for tt in range(NTT):
                    s_.wait_ge(p.sems["mm"], mmb + tt + 1)
                    if h >= 2:
                        pass
                    ins = s_.activation(qc1[h % 2][:, tt * 512:(tt + 1) * 512],
                                        pq[tt % 2], AF.Identity, bias=buq[:, h:h + 1])
                    p.inc(ins, "xp", 1)
                for tt in range(NTT):
                    s_.wait_ge(p.sems["mm"], mmb + NTT + tt + 1)
                    bank = bankA if tt % 2 == 0 else bankB
                    ins = s_.activation(qh[0:65, h * T + tt * 512:h * T + (tt + 1) * 512],
                                        bank[0:65, 0:512], AF.Identity,
                                        bias=qbias[:, h:h + 1])
                    p.inc(ins, "xp", 1)
        mm_p3 = p.c["mm"]; xp_p3 = p.c["xp"]

        @blk.tensor
        def _(t):
            if upto < 4 or (sk & 4):
                return
            t.wait_ge(p.sems["xp"], xp_p3)
            i = 0
            for h in range(H):
                for half in range(2):
                    W(t, "xp", xp_p3 + i - 1)
                    ins = t.matmul(pq64[i % 2][0], wuvtb[:, h * 32:(h + 1) * 32],
                                   wob[:, h * D + half * 512:h * D + (half + 1) * 512],
                                   start=True, stop=True)
                    p.inc(ins, "mm", 1)
                    i += 1
            if sk & 8:
                for half in range(2):
                    p.inc(t.nop(), "mm", 1)
            else:
                for half in range(2):
                    for c in range(DCH):
                        ins = t.matmul(bankC[0:4, :] if half == 0 else bankD[0:4, :],
                                       buvb[:, 4 * c:4 * c + 4],
                                       wob[:, c * D + half * 512:c * D + (half + 1) * 512],
                                       start=(c == 0), stop=(c == DCH - 1))
                    p.inc(ins, "mm", 1)

        @blk.scalar
        def _(s):
            if upto < 4 or (sk & 4):
                return
            i = 0
            for h in range(H):
                for half in range(2):
                    s.wait_ge(p.sems["mm"], mm_p3 + i + 1)
                    ins = s.copy(woe[:, h * D + half * 512:h * D + (half + 1) * 512],
                                 pq64[i % 2][0])
                    p.inc(ins, "xp", 1)
                    i += 1

        @blk.vector
        def _(v):
            if upto < 4 or (sk & 4):
                return
            p.gwait(v, "ld")
            for half in range(2):
                if sk & 24:
                    p.inc(v.engine_nop(), "dq", 1)
                    continue
                v.wait_ge(p.sems["mm"], mm_p3 + 16 + half + 1)
                v.tensor_add(beff_f[:, half * 512:(half + 1) * 512],
                             bankC[0:1, :] if half == 0 else bankD[0:1, :],
                             bo_row[:, half * 512:(half + 1) * 512])
                v.drain()
                ins = v.tensor_copy(beffb[:, half * 512:(half + 1) * 512],
                                    beff_f[:, half * 512:(half + 1) * 512])
                p.inc(ins, "dq", 1)
        mm_p4 = p.c["mm"]; xp_p4 = p.c["xp"]; dq_p4 = p.c["dq"]

        @blk.tensor
        def _(t):
            if upto < 4.05:
                return
            t.wait_ge(p.sems["dq"], dq_p4)
            if abs(upto - 4.13) < 1e-9:
                for kt in range(16):
                    ins = t.matmul(bankE[0:4, :] if kt % 2 == 0 else bankF[0:4, :],
                                   ones64c[:, :], sqt[:, :], start=True, stop=True)
                    p.inc(ins, "mm", 1)
                return
            if upto < 4.15:
                for kt in range(16):
                    p.inc(t.nop(), "mm", 1)
                return
            for kt in range(16):
                t.wait_ge(p.sems["dq"], dq_p4 + 2 * kt + 1)
                ins = t.matmul(bankE[0:4, :] if kt % 2 == 0 else bankF[0:4, :],
                               ones64c[:, :], sqt[:, :], start=True, stop=True)
                p.inc(ins, "mm", 1)

        @blk.vector
        def _(v):
            if abs(upto - 4.12) < 1e-9:
                v.wait_ge(p.sems["lk0"], 16)
                for kt in range(16):
                    ins = v.tensor_mul(sqt[:, :], khat[0:64, kt * 512:(kt + 1) * 512],
                                       khat[0:64, kt * 512:(kt + 1) * 512])
                    p.inc(ins, "dq", 2)
                p.inc(v.engine_nop(), "dq", 1)
                return
            if abs(upto - 4.13) < 1e-9:
                for kt in range(16):
                    p.inc(v.engine_nop(), "dq", 2)
                p.inc(v.engine_nop(), "dq", 1)
                return
            if upto < 4.1:
                if upto >= 4.07:
                    v.wait_ge(p.sems["lk0"], 16)
                if upto >= 4.05:
                    for kt in range(16):
                        p.inc(v.engine_nop(), "dq", 2)
                    p.inc(v.engine_nop(), "dq", 1)
                return
            v.wait_ge(p.sems["lk0"], 16)
            for kt in range(16):
                W(v, "mm", mm_p4 + kt)
                ins = v.tensor_mul(sqt[:, :], khat[0:65, kt * 512:(kt + 1) * 512],
                                   khat[0:65, kt * 512:(kt + 1) * 512])
                p.inc(ins, "dq", 1)
                if upto >= 4.2:
                    v.wait_ge(p.sems["mm"], mm_p4 + kt + 1)
                    ins = v.reduce_max(kn2m[:, kt:kt + 1],
                                       bankE[0:1, :] if kt % 2 == 0 else bankF[0:1, :], axis=AX)
                    p.inc(ins, "dq", 1)
                else:
                    p.inc(v.engine_nop(), "dq", 1)
            v.drain()
            ins = v.reduce_max(kmax[:, :], kn2m[:, :], axis=AX) if upto >= 4.2 else v.engine_nop()
            p.inc(ins, "dq", 1)
        mm_p5a = p.c["mm"]; dq_p5a = p.c["dq"]

        @blk.scalar
        def _(s):
            if upto < 4.4:
                return
            s.wait_ge(p.sems["dq"], dq_p5a)
            s.activation(kmax[:, :], kmax[:, :], AF.Ln)
            s.drain()
            ins = s.activation(kmax[:, :], kmax[:, :], AF.Exp, scale=0.5)
            p.inc(ins, "xp", 1)
        xp_p5a = p.c["xp"]

        @blk.tensor
        def _(t):
            if upto < 4.6:
                return
            for i in range(32):
                t.wait_ge(p.sems["dq"], dq_p5a + 2 * i + 1)
                ins = t.matmul(bankE[0:4, :] if i % 2 == 0 else bankF[0:4, :],
                               ones64c[:, :], sqt[:, :], start=True, stop=True)
                p.inc(ins, "mm", 1)

        @blk.vector
        def _(v):
            if upto < 4.6:
                return
            for i in range(32):
                h, tt = divmod(i, NTT)
                W(v, "mm", mm_p5a + i)
                ins = v.tensor_mul(sqt[:, :], qh[0:65, h * T + tt * 512:h * T + (tt + 1) * 512],
                                   qh[0:65, h * T + tt * 512:h * T + (tt + 1) * 512])
                p.inc(ins, "dq", 1)
                v.wait_ge(p.sems["mm"], mm_p5a + i + 1)
                ins = v.reduce_max(qm2[:, i:i + 1],
                                   bankE[0:1, :] if i % 2 == 0 else bankF[0:1, :], axis=AX)
                p.inc(ins, "dq", 1)
        mm_p5b = p.c["mm"]; dq_p5b = p.c["dq"]

        @blk.scalar
        def _(s):
            if upto < 4.7:
                return
            s.wait_ge(p.sems["dq"], dq_p5b)
            s.activation(qm2[:, :], qm2[:, :], AF.Ln)
            s.drain()
            ins = s.activation(qm2[:, :], qm2[:, :], AF.Exp, scale=0.5)
            p.inc(ins, "xp", 1)
        xp_p5b = p.c["xp"]

        @blk.vector
        def _(v):
            if upto < 4.8:
                return
            v.wait_ge(p.sems["xp"], xp_p5b)
            v.tensor_single_scalar(negu[:, :], qm2[:, :], kmax[0:1, 0:1], ALU.mult)
            v.drain()
            v.tensor_scalar_mul(negu[:, :], negu[:, :], -SCALE0)
            v.drain()
            ins = v.tensor_copy(negub[:, :], negu[:, :])
            p.inc(ins, "dq", 1)
        dq_p5c = p.c["dq"]

        @blk.tensor
        def _(t):
            if upto < 4.9:
                return
            t.wait_ge(p.sems["dq"], dq_p5c)
            ins = t.matmul(bankE[:, 0:32], onesr128[:, :], negub[:, :], start=True, stop=True)
            p.inc(ins, "mm", 1)
        mm_p5c = p.c["mm"]

        @blk.vector
        def _(v):
            if upto < 4.9:
                return
            v.wait_ge(p.sems["mm"], mm_p5c)
            ins = v.tensor_copy(Ubc[:, :], bankE[:, 0:32])
            p.inc(ins, "dq", 1)
        dq_ubc = p.c["dq"]

        if upto < 6:
            NTT_RUN = 0
        elif upto == 6:
            NTT_RUN = 1
        else:
            NTT_RUN = NTT
        pS = [bankA, bankB]
        pOT = bankC[:, 0:132]
        pB2 = bankD[:, 0:128]
        dva_prev_end = p.c["dva"]

        for tt in range(NTT_RUN):

            @blk.gpsimd
            def _(g, tt=tt, dva_prev_end=dva_prev_end):
                if tt == 0:
                    g.wait_ge(p.sems["lk0"], 16)
                if tt > 0:
                    g.wait_ge(p.sems["dva"], dva_prev_end)
                p.dma(g, "lx",
                      _chunked_sbuf_ap(xf, DCH, 512),
                      xT.rearrange("(c p) t -> p c t", p=128)[:, :, tt * 512:(tt + 1) * 512])
            ld_xf = p.c["lx0"]

            for h in range(H if upto > 6 else 2):
                U = tt * H + h
                msb = p.c["ms"]; xeb = p.c["xe"]; mvb = p.c["mv"]
                xob = p.c["xo"]; d6b = p.c["d6"]

                @blk.tensor
                def _(t, tt=tt, h=h, U=U, msb=msb, xeb=xeb, mvb=mvb, xob=xob, d6b=d6b):
                    def value_mms(cb):
                        t.wait_ge(p.sems["xe"], xeb + cb + 1)
                        for j in range(2):
                            kc = 2 * cb + j
                            for qc in range(4):
                                ins = t.matmul(pOT[:, qc * 33:(qc + 1) * 33],
                                               et[(xeb + cb) % 3][:, j * 512 + qc * 128:j * 512 + (qc + 1) * 128],
                                               vh[:, kc * 33:(kc + 1) * 33],
                                               start=(cb == 0 and j == 0),
                                               stop=(cb == NKB - 1 and j == 1),
                                               skip_group_check=True)
                        p.inc(ins, "mv", 1)

                    for cb in range(NKB):
                        W(t, "xe", xeb + cb - 1)
                        if cb == 0:
                            if U > 0:
                                W(t, "d6", d6b)
                            else:
                                t.wait_ge(p.sems["xp"], xp_p4)
                                t.wait_ge(p.sems["dq"], dq_ubc)
                                t.wait_ge(p.sems["lk0"], 16)
                        for j in range(2):
                            kc = 2 * cb + j
                            ins = t.matmul(pS[(xeb + cb) % 2][:, j * 512:(j + 1) * 512],
                                           khat[0:128, kc * 128:(kc + 1) * 128],
                                           qh[0:128, h * T + tt * 512:h * T + (tt + 1) * 512],
                                           start=True, stop=True)
                        p.inc(ins, "ms", 1)
                        if cb >= 2:
                            value_mms(cb - 2)
                    value_mms(NKB - 2)
                    value_mms(NKB - 1)

                @blk.scalar
                def _(s, tt=tt, h=h, U=U, msb=msb, xeb=xeb, mvb=mvb, d6b=d6b):
                    for cb in range(NKB):
                        s.wait_ge(p.sems["ms"], msb + cb + 1)
                        W(s, "mv", mvb + cb - 2)
                        ins = s.activation(et[(xeb + cb) % 3][:, :], pS[(xeb + cb) % 2][:, :],
                                           AF.Exp, bias=Ubc[:, h * NTT + tt:h * NTT + tt + 1],
                                           scale=SCALE0)
                        p.inc(ins, "xe", 1)

                @blk.vector
                def _(v, tt=tt, h=h, U=U, mvb=mvb, xob=xob):
                    v.wait_ge(p.sems["mv"], mvb + NKB)
                    for qc in range(4):
                        v.reciprocal(rq[:, :], pOT[:, qc * 33 + 32:qc * 33 + 33])
                        v.drain()
                        ins = v.tensor_single_scalar(otmp[:, qc * 32:(qc + 1) * 32],
                                                     pOT[:, qc * 33:qc * 33 + 32],
                                                     rq[0:128, 0:1], ALU.mult)
                        v.drain()
                    p.inc(ins, "d6", 1)

                xob_tr = p.c["xo"]

                @blk.tensor
                def _(t, d6b=d6b, xob_tr=xob_tr):
                    t.wait_ge(p.sems["d6"], d6b + 1)
                    W(t, "xo", xob_tr)
                    ins = t.transpose(pB2, otmp[:, :], identf2[:, :])
                    p.inc(ins, "mb", 1)

                @blk.scalar
                def _(s, h=h, U=U):
                    s.wait_ge(p.sems["mb"], U + 1)
                    for qc in range(4):
                        ins = s.copy(onrm[:, h * 512 + qc * 128:h * 512 + (qc + 1) * 128],
                                     pB2[qc * 32:(qc + 1) * 32, 0:128])
                    p.inc(ins, "xo", 1)

            if upto == 6:
                continue
            d6_done = p.c["d6"]
            xe_done = p.c["xe"]
            mpb = p.c["mp"]
            dvab = p.c["dva"]
            pA = [bankA[:, 0:512], bankB[:, 0:512]]
            pSum = bankE[0:4, :]
            pSq = bankF[0:4, :]

            @blk.tensor
            def _(t, tt=tt, mpb=mpb, dvab=dvab, d6_done=d6_done, xe_done=xe_done,
                  xo_done=p.c["xo"], ld_xf=ld_xf, dva_prev_end=dva_prev_end):
                t.wait_ge(p.sems["xo"], xo_done)
                t.wait_ge(p.sems["xe"], xe_done)
                t.wait_ge(p.sems["lx0"], ld_xf)
                for c in range(DCH):
                    W(t, "dva", dvab + 3 * (c - 2) + 1 if c >= 2 else 0)
                    for h in range(H):
                        t.matmul(pA[c % 2], woe[:, h * D + c * 128:h * D + (c + 1) * 128],
                                 onrm[:, h * 512:(h + 1) * 512], start=(h == 0), stop=False)
                    ins = t.matmul(pA[c % 2], beffb[:, c * 128:(c + 1) * 128], onesq[:, :],
                                   start=False, stop=True)
                    p.inc(ins, "mp", 1)

            @blk.vector
            def _(v, tt=tt, mpb=mpb):
                for c in range(DCH):
                    v.wait_ge(p.sems["mp"], mpb + c + 1)
                    ins = v.tensor_add(ts[:, c * 512:(c + 1) * 512],
                                       xf[:, c * 512:(c + 1) * 512], pA[c % 2])
                    p.inc(ins, "dva", 1)
                    v.drain()
                    ins = v.tensor_copy(tb[:, c * 512:(c + 1) * 512],
                                        ts[:, c * 512:(c + 1) * 512])
                    p.inc(ins, "dva", 1)
                    v.drain()
                    ins = v.tensor_mul(sqb[:, c * 512:(c + 1) * 512],
                                       tb[:, c * 512:(c + 1) * 512],
                                       tb[:, c * 512:(c + 1) * 512])
                    p.inc(ins, "dva", 1)

            @blk.tensor
            def _(t, tt=tt, mpb=mpb, dvab=dvab, dva_prev_end=dva_prev_end):
                if tt == 0:
                    t.wait_ge(p.sems["dq"], dq_ubc)
                for c in range(DCH):
                    t.wait_ge(p.sems["dva"], dvab + 3 * c + 3)
                    t.matmul(pSum, ones128c[:, :], tb[:, c * 512:(c + 1) * 512],
                             start=(c == 0), stop=(c == DCH - 1), skip_group_check=True)
                    ins = t.matmul(pSq, ones128c[:, :], sqb[:, c * 512:(c + 1) * 512],
                                   start=(c == 0), stop=(c == DCH - 1), skip_group_check=True)
                p.inc(ins, "mp", 1)

            @blk.vector
            def _(v, tt=tt, mpb=mpb):
                v.wait_ge(p.sems["mp"], mpb + DCH + 1)
                v.tensor_scalar_mul(mrow[:, :], pSum[0:1, :], 1.0 / D)
                v.drain()
                v.tensor_mul(m2row[:, :], mrow[:, :], mrow[:, :])
                v.drain()
                v.scalar_tensor_tensor(vrow[:, :], pSq[0:1, :], 1.0 / D, m2row[:, :],
                                       ALU.mult, ALU.subtract)
                v.drain()
                ins = v.tensor_scalar_add(vrow[:, :], vrow[:, :], LN_EPS)
                p.inc(ins, "dva", 1)

            @blk.scalar
            def _(s, tt=tt, dvab=dvab):
                s.wait_ge(p.sems["dva"], dvab + 25)
                s.activation(vrow[:, :], vrow[:, :], AF.Ln)
                s.drain()
                ins = s.activation(rrow[:, :], vrow[:, :], AF.Exp, scale=-0.5)
                p.inc(ins, "xr", 1)

            @blk.vector
            def _(v, tt=tt):
                v.wait_ge(p.sems["xr"], tt + 1)
                v.drain()
                ins = v.tensor_mul(brow[:, :], mrow[:, :], rrow[:, :])
                p.inc(ins, "dva", 1)

            @blk.tensor
            def _(t, tt=tt, dvab=dvab):
                t.wait_ge(p.sems["dva"], dvab + 26)
                t.matmul(bankE[:, :], ones1x128f[:, :], rrow[:, :], start=True, stop=True)
                ins = t.matmul(bankF[:, :], ones1x128f[:, :], brow[:, :], start=True, stop=True)
                p.inc(ins, "mn", 1)

            @blk.vector
            def _(v, tt=tt, dvab=dvab):
                v.wait_ge(p.sems["mn"], tt + 1)
                for c in range(DCH):
                    n_prev = tt * DCH + c - 2
                    if n_prev >= 0:
                        v.wait_ge(p.sems["so%d" % (n_prev % 2)], 16 * (n_prev // 2 + 1))
                    v.tensor_mul(tmpf[:, :], ts[:, c * 512:(c + 1) * 512], bankE[:, 0:512])
                    v.drain()
                    ins = v.tensor_sub(x1c[c % 2][:, :], tmpf[:, :], bankF[:, 0:512])
                    p.inc(ins, "dva", 1)
                    v.drain()

            @blk.sync
            def _(sy, tt=tt, dvab=dvab):
                for c in range(DCH):
                    sy.wait_ge(p.sems["dva"], dvab + 26 + c + 1)
                    p.dma(sy, "so", x1T[c * 128:(c + 1) * 128, tt * 512:(tt + 1) * 512],
                          x1c[c % 2][:, :])
            dva_prev_end = p.c["dva"]

        @blk.gpsimd
        def _(g):
            p.gwait(g, "so")
    return nc


def emit_ln_tail(p, blk, dvab, tt, ts, tb, sqb, tmpf, x1c, rows, ones128c,
                 ones1x128f, bankE, bankF, outT):
    mrow, m2row, vrow, rrow, brow = rows
    mpb = p.c["mp"]
    pSum = bankE[0:4, :]
    pSq = bankF[0:4, :]

    @blk.tensor
    def _(t):
        for c in range(DCH):
            t.wait_ge(p.sems["dva"], dvab + 3 * c + 3)
            t.matmul(pSum, ones128c[:, 0:4], tb[:, c * 512:(c + 1) * 512],
                     start=(c == 0), stop=(c == DCH - 1), skip_group_check=True)
            ins = t.matmul(pSq, ones128c[:, 0:4], sqb[:, c * 512:(c + 1) * 512],
                           start=(c == 0), stop=(c == DCH - 1), skip_group_check=True)
        p.inc(ins, "mp", 1)
    mp_stats = p.c["mp"]

    @blk.vector
    def _(v):
        v.wait_ge(p.sems["mp"], mp_stats)
        v.tensor_scalar_mul(mrow[:, :], pSum[0:1, :], 1.0 / D)
        v.drain()
        v.tensor_mul(m2row[:, :], mrow[:, :], mrow[:, :])
        v.drain()
        v.scalar_tensor_tensor(vrow[:, :], pSq[0:1, :], 1.0 / D, m2row[:, :],
                               ALU.mult, ALU.subtract)
        v.drain()
        ins = v.tensor_scalar_add(vrow[:, :], vrow[:, :], LN_EPS)
        p.inc(ins, "dva", 1)

    @blk.scalar
    def _(s):
        s.wait_ge(p.sems["dva"], dvab + 25)
        s.activation(vrow[:, :], vrow[:, :], AF.Ln)
        s.drain()
        ins = s.activation(rrow[:, :], vrow[:, :], AF.Exp, scale=-0.5)
        p.inc(ins, "xr", 1)
    xr_now = p.c["xr"]

    @blk.vector
    def _(v):
        v.wait_ge(p.sems["xr"], xr_now)
        v.drain()
        ins = v.tensor_mul(brow[:, :], mrow[:, :], rrow[:, :])
        p.inc(ins, "dva", 1)

    @blk.tensor
    def _(t):
        t.wait_ge(p.sems["dva"], dvab + 26)
        t.matmul(bankE[:, 0:512], ones1x128f[:, :], rrow[:, :], start=True, stop=True)
        ins = t.matmul(bankF[:, 0:512], ones1x128f[:, :], brow[:, :], start=True, stop=True)
        p.inc(ins, "mn", 1)
    mn_now = p.c["mn"]
    n_base = p.groups["so"]["i"]

    @blk.vector
    def _(v):
        v.wait_ge(p.sems["mn"], mn_now)
        for c in range(DCH):
            n_prev = n_base + c - 2
            if n_prev >= 0:
                v.wait_ge(p.sems["so%d" % (n_prev % 2)], 16 * (n_prev // 2 + 1))
            v.tensor_mul(tmpf[:, :], ts[:, c * 512:(c + 1) * 512], bankE[:, 0:512])
            v.drain()
            ins = v.tensor_sub(x1c[c % 2][:, :], tmpf[:, :], bankF[:, 0:512])
            p.inc(ins, "dva", 1)
            v.drain()

    @blk.sync
    def _(sy):
        for c in range(DCH):
            sy.wait_ge(p.sems["dva"], dvab + 26 + c + 1)
            p.dma(sy, "so", outT[c * 128:(c + 1) * 128, tt * 512:(tt + 1) * 512],
                  x1c[c % 2][:, :])


def emit_ln_tail2(p, blk, dvab, mpb, tt, ts, tb, sqb, tmpf, x1c, rows, ones128c,
                  ones1x128f, bankE, bankF, outT):
    mrow, m2row, vrow, rrow, brow = rows
    pSum = bankE[0:4, :]
    pSq = bankF[0:4, :]
    mp_stats = p.c["mp"]

    @blk.vector
    def _(v):
        v.wait_ge(p.sems["mp"], mp_stats)
        v.tensor_scalar_mul(mrow[:, :], pSum[0:1, :], 1.0 / D)
        v.drain()
        v.tensor_mul(m2row[:, :], mrow[:, :], mrow[:, :])
        v.drain()
        v.scalar_tensor_tensor(vrow[:, :], pSq[0:1, :], 1.0 / D, m2row[:, :],
                               ALU.mult, ALU.subtract)
        v.drain()
        ins = v.tensor_scalar_add(vrow[:, :], vrow[:, :], LN_EPS)
        p.inc(ins, "dva", 1)

    @blk.scalar
    def _(s):
        s.wait_ge(p.sems["dva"], dvab + 25)
        s.activation(vrow[:, :], vrow[:, :], AF.Ln)
        s.drain()
        ins = s.activation(rrow[:, :], vrow[:, :], AF.Exp, scale=-0.5)
        p.inc(ins, "xr", 1)
    xr_now = p.c["xr"]

    @blk.vector
    def _(v):
        v.wait_ge(p.sems["xr"], xr_now)
        v.drain()
        ins = v.tensor_mul(brow[:, :], mrow[:, :], rrow[:, :])
        p.inc(ins, "dva", 1)

    @blk.tensor
    def _(t):
        t.wait_ge(p.sems["dva"], dvab + 26)
        t.matmul(bankE[:, 0:512], ones1x128f[:, :], rrow[:, :], start=True, stop=True)
        ins = t.matmul(bankF[:, 0:512], ones1x128f[:, :], brow[:, :], start=True, stop=True)
        p.inc(ins, "mn", 1)
    mn_now = p.c["mn"]
    n_base = p.groups["so"]["i"]

    @blk.vector
    def _(v):
        v.wait_ge(p.sems["mn"], mn_now)
        for c in range(DCH):
            n_prev = n_base + c - 2
            if n_prev >= 0:
                v.wait_ge(p.sems["so%d" % (n_prev % 2)], 16 * (n_prev // 2 + 1))
            v.tensor_mul(tmpf[:, :], ts[:, c * 512:(c + 1) * 512], bankE[:, 0:512])
            v.drain()
            ins = v.tensor_sub(x1c[c % 2][:, :], tmpf[:, :], bankF[:, 0:512])
            p.inc(ins, "dva", 1)
            v.drain()

    @blk.sync
    def _(sy):
        for c in range(DCH):
            sy.wait_ge(p.sems["dva"], dvab + 26 + c + 1)
            p.dma(sy, "so", outT[c * 128:(c + 1) * 128, tt * 512:(tt + 1) * 512],
                  x1c[c % 2][:, :])


def build_s_l0ffn():
    p = P("s_l0ffn")
    nc = p.nc
    E = 8
    HT = 8
    NT = 16
    NG = 2
    KC = 17
    x1T = p.inp("x1T", [D, T])
    Wgt = p.inp("Wgt", [D, E])
    W1t = p.inp("W1t", [E, D, 4096])
    b1 = p.inp("b1", [E, 4096])
    WotP = p.inp("WotP", [KC * 128, D])
    x2T = p.out("x2T", [D, T])
    with p.es:
        x1Tb = p.sb("x1Tb", [128, DCH * T], BF16)
        wgtf = p.sb("wgtf", [128, DCH * E], F32)
        xrc = [p.sb("xrc0", [128, DCH * 128], F32)] * 2
        b1buf = [p.sb("b1buf%d" % i, [1, 4096], BF16) for i in range(2)]
        w1buf = [p.sb("w1buf%d" % i, [128, DCH * 512], BF16) for i in range(2)]
        wotbuf = [p.sb("wotbuf%d" % i, [128, KC * 128], BF16) for i in range(2)]
        sgb = p.sb("sgb", [128, 8 * 2048], BF16)
        gsum = p.sb("gsum", [128, 8 * (2048 + 128)], BF16)
        gsumT = p.sb("gsumT", [128, KC * 512], BF16)
        uf = p.sb("uf", [128, 512], F32)
        identb = p.sb("identb", [128, 128], BF16)
        onesr = p.sb("onesr", [1, 128], BF16)
        ones128c = p.sb("ones128c", [128, 4], BF16)
        ones1x128f = p.sb("ones1x128f", [1, 128], F32)
        nm = p.sb("nm", [128, 1], F32)
        e8 = p.sb("e8", [128, E], F32)
        se = p.sb("se", [128, 1], F32)
        rcp = p.sb("rcp", [128, 1], F32)
        gs = p.sb("gs", [128, E], F32)
        m1v = p.sb("m1v", [128, 1], F32)
        m2v = p.sb("m2v", [128, 1], F32)
        msk1 = p.sb("msk1", [128, E], F32)
        msk2 = p.sb("msk2", [128, E], F32)
        g2 = p.sb("g2", [128, E], F32)
        den = p.sb("den", [128, 1], F32)
        w1v = p.sb("w1v", [128, 1], F32)
        w2v = p.sb("w2v", [128, 1], F32)
        t1m = p.sb("t1m", [128, E], F32)
        comb = p.sb("comb", [128, NT * E], F32)
        csum = p.sb("csum", [128, NT], F32)
        xf = p.sb("xf", [128, DCH * 512], F32)
        ts = p.sb("ts", [128, DCH * 512], F32)
        tb = p.sb("tb", [128, 512], BF16)
        sqb = p.sb("sqb", [128, 512], BF16)
        tmpf = uf
        x1c = [p.sb("x1c%d" % i, [128, 512], F32) for i in range(2)]
        _r = {n: p.sb(n, [1, 512], F32) for n in ("mrow", "m2row", "vrow")}
        rows = (_r["mrow"], _r["m2row"], _r["vrow"], _r["vrow"], _r["m2row"])
        bankA = p.ps("bankA", [128, 512])
        bankB = p.ps("bankB", [128, 512])
        bankC = p.ps("bankC", [128, 512])
        bankD = p.ps("bankD", [128, 512])
        bankE = p.ps("bankE", [128, 512])
        bankF = p.ps("bankF", [128, 512])
        bankG = p.ps("bankG", [128, 128], BF16)
        bankH = p.ps("bankH", [128, 128], BF16)
        for s_ in ("vm", "mm", "xp", "dq", "m1", "sg", "da", "mt", "xt", "m2",
                   "mp", "dva", "xr", "mn"):
            p.sem(s_)
        p.dma_group("ld", 4)
        p.dma_group("lr", 1)
        p.dma_group("lb", 2)
        p.dma_group("lw", 2)
        p.dma_group("lo", 2)
        p.dma_group("lx", 1)
        p.dma_group("so", 2)
        blk = p.block()

        def W(eng, name, val):
            if val > 0:
                eng.wait_ge(p.sems[name], val)

        @blk.gpsimd
        def _(g):
            p.dma(g, "ld", _chunked_sbuf_ap(x1Tb, DCH, T), _featmaj_ap(x1T, D, T))
            p.dma(g, "ld", _chunked_sbuf_ap(wgtf, DCH, E), _featmaj_ap(Wgt, D, E))


        @blk.vector
        def _(v):
            v.memset(onesr[:, :], 1.0)
            v.memset(ones128c[:, :], 1.0)
            v.memset(ones1x128f[:, :], 1.0)
            v.memset(identb[:, :], 1.0)
            v.memset(bass.AP(gsum, 2049, [[gsum.shape[1], 128], [2176, 8], [1, 127]]), 0.0)
            p.inc(v.engine_nop(), "vm", 1)

        @blk.gpsimd
        def _(g):
            g.wait_ge(p.sems["vm"], 1)
            ins = g.affine_select(identb[:, :], identb[:, :], [[-1, 128]],
                                  ALU.is_equal, 0.0, channel_multiplier=1)
            p.inc(ins, "vm", 1)

        pR = [bankC[:, 0:E], bankD[:, 0:E]]

        @blk.gpsimd
        def _(g):
            for tc in range(NT):
                if tc >= 1:
                    g.wait_ge(p.sems["mm"], tc)
                p.dma(g, "lr", _chunked_sbuf_ap(xrc[tc % 2], DCH, 128),
                      x1T.rearrange("(c p) t -> p c t", p=128)[:, :, tc * 128:(tc + 1) * 128])

        @blk.tensor
        def _(t):
            p.gwait(t, "ld")
            t.wait_ge(p.sems["vm"], 1)
            for tc in range(NT):
                t.wait_ge(p.sems["lr0"], 16 * (tc + 1))
                W(t, "xp", tc - 1)
                for dc in range(DCH):
                    ins = t.matmul(pR[tc % 2], xrc[tc % 2][:, dc * 128:(dc + 1) * 128],
                                   wgtf[:, dc * E:(dc + 1) * E],
                                   start=(dc == 0), stop=(dc == DCH - 1))
                p.inc(ins, "mm", 1)

        @blk.scalar
        def _(s):
            for tc in range(NT):
                s.wait_ge(p.sems["dq"], 18 * tc + 1)
                ins = s.activation(e8[:, :], pR[tc % 2], AF.Exp, bias=nm[:, :],
                                   accum_out=se[:, :])
                p.inc(ins, "xp", 1)

        @blk.vector
        def _(v):
            for tc in range(NT):
                v.wait_ge(p.sems["mm"], tc + 1)
                W(v, "xp", tc)
                ins = v.reduce_max(nm[:, :], pR[tc % 2], axis=AX, negate=True)
                p.inc(ins, "dq", 1)
                v.wait_ge(p.sems["xp"], tc + 1)
                v.reciprocal(rcp[:, :], se[:, :])
                v.drain()
                v.tensor_single_scalar(gs[:, :], e8[:, :], rcp[0:128, 0:1], ALU.mult)
                v.drain()
                v.reduce_max(m1v[:, :], gs[:, :], axis=AX)
                v.drain()
                v.tensor_single_scalar(msk1[:, :], gs[:, :], m1v[0:128, 0:1], ALU.is_ge)
                v.drain()
                v.scalar_tensor_tensor(g2[:, :], msk1[:, :], -1e30, gs[:, :],
                                       ALU.mult, ALU.add)
                v.drain()
                v.reduce_max(m2v[:, :], g2[:, :], axis=AX)
                v.drain()
                v.tensor_single_scalar(msk2[:, :], g2[:, :], m2v[0:128, 0:1], ALU.is_ge)
                v.tensor_add(den[:, :], m1v[:, :], m2v[:, :])
                v.drain()
                v.tensor_scalar_add(den[:, :], den[:, :], 1e-9)
                v.drain()
                v.reciprocal(den[:, :], den[:, :])
                v.drain()
                v.tensor_single_scalar(w1v[:, :], m1v[:, :], den[0:128, 0:1], ALU.mult)
                v.tensor_single_scalar(w2v[:, :], m2v[:, :], den[0:128, 0:1], ALU.mult)
                v.drain()
                v.tensor_single_scalar(t1m[:, :], msk1[:, :], w1v[0:128, 0:1], ALU.mult)
                v.tensor_add(csum[:, tc:tc + 1], w1v[:, :], w2v[:, :])
                v.drain()
                ins = v.scalar_tensor_tensor(comb[:, tc * E:(tc + 1) * E], msk2[:, :],
                                             w2v[0:128, 0:1], t1m[:, :], ALU.mult, ALU.add)
                p.inc(ins, "dq", 17)
        mm_rt = p.c["mm"]; dq_rt = p.c["dq"]; xp_rt = p.c["xp"]

        ht_seq = [4, 5, 6, 7, 0, 1, 2, 3]
        pH = [bankA, bankB]
        hfree = {0: None, 1: None}
        for g_i in range(NG):
            for e in range(E):
                @blk.gpsimd
                def _(g, g_i=g_i, e=e, m1b_=None):
                    if g_i == 0 and e == 0:
                        g.wait_ge(p.sems["vm"], 2)
                    m1_now = p.c["m1"]
                    ei = g_i * E + e
                    if ei >= 2:
                        g.wait_ge(p.sems["m1"], m1_now - 64)
                    p.dma(g, "lb", b1buf[ei % 2][:, :], b1[e, :][None, :])
                    for i, ht in enumerate(ht_seq):
                        gi = (g_i * E + e) * HT + i
                        if gi >= 2:
                            g.wait_ge(p.sems["m1"], (gi - 1) * 8)
                        p.dma(g, "lw", _chunked_sbuf_ap(w1buf[gi % 2], DCH, 512),
                              _featmaj_ap(W1t[e, :, :], D, 4096)[:, :, ht * 512:(ht + 1) * 512])

                m1b = p.c["m1"]; sgb_ = p.c["sg"]; dab = p.c["da"]

                @blk.tensor
                def _(t, g_i=g_i, e=e, m1b=m1b, sgb_=sgb_, dab=dab):
                    ei = g_i * E + e
                    t.wait_ge(p.sems["lb%d" % (ei % 2)], 16 * (ei // 2 + 1))
                    for i, ht in enumerate(ht_seq):
                        gi = ei * HT + i
                        t.wait_ge(p.sems["lw%d" % (gi % 2)], 16 * (gi // 2 + 1))
                        for tcg in range(8):
                            tc = g_i * 8 + tcg
                            k = i * 8 + tcg
                            par = k % 2
                            tok = hfree[par]
                            if tok is not None:
                                t.wait_ge(p.sems[tok[0]], tok[1])
                            for dc in range(DCH):
                                t.matmul(pH[par][:, :],
                                         x1Tb[:, dc * T + tc * 128:dc * T + (tc + 1) * 128],
                                         w1buf[gi % 2][:, dc * 512:(dc + 1) * 512],
                                         start=(dc == 0), stop=False)
                            ins = t.matmul(pH[par][:, :], onesr[:, :],
                                           b1buf[(g_i * E + e) % 2][:, ht * 512:(ht + 1) * 512],
                                           start=False, stop=True)
                            p.inc(ins, "m1", 1)
                            if i < 4:
                                hfree[par] = ("sg", sgb_ + i * 8 + tcg + 1)
                            else:
                                hfree[par] = ("da", dab + (i - 4) * 8 + tcg + 1)

                @blk.scalar
                def _(s, g_i=g_i, e=e, m1b=m1b, dab=dab):
                    W(s, "da", dab)
                    for i in range(4):
                        for tcg in range(8):
                            s.wait_ge(p.sems["m1"], m1b + i * 8 + tcg + 1)
                            par = (i * 8 + tcg) % 2
                            ins = s.activation(
                                sgb[:, tcg * 2048 + i * 512:tcg * 2048 + (i + 1) * 512],
                                pH[par][:, :], AF.Sigmoid)
                            p.inc(ins, "sg", 1)

                @blk.vector
                def _(v, g_i=g_i, e=e, m1b=m1b, dab=dab, sgb_=sgb_):
                    if g_i > 0 and e == 0:
                        v.wait_ge(p.sems["mt"], 136 * g_i)
                    for i in range(4, 8):
                        ht = ht_seq[i]
                        for tcg in range(8):
                            tc = g_i * 8 + tcg
                            v.wait_ge(p.sems["m1"], m1b + i * 8 + tcg + 1)
                            v.wait_ge(p.sems["sg"], sgb_ + ht * 8 + tcg + 1)
                            par = (i * 8 + tcg) % 2
                            v.tensor_mul(uf[:, :], pH[par][:, :],
                                         sgb[:, tcg * 2048 + ht * 512:tcg * 2048 + (ht + 1) * 512])
                            v.drain()
                            dst = gsum[:, tcg * 2176 + ht * 512:tcg * 2176 + (ht + 1) * 512]
                            ce = comb[:, tc * E + e:tc * E + e + 1]
                            if e == 0:
                                ins = v.tensor_single_scalar(dst, uf[:, :], ce, ALU.mult)
                            else:
                                ins = v.scalar_tensor_tensor(dst, uf[:, :], ce, dst,
                                                             ALU.mult, ALU.add)
                            p.inc(ins, "da", 1)
                            v.drain()

            da_exp = p.c["da"]

            @blk.vector
            def _(v, g_i=g_i, da_exp=da_exp):
                for tcg in range(8):
                    tc = g_i * 8 + tcg
                    ins = v.tensor_copy(gsum[:, tcg * 2176 + 2048:tcg * 2176 + 2049],
                                        csum[:, tc:tc + 1])
                    v.drain()
                p.inc(ins, "da", 1)
            da_cs = p.c["da"]

            pT = [bankG, bankH]
            for tloc in range(2):
                tt = g_i * 2 + tloc
                mtb = p.c["mt"]; xtb = p.c["xt"]

                @blk.gpsimd
                def _(g, tt=tt):
                    if tt > 0:
                        g.wait_ge(p.sems["dva"], tt * 34)
                    p.dma(g, "lx", _chunked_sbuf_ap(xf, DCH, 512),
                          x1T.rearrange("(c p) t -> p c t", p=128)[:, :, tt * 512:(tt + 1) * 512])
                    for oc in range(DCH):
                        n = tt * DCH + oc
                        if n >= 2:
                            g.wait_ge(p.sems["m2"], n - 1)
                        p.dma(g, "lo", _chunked_sbuf_ap(wotbuf[n % 2], KC, 128),
                              _featmaj_ap(WotP[:, oc * 128:(oc + 1) * 128], KC * 128, 128))

                @blk.tensor
                def _(t, g_i=g_i, tloc=tloc, da_cs=da_cs, mtb=mtb, xtb=xtb):
                    for tq in range(4):
                        tcg = tloc * 4 + tq
                        for ch in range(KC):
                            k = tq * KC + ch
                            t.wait_ge(p.sems["da"], da_cs)
                            W(t, "xt", xtb + k - 1)
                            ins = t.transpose(pT[k % 2][:, :],
                                              gsum[:, tcg * 2176 + ch * 128:tcg * 2176 + (ch + 1) * 128],
                                              identb[:, :])
                            p.inc(ins, "mt", 1)

                @blk.scalar
                def _(s, tloc=tloc, mtb=mtb):
                    for tq in range(4):
                        for ch in range(KC):
                            k = tq * KC + ch
                            s.wait_ge(p.sems["mt"], mtb + k + 1)
                            ins = s.copy(gsumT[:, ch * 512 + tq * 128:ch * 512 + (tq + 1) * 128],
                                         pT[k % 2][:, :])
                            p.inc(ins, "xt", 1)
                xt_done = p.c["xt"]

                pX = [bankC, bankD]
                m2b = p.c["m2"]
                dvab = p.c["dva"]
                mpb = p.c["mp"]

                @blk.tensor
                def _(t, tt=tt, xt_done=xt_done, m2b=m2b, dvab=dvab):
                    t.wait_ge(p.sems["xt"], xt_done)
                    if tt == 0:
                        t.wait_ge(p.sems["xp"], NT)

                    def stats(c):
                        t.wait_ge(p.sems["dva"], dvab + 3 * c + 3)
                        t.matmul(bankE[0:4, :], ones128c[:, 0:4], tb[:, :],
                                 start=(c == 0), stop=(c == DCH - 1), skip_group_check=True)
                        ins = t.matmul(bankF[0:4, :], ones128c[:, 0:4], sqb[:, :],
                                       start=(c == 0), stop=(c == DCH - 1), skip_group_check=True)
                        p.inc(ins, "mp", 1)

                    for oc in range(DCH):
                        n = tt * DCH + oc
                        t.wait_ge(p.sems["lo%d" % (n % 2)], 16 * (n // 2 + 1))
                        if oc >= 2:
                            t.wait_ge(p.sems["dva"], dvab + 3 * (oc - 2) + 1)
                        elif tt > 0:
                            t.wait_ge(p.sems["dva"], dvab - 10)
                        for ch in range(KC):
                            ins = t.matmul(pX[oc % 2][:, :],
                                           wotbuf[n % 2][:, ch * 128:(ch + 1) * 128],
                                           gsumT[:, ch * 512:(ch + 1) * 512],
                                           start=(ch == 0), stop=(ch == KC - 1))
                        p.inc(ins, "m2", 1)
                        if oc >= 2:
                            stats(oc - 2)
                    stats(DCH - 2)
                    stats(DCH - 1)

                @blk.vector
                def _(v, tt=tt, m2b=m2b, mpb=mpb):
                    v.wait_ge(p.sems["lx0"], 16 * (tt + 1))
                    for oc in range(DCH):
                        v.wait_ge(p.sems["m2"], m2b + oc + 1)
                        W(v, "mp", mpb + oc)
                        ins = v.tensor_add(ts[:, oc * 512:(oc + 1) * 512],
                                           xf[:, oc * 512:(oc + 1) * 512], pX[oc % 2][:, :])
                        p.inc(ins, "dva", 1)
                        v.drain()
                        ins = v.tensor_copy(tb[:, :], ts[:, oc * 512:(oc + 1) * 512])
                        p.inc(ins, "dva", 1)
                        v.drain()
                        ins = v.tensor_mul(sqb[:, :], tb[:, :], tb[:, :])
                        p.inc(ins, "dva", 1)

                emit_ln_tail2(p, blk, dvab, mpb, tt, ts, tb, sqb, tmpf, x1c, rows,
                              ones128c, ones1x128f, bankE, bankF, x2T)

        @blk.gpsimd
        def _(g):
            p.gwait(g, "so")
    return nc


def build_s_l1qkv():
    p = P("s_l1qkv")
    nc = p.nc
    x2T = p.inp("x2T", [D, T])
    W_qkv = p.inp("W_qkv", [D, D])
    qkv = p.out("qkv", [T, D], BF16)
    kno = p.out("kno", [128, (T // 128) * 16])
    NT = T // 128
    with p.es:
        x2Tb = p.sb("x2Tb", [128, DCH * T], BF16)
        wqb = p.sb("wqb", [128, DCH * D], BF16)
        qst = p.sb("qst", [128, NT * D], BF16)
        sq = p.sb("sq", [128, 512], F32)
        knq = p.sb("knq", [128, NT * 16], F32)
        pq = [p.ps("pq%d" % i, [128, 512]) for i in range(2)]
        for s_ in ("mm", "xp", "dv"):
            p.sem(s_)
        p.dma_group("ld", 2)
        p.dma_group("so", 1)
        blk = p.block()

        @blk.gpsimd
        def _(g):
            p.dma(g, "ld", _chunked_sbuf_ap(x2Tb, DCH, T), _featmaj_ap(x2T, D, T))
            p.dma(g, "ld", _chunked_sbuf_ap(wqb, DCH, D), _featmaj_ap(W_qkv, D, D))

        @blk.tensor
        def _(t):
            p.gwait(t, "ld")
            i = 0
            for tc in range(NT):
                for half in range(2):
                    if i >= 2:
                        t.wait_ge(p.sems["xp"], i - 1)
                        t.wait_ge(p.sems["dv"], i - 1)
                    for dc in range(DCH):
                        ins = t.matmul(pq[i % 2][:, :],
                                       x2Tb[:, dc * T + tc * 128:dc * T + (tc + 1) * 128],
                                       wqb[:, dc * D + half * 512:dc * D + (half + 1) * 512],
                                       start=(dc == 0), stop=(dc == DCH - 1))
                    p.inc(ins, "mm", 1)
                    i += 1

        @blk.scalar
        def _(s):
            i = 0
            for tc in range(NT):
                for half in range(2):
                    s.wait_ge(p.sems["mm"], i + 1)
                    ins = s.copy(qst[:, tc * D + half * 512:tc * D + (half + 1) * 512],
                                 pq[i % 2][:, :])
                    p.inc(ins, "xp", 1)
                    i += 1

        @blk.vector
        def _(v):
            i = 0
            for tc in range(NT):
                for half in range(2):
                    v.wait_ge(p.sems["mm"], i + 1)
                    v.wait_ge(p.sems["xp"], i + 1)
                    v.tensor_copy(sq[:, :], pq[i % 2][:, :])
                    v.drain()
                    v.tensor_mul(sq[:, :], sq[:, :], sq[:, :])
                    v.drain()
                    for h in range(8):
                        v.reduce_sum(knq[:, tc * 16 + half * 8 + h:tc * 16 + half * 8 + h + 1],
                                     sq[:, h * 64:(h + 1) * 64], axis=AX)
                    ins = v.engine_nop()
                    p.inc(ins, "dv", 1)
                    v.drain()
                    i += 1

        @blk.sync
        def _(sy):
            sy.wait_ge(p.sems["xp"], 2 * NT)
            p.dma(sy, "so", qkv.rearrange("(k p) c -> p k c", p=128),
                  _chunked_sbuf_ap(qst, NT, D))
            sy.wait_ge(p.sems["dv"], 2 * NT)
            p.dma(sy, "so", kno[:, :], knq[:, :])

        @blk.gpsimd
        def _(g):
            p.gwait(g, "so")
    return nc


def build_s_l1rout():
    p = P("s_l1rout")
    nc = p.nc
    E = 16
    NT = T // 128
    x3T = p.inp("x3T", [D, T])
    router_w = p.inp("router_w", [D, E])
    router_b = p.inp("router_b", [E])
    m1o = p.out("m1o", [128, (T // 128) * E])
    m2o = p.out("m2o", [128, (T // 128) * E])
    w1o = p.out("w1o", [128, T // 128])
    w2o = p.out("w2o", [128, T // 128])
    counts = p.out("counts", [1, E])
    with p.es:
        xrc = p.sb("xrc", [128, DCH * 128], F32)
        wrf = p.sb("wrf", [128, DCH * E], F32)
        rbf = p.sb("rbf", [1, E], F32)
        onesf = p.sb("onesf", [1, 128], F32)
        ones128c = p.sb("ones128c", [128, 4], BF16)
        nm = p.sb("nm", [128, 1], F32)
        e16 = p.sb("e16", [128, E], F32)
        se = p.sb("se", [128, 1], F32)
        gs = p.sb("gs", [128, E], F32)
        m1v = p.sb("m1v", [128, 1], F32)
        m2v = p.sb("m2v", [128, 1], F32)
        msk1 = p.sb("msk1", [128, NT * E], F32)
        msk2 = p.sb("msk2", [128, NT * E], F32)
        g2 = p.sb("g2", [128, E], F32)
        den = p.sb("den", [128, 1], F32)
        w1v = p.sb("w1v", [128, NT], F32)
        w2v = p.sb("w2v", [128, NT], F32)
        cntb = p.sb("cntb", [128, E], BF16)
        crow = p.sb("crow", [1, E], F32)
        pR = None
        bankA = p.ps("bankA", [128, 512])
        bankB = p.ps("bankB", [128, 512])
        bankC = p.ps("bankC", [128, 512])
        for s_ in ("vm", "mm", "xp", "dq", "mc", "dc"):
            p.sem(s_)
        p.dma_group("ld", 2)
        p.dma_group("lr", 1)
        p.dma_group("so", 2)
        blk = p.block()

        def W(eng, name, val):
            if val > 0:
                eng.wait_ge(p.sems[name], val)

        @blk.gpsimd
        def _(g):
            p.dma(g, "ld", _chunked_sbuf_ap(wrf, DCH, E), _featmaj_ap(router_w, D, E))
            p.dma(g, "ld", rbf[:, :], router_b[None, :])
            for tc in range(NT):
                if tc >= 1:
                    g.wait_ge(p.sems["mm"], tc)
                p.dma(g, "lr", _chunked_sbuf_ap(xrc, DCH, 128),
                      x3T.rearrange("(c p) t -> p c t", p=128)[:, :, tc * 128:(tc + 1) * 128])

        @blk.vector
        def _(v):
            v.memset(onesf[:, :], 1.0)
            v.memset(ones128c[:, :], 1.0)
            p.inc(v.engine_nop(), "vm", 1)

        pRb = [bankA[:, 0:E], bankB[:, 0:E]]

        @blk.tensor
        def _(t):
            p.gwait(t, "ld")
            t.wait_ge(p.sems["vm"], 1)

            def cnt_mm(j):
                t.wait_ge(p.sems["dq"], 18 * (j + 1))
                ins = t.matmul(bankC[0:4, :E], ones128c[:, :], cntb[:, :],
                               start=(j == 0), stop=(j == NT - 1), skip_group_check=True)
                p.inc(ins, "mc", 1)

            for tc in range(NT):
                t.wait_ge(p.sems["lr0"], 16 * (tc + 1))
                W(t, "xp", tc - 1)
                for dc in range(DCH):
                    t.matmul(pRb[tc % 2], xrc[:, dc * 128:(dc + 1) * 128],
                             wrf[:, dc * E:(dc + 1) * E],
                             start=(dc == 0), stop=False, skip_group_check=True)
                ins = t.matmul(pRb[tc % 2], onesf[:, :], rbf[:, :], start=False, stop=True,
                               skip_group_check=True)
                p.inc(ins, "mm", 1)
                if tc >= 2:
                    cnt_mm(tc - 2)
            cnt_mm(NT - 2)
            cnt_mm(NT - 1)

        @blk.scalar
        def _(s):
            for tc in range(NT):
                s.wait_ge(p.sems["dq"], 18 * tc + 1)
                ins = s.activation(e16[:, :], pRb[tc % 2], AF.Exp, bias=nm[:, :],
                                   accum_out=se[:, :])
                p.inc(ins, "xp", 1)

        @blk.vector
        def _(v):
            for tc in range(NT):
                v.wait_ge(p.sems["mm"], tc + 1)
                W(v, "xp", tc)
                ins = v.reduce_max(nm[:, :], pRb[tc % 2], axis=AX, negate=True)
                p.inc(ins, "dq", 1)
                v.wait_ge(p.sems["xp"], tc + 1)
                v.reciprocal(den[:, :], se[:, :])
                v.drain()
                v.tensor_single_scalar(gs[:, :], e16[:, :], den[0:128, 0:1], ALU.mult)
                v.drain()
                v.reduce_max(m1v[:, :], gs[:, :], axis=AX)
                v.drain()
                v.tensor_single_scalar(msk1[:, tc * E:(tc + 1) * E], gs[:, :],
                                       m1v[0:128, 0:1], ALU.is_ge)
                v.drain()
                v.scalar_tensor_tensor(g2[:, :], msk1[:, tc * E:(tc + 1) * E], -1e30,
                                       gs[:, :], ALU.mult, ALU.add)
                v.drain()
                v.reduce_max(m2v[:, :], g2[:, :], axis=AX)
                v.drain()
                v.tensor_single_scalar(msk2[:, tc * E:(tc + 1) * E], g2[:, :],
                                       m2v[0:128, 0:1], ALU.is_ge)
                v.tensor_add(den[:, :], m1v[:, :], m2v[:, :])
                v.drain()
                v.tensor_scalar_add(den[:, :], den[:, :], 1e-9)
                v.drain()
                v.reciprocal(den[:, :], den[:, :])
                v.drain()
                v.tensor_single_scalar(w1v[:, tc:tc + 1], m1v[:, :], den[0:128, 0:1], ALU.mult)
                v.tensor_single_scalar(w2v[:, tc:tc + 1], m2v[:, :], den[0:128, 0:1], ALU.mult)
                v.drain()
                W(v, "mc", tc)
                ins = v.tensor_add(cntb[:, :], msk1[:, tc * E:(tc + 1) * E],
                                   msk2[:, tc * E:(tc + 1) * E])
                p.inc(ins, "dq", 17)

        @blk.vector
        def _(v):
            v.wait_ge(p.sems["mc"], NT)
            ins = v.tensor_copy(crow[:, :], bankC[0:1, 0:E])
            p.inc(ins, "dc", 1)

        @blk.sync
        def _(sy):
            sy.wait_ge(p.sems["dq"], 18 * NT)
            p.dma(sy, "so", m1o[:, :], msk1[:, :])
            p.dma(sy, "so", m2o[:, :], msk2[:, :])
            p.dma(sy, "so", w1o[:, :], w1v[:, :])
            p.dma(sy, "so", w2o[:, :], w2v[:, :])
            sy.wait_ge(p.sems["dc"], 1)
            p.dma(sy, "so", counts[:, :], crow[:, :])

        @blk.gpsimd
        def _(g):
            p.gwait(g, "so")
    return nc


def build_s_l1attn():
    p = P("s_l1attn")
    nc = p.nc
    H = 16
    NP = 8
    NKC = S // 128
    x2T = p.inp("x2T", [D, T])
    khat_in = p.inp("khat_in", [NP * 128, S], BF16)
    qhpT_in = p.inp("qhpT_in", [NP * 128, T], BF16)
    vsw_in = p.inp("vsw_in", [NP * 128, NKC * 130], BF16)
    knTo = p.inp("knTo", [H, T])
    knTb = p.inp("knTb", [H, S])
    W_out = p.inp("W_out", [D, D])
    b_out = p.inp("b_out", [D])
    x3T = p.out("x3T", [D, T])
    with p.es:
        khat = [p.sb("khat%d" % i, [128, S], BF16) for i in range(2)]
        qhp = [p.sb("qhp%d" % i, [128, T], BF16) for i in range(2)]
        vsw = [p.sb("vsw%d" % i, [128, NKC * 130], BF16) for i in range(2)]
        onrm = p.sb("onrm", [64, H * 2 * 512], BF16)
        woutb = p.sb("woutb", [64, H * D], BF16)
        b_outb = p.sb("b_outb", [1, D], BF16)
        onesq = p.sb("onesq", [1, 512], BF16)
        ones65 = p.sb("ones65", [65, 64], BF16)
        onesr128 = p.sb("onesr128", [1, 128], BF16)
        ones128c = p.sb("ones128c", [128, 4], BF16)
        ones1x128f = p.sb("ones1x128f", [1, 128], F32)
        identf = p.sb("identf", [16, 16], F32)
        knb_p = [p.sb("knb_p%d" % i, [H, T], BF16) for i in range(2)]
        kno_s = knb_p[0]
        qk5 = p.sb("qk5", [H, 5], F32)
        km16 = p.sb("km16", [H, 16], F32)
        t5 = p.sb("t5", [5, 16], F32)
        m5 = p.sb("m5", [5, 1], F32)
        r5 = p.sb("r5", [1, 5], F32)
        u4 = p.sb("u4", [1, 4], F32)
        negub = p.sb("negub", [1, 4], BF16)
        Ubc = p.sb("Ubc", [128, 4], F32)
        et = [p.sb("et%d" % i, [128, 1024], BF16) for i in range(3)]
        rs = p.sb("rs", [65, 512], F32)
        rsb = p.sb("rsb", [65, 512], BF16)
        pOs = p.sb("pOs", [64, 512], F32)
        onf = p.sb("onf", [64, 512], F32)
        xf = p.sb("xf", [128, DCH * 512], F32)
        ts = p.sb("ts", [128, DCH * 512], F32)
        tb = p.sb("tb", [128, 512], BF16)
        sqb = p.sb("sqb", [128, 512], BF16)
        x1c = [p.sb("x1c%d" % i, [128, 512], F32) for i in range(2)]
        _r = {n: p.sb(n, [1, 512], F32) for n in ("mrow", "m2row", "vrow")}
        rows = (_r["mrow"], _r["m2row"], _r["vrow"], _r["vrow"], _r["m2row"])
        tmpf = p.sb("tmpf", [128, 512], F32)
        bankA = p.ps("bankA", [128, 1024])
        bankB = p.ps("bankB", [128, 1024])
        bankC = p.ps("bankC", [128, 512])
        bankD = p.ps("bankD", [128, 512])
        bankE = p.ps("bankE", [128, 512])
        bankF = p.ps("bankF", [128, 512])
        for s_ in ("vm", "mm", "xp", "dq", "ms", "xe", "mv", "xo", "d6", "mb",
                   "mp", "dva", "xr", "mn"):
            p.sem(s_)
        p.dma_group("ld", 2)
        p.dma_group("lk", 2)
        p.dma_group("lt", 1)
        p.dma_group("lx", 1)
        p.dma_group("so", 2)
        blk = p.block()

        def W(eng, name, val):
            if val > 0:
                eng.wait_ge(p.sems[name], val)

        @blk.gpsimd
        def _(g):
            p.dma(g, "ld", woutb[:, :],
                  W_out.rearrange("(h p) n -> p h n", p=64))
            p.dma(g, "ld", b_outb[:, :], b_out[None, :])


        @blk.vector
        def _(v):
            for t_, val in [(onesq, 1.0), (ones65, 1.0), (identf, 1.0),
                            (onesr128, 1.0), (ones128c, 1.0), (ones1x128f, 1.0)]:
                v.memset(t_[:, :], val)
            p.inc(v.engine_nop(), "vm", 1)

        @blk.gpsimd
        def _(g):
            g.wait_ge(p.sems["vm"], 1)
            ins = g.affine_select(identf[:, :], identf[:, :], [[-1, 16]],
                                  ALU.is_equal, 0.0, channel_multiplier=1)
            p.inc(ins, "vm", 1)

        kn_toks = []

        @blk.gpsimd
        def _(g):
            kn_toks.append(p.dma(g, "lk", kno_s[:, :], knTo[:, :]))
            for r in range(4):
                g.wait_ge(p.sems["dq"], max(1, r))
                kn_toks.append(p.dma(g, "lk", knb_p[r % 2][:, :],
                                     knTb[:, r * T:(r + 1) * T]))

        @blk.vector
        def _(v):
            p.gwait(v, "ld")
            v.wait_ge(p.sems["vm"], 2)
            tok = kn_toks[0]
            v.wait_ge(p.sems[tok[0]], tok[1])
            for tt in range(NTT):
                v.reduce_max(qk5[:, tt:tt + 1], kno_s[:, tt * 512:(tt + 1) * 512],
                             axis=AX)
            ins = v.engine_nop()
            p.inc(ins, "dq", 1)
            for r in range(4):
                tok = kn_toks[r + 1]
                v.wait_ge(p.sems[tok[0]], tok[1])
                for i in range(4):
                    v.reduce_max(km16[:, 4 * r + i:4 * r + i + 1],
                                 knb_p[r % 2][:, i * 512:(i + 1) * 512], axis=AX)
                ins = v.engine_nop()
                p.inc(ins, "dq", 1)
            v.drain()
            ins = v.reduce_max(qk5[:, 4:5], km16[:, :], axis=AX)
            p.inc(ins, "dq", 1)

        @blk.tensor
        def _(t):
            t.wait_ge(p.sems["dq"], 6)
            ins = t.transpose(bankE[0:5, 0:16], qk5[:, :], identf[:, :])
            p.inc(ins, "mm", 1)

        @blk.vector
        def _(v):
            v.wait_ge(p.sems["mm"], 1)
            v.tensor_copy(t5[:, :], bankE[0:5, 0:16])
            v.drain()
            ins = v.reduce_max(m5[:, :], t5[:, :], axis=AX)
            p.inc(ins, "dq", 1)

        @blk.tensor
        def _(t):
            t.wait_ge(p.sems["dq"], 7)
            ins = t.transpose(bankF[0:1, 0:5], m5[:, :], identf[0:5, 0:5])
            p.inc(ins, "mm", 1)

        @blk.vector
        def _(v):
            v.wait_ge(p.sems["mm"], 2)
            ins = v.tensor_copy(r5[:, :], bankF[0:1, 0:5])
            p.inc(ins, "dq", 1)

        @blk.scalar
        def _(s):
            s.wait_ge(p.sems["dq"], 8)
            ins = s.activation(r5[:, :], r5[:, :], AF.Ln)
            p.inc(ins, "xp", 1)

        @blk.vector
        def _(v):
            v.wait_ge(p.sems["xp"], 1)
            ins = v.tensor_single_scalar(u4[:, :], r5[0:1, 0:4], r5[0:1, 4:5],
                                         ALU.add)
            p.inc(ins, "dq", 1)

        @blk.scalar
        def _(s):
            s.wait_ge(p.sems["dq"], 9)
            ins = s.activation(u4[:, :], u4[:, :], AF.Exp, scale=0.5)
            p.inc(ins, "xp", 1)

        @blk.vector
        def _(v):
            v.wait_ge(p.sems["xp"], 2)
            ins = v.tensor_scalar_mul(negub[:, :], u4[:, :], -SCALE1)
            p.inc(ins, "dq", 1)

        @blk.tensor
        def _(t):
            t.wait_ge(p.sems["dq"], 10)
            ins = t.matmul(bankE[:, 0:4], onesr128[:, :], negub[:, :],
                           start=True, stop=True)
            p.inc(ins, "mm", 1)

        @blk.vector
        def _(v):
            v.wait_ge(p.sems["mm"], 3)
            ins = v.tensor_copy(Ubc[:, :], bankE[:, 0:4])
            p.inc(ins, "dq", 1)
        dq_ub0 = p.c["dq"]

        pS = [bankA, bankB]
        pOA = bankC[0:65, :]
        pOB = bankD[0:65, :]

        for th in range(2):
            for ph in range(NP):
                u = th * NP + ph
                buf = u % 2

                @blk.sync
                def _(sy, th=th, ph=ph, u=u, buf=buf):
                    if u >= 2:
                        sy.wait_ge(p.sems["mv"], (u - 1) * 2 * NKC)
                    p.dma(sy, "lt", khat[buf][:, :],
                          khat_in[ph * 128:(ph + 1) * 128, :])
                    p.dma(sy, "lt", qhp[buf][:, :],
                          qhpT_in[ph * 128:(ph + 1) * 128, :])
                    p.dma(sy, "lt", vsw[buf][:, :],
                          vsw_in[ph * 128:(ph + 1) * 128, :])
                lt_done = p.c["lt0"]
                d6_snap = p.c["d6"]
                dva_snap = p.c["dva"]
                mp_snap = p.c["mp"]

                for tq in range(2):
                    tt = th * 2 + tq
                    msb = p.c["ms"]; xeb = p.c["xe"]; mvb = p.c["mv"]
                    xob = p.c["xo"]; d6b = p.c["d6"]; mbq = p.c["mb"]

                    @blk.tensor
                    def _(t, buf=buf, tt=tt, ph=ph, tq=tq, u=u, msb=msb, xeb=xeb,
                          mvb=mvb, xob=xob, d6b=d6b, lt_done=lt_done,
                          d6_snap=d6_snap, dva_snap=dva_snap, mp_snap=mp_snap):
                        t.wait_ge(p.sems["dq"], dq_ub0)
                        t.wait_ge(p.sems["lt0"], lt_done)
                        if tq == 0 and u > 0:
                            W(t, "d6", d6_snap)
                            W(t, "dva", dva_snap)
                            W(t, "mp", mp_snap)

                        def value_mms(kc):
                            t.wait_ge(p.sems["xe"], xeb + kc + 1)
                            t.matmul(pOA, vsw[buf][:, kc * 130:kc * 130 + 65],
                                     et[(xeb + kc) % 3][:, 0:512],
                                     start=(kc == 0), stop=(kc == NKC - 1))
                            ins = t.matmul(pOB, vsw[buf][:, kc * 130 + 65:kc * 130 + 130],
                                           et[(xeb + kc) % 3][:, 512:1024],
                                           start=(kc == 0), stop=(kc == NKC - 1))
                            p.inc(ins, "mv", 1)

                        for kc in range(NKC):
                            W(t, "xe", xeb + kc - 1)
                            if kc == 0:
                                W(t, "xo", xob)
                                W(t, "d6", d6b - 1)
                            t.matmul(pS[(xeb + kc) % 2][:, 0:512],
                                     khat[buf][0:64, kc * 128:(kc + 1) * 128],
                                     qhp[buf][0:64, tt * 512:(tt + 1) * 512],
                                     start=True, stop=True, tile_position=(0, 0))
                            ins = t.matmul(pS[(xeb + kc) % 2][:, 512:1024],
                                           khat[buf][64:128, kc * 128:(kc + 1) * 128],
                                           qhp[buf][64:128, tt * 512:(tt + 1) * 512],
                                           start=True, stop=True, tile_position=(64, 0))
                            p.inc(ins, "ms", 1)
                            if kc >= 2:
                                value_mms(kc - 2)
                        value_mms(NKC - 2)
                        value_mms(NKC - 1)
                        t.wait_ge(p.sems["d6"], d6b + 2)
                        ins = t.matmul(bankE[0:64, :], ones65[64:65, :], rsb[64:65, :],
                                       start=True, stop=True, tile_position=(64, 0))
                        p.inc(ins, "mb", 1)
                        t.wait_ge(p.sems["d6"], d6b + 4)
                        ins = t.matmul(bankF[0:64, :], ones65[64:65, :], rsb[64:65, :],
                                       start=True, stop=True, tile_position=(64, 0))
                        p.inc(ins, "mb", 1)

                    @blk.scalar
                    def _(s, tt=tt, ph=ph, tq=tq, msb=msb, xeb=xeb, mvb=mvb,
                          d6b=d6b, u=u):
                        for kc in range(NKC):
                            s.wait_ge(p.sems["ms"], msb + kc + 1)
                            W(s, "mv", mvb + kc - 2)
                            ins = s.activation(et[(xeb + kc) % 3][:, :],
                                               pS[(xeb + kc) % 2][:, :], AF.Exp,
                                               bias=Ubc[:, tt:tt + 1],
                                               scale=SCALE1)
                            p.inc(ins, "xe", 1)
                        s.wait_ge(p.sems["mv"], mvb + NKC)
                        W(s, "d6", d6b)
                        ins = s.copy(pOs[:, :], pOA[0:64, :])
                        p.inc(ins, "xo", 1)
                        s.wait_ge(p.sems["d6"], d6b + 3)
                        ins = s.copy(pOs[:, :], pOB[0:64, :])
                        p.inc(ins, "xo", 1)

                    @blk.vector
                    def _(v, tt=tt, ph=ph, tq=tq, mvb=mvb, xob=xob, d6b=d6b, mbq=mbq):
                        h0 = ph * 2
                        v.wait_ge(p.sems["mv"], mvb + NKC)
                        ins = v.reciprocal(rs[64:65, :], pOA[64:65, :])
                        p.inc(ins, "d6", 1)
                        v.drain()
                        ins = v.tensor_copy(rsb[64:65, :], rs[64:65, :])
                        p.inc(ins, "d6", 1)
                        v.wait_ge(p.sems["xo"], xob + 1)
                        v.wait_ge(p.sems["mb"], mbq + 1)
                        v.tensor_mul(onf[:, :], pOs[:, :], bankE[0:64, :])
                        v.drain()
                        ins = v.tensor_copy(onrm[:, (h0 * 2 + tq) * 512:(h0 * 2 + tq + 1) * 512],
                                            onf[:, :])
                        p.inc(ins, "d6", 1)
                        v.reciprocal(rs[64:65, :], pOB[64:65, :])
                        v.drain()
                        ins = v.tensor_copy(rsb[64:65, :], rs[64:65, :])
                        p.inc(ins, "d6", 1)
                        v.wait_ge(p.sems["xo"], xob + 2)
                        v.wait_ge(p.sems["mb"], mbq + 2)
                        v.tensor_mul(onf[:, :], pOs[:, :], bankF[0:64, :])
                        v.drain()
                        ins = v.tensor_copy(onrm[:, ((h0 + 1) * 2 + tq) * 512:((h0 + 1) * 2 + tq + 1) * 512],
                                            onf[:, :])
                        p.inc(ins, "d6", 1)

            d6_half = p.c["d6"]
            for tq in range(2):
                tt = th * 2 + tq

                @blk.sync
                def _(sy, tt=tt):
                    if tt > 0:
                        sy.wait_ge(p.sems["dva"], tt * 34)
                    p.dma(sy, "lx", _chunked_sbuf_ap(xf, DCH, 512),
                          x2T.rearrange("(c p) t -> p c t", p=128)[:, :, tt * 512:(tt + 1) * 512])
                lx_now = p.c["lx0"]

                pA = [bankA[:, 0:512], bankB[:, 0:512]]
                mpb = p.c["mp"]
                dvab = p.c["dva"]
                xe_now = p.c["xe"]

                @blk.tensor
                def _(t, th=th, tq=tq, tt=tt, mpb=mpb, dvab=dvab, d6_half=d6_half,
                      xe_now=xe_now, lx_now=lx_now):
                    if tt == 0:
                        p.gwait(t, "ld")
                    t.wait_ge(p.sems["d6"], d6_half)
                    t.wait_ge(p.sems["xe"], xe_now)
                    t.wait_ge(p.sems["lx0"], lx_now)

                    def stats(c):
                        t.wait_ge(p.sems["dva"], dvab + 3 * c + 3)
                        t.matmul(bankE[0:4, :], ones128c[:, :], tb[:, :],
                                 start=(c == 0), stop=(c == DCH - 1), skip_group_check=True)
                        ins = t.matmul(bankF[0:4, :], ones128c[:, :], sqb[:, :],
                                       start=(c == 0), stop=(c == DCH - 1), skip_group_check=True)
                        p.inc(ins, "mp", 1)

                    for oc in range(DCH):
                        if oc >= 2:
                            t.wait_ge(p.sems["dva"], dvab + 3 * (oc - 2) + 1)
                        for h in range(H):
                            t.matmul(pA[oc % 2], woutb[:, h * D + oc * 128:h * D + (oc + 1) * 128],
                                     onrm[:, (h * 2 + tq) * 512:(h * 2 + tq + 1) * 512],
                                     start=(h == 0), stop=False, skip_group_check=True)
                        ins = t.matmul(pA[oc % 2], b_outb[:, oc * 128:(oc + 1) * 128],
                                       onesq[:, :], start=False, stop=True,
                                       skip_group_check=True)
                        p.inc(ins, "mn", 1)
                        if oc >= 2:
                            stats(oc - 2)
                    stats(DCH - 2)
                    stats(DCH - 1)
                mn_prj = p.c["mn"]

                @blk.vector
                def _(v, tt=tt, mpb=mpb, mn_prj=mn_prj):
                    for oc in range(DCH):
                        v.wait_ge(p.sems["mn"], mn_prj - DCH + oc + 1)
                        W(v, "mp", mpb + oc)
                        ins = v.tensor_add(ts[:, oc * 512:(oc + 1) * 512],
                                           xf[:, oc * 512:(oc + 1) * 512], pA[oc % 2])
                        p.inc(ins, "dva", 1)
                        v.drain()
                        ins = v.tensor_copy(tb[:, :], ts[:, oc * 512:(oc + 1) * 512])
                        p.inc(ins, "dva", 1)
                        v.drain()
                        ins = v.tensor_mul(sqb[:, :], tb[:, :], tb[:, :])
                        p.inc(ins, "dva", 1)

                emit_ln_tail2(p, blk, dvab, mpb, tt, ts, tb, sqb, tmpf, x1c, rows,
                              ones128c, ones1x128f, bankE, bankF, x3T)

        @blk.gpsimd
        def _(g):
            p.gwait(g, "so")
    return nc


def build_s_l1ffn():
    p = P("s_l1ffn")
    nc = p.nc
    E = 16
    FC = 11
    NT = 16
    CAP = 2560.0
    x3T = p.inp("x3T", [D, T])
    m1i = p.inp("m1i", [128, NT * E])
    m2i = p.inp("m2i", [128, NT * E])
    w1i = p.inp("w1i", [128, NT])
    w2i = p.inp("w2i", [128, NT])
    counts_all = p.inp("counts_all", [8, E])
    core_sel = p.inp("core_sel", [8, 1])
    w_gate = p.inp("w_gate", [E, D, 1408])
    w_up = p.inp("w_up", [E, D, 1408])
    w_down = p.inp("w_down", [E, 1408, D])
    sw_gate = p.inp("sw_gate", [D, 1408])
    sw_up = p.inp("sw_up", [D, 1408])
    sw_down = p.inp("sw_down", [1408, D])
    yT = p.out("yT", [D, T])
    with p.es:
        x3Tb = p.sb("x3Tb", [128, DCH * T], BF16)
        m1s = p.sb("m1s", [128, NT * E], F32)
        m2s = p.sb("m2s", [128, NT * E], F32)
        w1s = p.sb("w1s", [128, NT], F32)
        w2s = p.sb("w2s", [128, NT], F32)
        ca = p.sb("ca", [8, E], F32)
        cse = p.sb("cse", [8, 1], F32)
        B8 = p.sb("B8", [8, 8], F32)
        m8 = p.sb("m8", [8, 1], F32)
        offsf = p.sb("offsf", [16, 1], F32)
        acc = p.sb("acc", [16, 1], F32)
        triL = p.sb("triL", [128, 128], BF16)
        identf = p.sb("identf", [128, 128], F32)
        identb = p.sb("identb", [128, 128], BF16)
        ri128 = p.sb("ri128", [128, 128], F32)
        ci128 = p.sb("ci128", [128, 128], F32)
        cnt2 = p.sb("cnt2", [128, E], BF16)
        grT = p.sb("grT", [16, 128], F32)
        t16 = p.sb("t16", [128, E], F32)
        win1 = p.sb("win1", [128, 1], F32)
        win2 = p.sb("win2", [128, 1], F32)
        gw1 = p.sb("gw1", [128, 1], F32)
        gw2 = p.sb("gw2", [128, 1], F32)
        G = p.sb("G", [128, NT * E], F32)
        Gb = p.sb("Gb", [128, E], BF16)
        GT = p.sb("GT", [16, NT * 128], BF16)
        grows = p.sb("grows", [1, T], BF16)
        onesr128 = p.sb("onesr128", [1, 128], BF16)
        ones128c = p.sb("ones128c", [128, 4], BF16)
        ones1x128f = p.sb("ones1x128f", [1, 128], F32)
        wgub = p.sb("wgub", [128, 2 * DCH * 1408], BF16)
        wdb = p.sb("wdb", [128, FC * D], BF16)
        sgE = p.sb("sgE", [128, 512], BF16)
        t1f = p.sb("t1f", [128, 512], F32)
        gateS = p.sb("gateS", [128, 512], F32)
        glg = p.sb("glg", [128, FC * 512], BF16)
        yaccT = p.sb("yaccT", [128, DCH * 1024], F32)
        xf = p.sb("xf", [128, DCH * 512], F32)
        ts = p.sb("ts", [128, DCH * 512], F32)
        tb = p.sb("tb", [128, 512], BF16)
        sqb = p.sb("sqb", [128, 512], BF16)
        x1c = [p.sb("x1c%d" % i, [128, 512], F32) for i in range(2)]
        _r = {n: p.sb(n, [1, 512], F32) for n in ("mrow", "m2row", "vrow")}
        rows = (_r["mrow"], _r["m2row"], _r["vrow"], _r["vrow"], _r["m2row"])
        tmpf = t1f
        bankA = p.ps("bankA", [128, 512])
        bankB = p.ps("bankB", [128, 512])
        bankC = p.ps("bankC", [128, 512])
        bankD = p.ps("bankD", [128, 512])
        bankE = p.ps("bankE", [128, 512])
        bankF = p.ps("bankF", [128, 512])
        bankG = p.ps("bankG", [128, 512])
        bankH = p.ps("bankH", [128, 128], BF16)
        for s_ in ("vm", "mm", "xp", "dq", "m1", "sg", "da", "m2", "dy",
                   "mp", "dva", "xr", "mn"):
            p.sem(s_)
        p.dma_group("ld", 4)
        p.dma_group("lw", 2)
        p.dma_group("lg", 1)
        p.dma_group("lx", 1)
        p.dma_group("so", 2)
        blk = p.block()

        def W(eng, name, val):
            if val > 0:
                eng.wait_ge(p.sems[name], val)

        @blk.gpsimd
        def _(g):
            p.dma(g, "ld", _chunked_sbuf_ap(x3Tb, DCH, T), _featmaj_ap(x3T, D, T))
            p.dma(g, "ld", m1s[:, :], m1i[:, :])
            p.dma(g, "ld", m2s[:, :], m2i[:, :])
            p.dma(g, "ld", w1s[:, :], w1i[:, :])
            p.dma(g, "ld", w2s[:, :], w2i[:, :])
            p.dma(g, "ld", ca[:, :], counts_all[:, :])
            p.dma(g, "ld", cse[:, :], core_sel[:, :])

        @blk.vector
        def _(v):
            v.memset(onesr128[:, :], 1.0)
            v.memset(ones128c[:, :], 1.0)
            v.memset(ones1x128f[:, :], 1.0)
            v.memset(triL[:, :], 1.0)
            v.memset(identf[:, :], 1.0)
            v.memset(identb[:, :], 1.0)
            v.memset(B8[:, :], 1.0)
            p.inc(v.engine_nop(), "vm", 1)

        @blk.gpsimd
        def _(g):
            g.wait_ge(p.sems["vm"], 1)
            g.iota(ri128[:, :], [[0, 128]], channel_multiplier=1,
                   allow_small_or_imprecise_dtypes=True)
            g.iota(ci128[:, :], [[1, 128]], channel_multiplier=0,
                   allow_small_or_imprecise_dtypes=True)
            g.affine_select(identf[:, :], identf[:, :], [[-1, 128]], ALU.is_equal, 0.0,
                            channel_multiplier=1)
            ins = g.affine_select(identb[:, :], identb[:, :], [[-1, 128]], ALU.is_equal, 0.0,
                                  channel_multiplier=1)
            p.inc(ins, "vm", 1)

        @blk.vector
        def _(v):
            v.wait_ge(p.sems["vm"], 2)
            v.tensor_tensor(triL[:, :], ri128[:, :], ci128[:, :], ALU.is_lt)
            ins = v.tensor_tensor(B8[:, :], ri128[0:8, 0:8], ci128[0:8, 0:8], ALU.is_gt)
            p.inc(ins, "vm", 1)

        @blk.tensor
        def _(t):
            p.gwait(t, "ld")
            t.wait_ge(p.sems["vm"], 3)
            ins = t.matmul(bankC[0:8, 0:1], B8[:, :], cse[:, :], start=True, stop=True)
            p.inc(ins, "mm", 1)

        @blk.vector
        def _(v):
            v.wait_ge(p.sems["mm"], 1)
            ins = v.tensor_copy(m8[:, :], bankC[0:8, 0:1])
            p.inc(ins, "dq", 1)

        @blk.tensor
        def _(t):
            t.wait_ge(p.sems["dq"], 1)
            ins = t.matmul(bankC[0:16, 0:1], ca[:, :], m8[:, :], start=True, stop=True)
            p.inc(ins, "mm", 1)

        @blk.vector
        def _(v):
            v.wait_ge(p.sems["mm"], 2)
            v.tensor_copy(offsf[:, :], bankC[0:16, 0:1])
            v.drain()
            ins = v.tensor_copy(acc[:, :], offsf[:, :])
            p.inc(ins, "dq", 1)

        for tc in range(NT):
            dqb = p.c["dq"]; mmb = p.c["mm"]

            @blk.vector
            def _(v, tc=tc, dqb=dqb, mmb=mmb):
                W(v, "mm", mmb - 2)
                ins = v.tensor_add(cnt2[:, :], m1s[:, tc * E:(tc + 1) * E],
                                   m2s[:, tc * E:(tc + 1) * E])
                p.inc(ins, "dq", 1)

            @blk.tensor
            def _(t, tc=tc, dqb=dqb, mmb=mmb):
                t.wait_ge(p.sems["dq"], dqb + 1)
                t.matmul(bankC[0:16, 0:128], cnt2[:, :], triL[:, :], start=True,
                         stop=True, skip_group_check=True)
                ins = t.matmul(bankD[0:16, 0:4], cnt2[:, :], ones128c[:, :],
                               start=True, stop=True, skip_group_check=True)
                p.inc(ins, "mm", 2)

            @blk.vector
            def _(v, tc=tc, dqb=dqb, mmb=mmb):
                v.wait_ge(p.sems["mm"], mmb + 2)
                v.tensor_single_scalar(grT[:, :], bankC[0:16, 0:128], acc[0:16, 0:1],
                                       ALU.add)
                v.drain()
                v.tensor_single_scalar(grT[:, :], grT[:, :], CAP, ALU.is_lt)
                v.drain()
                ins = v.tensor_add(acc[:, :], acc[:, :], bankD[0:16, 0:1])
                p.inc(ins, "dq", 1)

            @blk.tensor
            def _(t, tc=tc, dqb=dqb):
                t.wait_ge(p.sems["dq"], dqb + 2)
                ins = t.transpose(bankE[:, 0:16], grT[:, :], identf[0:16, 0:16])
                p.inc(ins, "mm", 1)

            @blk.vector
            def _(v, tc=tc, dqb=dqb, mmb=mmb):
                v.wait_ge(p.sems["mm"], mmb + 3)
                v.tensor_mul(t16[:, :], m1s[:, tc * E:(tc + 1) * E], bankE[:, 0:16])
                v.drain()
                v.reduce_sum(win1[:, :], t16[:, :], axis=AX)
                v.drain()
                v.tensor_mul(t16[:, :], m2s[:, tc * E:(tc + 1) * E], bankE[:, 0:16])
                v.drain()
                v.reduce_sum(win2[:, :], t16[:, :], axis=AX)
                v.drain()
                v.tensor_mul(gw1[:, :], w1s[:, tc:tc + 1], win1[:, :])
                v.tensor_mul(gw2[:, :], w2s[:, tc:tc + 1], win2[:, :])
                v.drain()
                v.tensor_single_scalar(G[:, tc * E:(tc + 1) * E],
                                       m1s[:, tc * E:(tc + 1) * E], gw1[0:128, 0:1],
                                       ALU.mult)
                v.drain()
                v.scalar_tensor_tensor(G[:, tc * E:(tc + 1) * E],
                                       m2s[:, tc * E:(tc + 1) * E], gw2[0:128, 0:1],
                                       G[:, tc * E:(tc + 1) * E], ALU.mult, ALU.add)
                v.drain()
                ins = v.tensor_copy(Gb[:, :], G[:, tc * E:(tc + 1) * E])
                p.inc(ins, "dq", 1)

            @blk.tensor
            def _(t, tc=tc, dqb=dqb, mmb=mmb):
                t.wait_ge(p.sems["dq"], dqb + 3)
                W(t, "xp", tc)
                ins = t.transpose(bankH[0:16, :], Gb[:, :], identb[:, :])
                p.inc(ins, "mm", 1)

            @blk.scalar
            def _(s, tc=tc, mmb=mmb):
                s.wait_ge(p.sems["mm"], mmb + 4)
                ins = s.copy(GT[:, tc * 128:(tc + 1) * 128], bankH[0:16, :])
                p.inc(ins, "xp", 1)
        dq_gates = p.c["dq"]; xp_gates = p.c["xp"]; mm_gates = p.c["mm"]

        for hf in range(2):
            for ei in range(17):
                e = ei - 1
                li = hf * 17 + ei
                m1b = p.c["m1"]; sgb_ = p.c["sg"]; dab = p.c["da"]
                m2b = p.c["m2"]; dyb = p.c["dy"]

                @blk.gpsimd
                def _(g, hf=hf, ei=ei, e=e, li=li, m1b=m1b, m2b=m2b):
                    if li == 0:
                        g.wait_ge(p.sems["vm"], 3)
                    if li >= 1:
                        g.wait_ge(p.sems["m1"], m1b)
                    if ei == 0:
                        p.dma(g, "lw", _chunked_sbuf_ap(wgub, 2 * DCH, 1408)[:, 0:DCH, :],
                              _featmaj_ap(sw_gate, D, 1408))
                        p.dma(g, "lw", _chunked_sbuf_ap(wgub, 2 * DCH, 1408)[:, DCH:2 * DCH, :],
                              _featmaj_ap(sw_up, D, 1408))
                    else:
                        p.dma(g, "lw", _chunked_sbuf_ap(wgub, 2 * DCH, 1408)[:, 0:DCH, :],
                              _featmaj_ap(w_gate[e, :, :], D, 1408))
                        p.dma(g, "lw", _chunked_sbuf_ap(wgub, 2 * DCH, 1408)[:, DCH:2 * DCH, :],
                              _featmaj_ap(w_up[e, :, :], D, 1408))
                        g.wait_ge(p.sems["xp"], xp_gates)
                        p.dma(g, "lg", grows[:, :], GT[e:e + 1, :])
                    if li >= 1:
                        g.wait_ge(p.sems["m2"], m2b)
                    if ei == 0:
                        p.dma(g, "lw", _chunked_sbuf_ap(wdb, FC, D),
                              _featmaj_ap(sw_down, 1408, D))
                    else:
                        p.dma(g, "lw", _chunked_sbuf_ap(wdb, FC, D),
                              _featmaj_ap(w_down[e, :, :], 1408, D))
                lw_done = p.c["lw0"] + p.c["lw1"]
                lw0_now = p.c["lw0"]; lw1_now = p.c["lw1"]; lg_now = p.c["lg0"]

                for tq in range(2):
                    tt = hf * 2 + tq
                    m1c = p.c["m1"]; sgc = p.c["sg"]; dac = p.c["da"]
                    m2q = p.c["m2"]
                    dva_snap = p.c["dva"]

                    @blk.tensor
                    def _(t, hf=hf, ei=ei, tq=tq, tt=tt, li=li, m1c=m1c, sgc=sgc,
                          dac=dac, lw0_now=lw0_now, lw1_now=lw1_now, lg_now=lg_now):
                        off = 1 if ei > 0 else 0
                        t.wait_ge(p.sems["lw0"], lw0_now)
                        t.wait_ge(p.sems["lw1"], lw1_now)
                        if li == 0 and tq == 0:
                            p.gwait(t, "ld")
                            W(t, "dq", p.c["dq"])
                        if ei > 0:
                            t.wait_ge(p.sems["lg0"], lg_now)
                            W(t, "da", dac)
                            ins = t.matmul(bankC[:, :], onesr128[:, :],
                                           grows[:, tt * 512:(tt + 1) * 512],
                                           start=True, stop=True)
                            p.inc(ins, "m1", 1)
                        for fc in range(FC):
                            W(t, "sg", sgc + fc)
                            for dc in range(DCH):
                                ins = t.matmul(bankA[:, :],
                                               wgub[:, dc * 1408 + fc * 128:dc * 1408 + (fc + 1) * 128],
                                               x3Tb[:, dc * T + tt * 512:dc * T + (tt + 1) * 512],
                                               start=(dc == 0), stop=(dc == DCH - 1))
                            p.inc(ins, "m1", 1)
                            W(t, "da", dac + off + 2 * fc - 1)
                            for dc in range(DCH):
                                ins = t.matmul(bankB[:, :],
                                               wgub[:, (DCH + dc) * 1408 + fc * 128:(DCH + dc) * 1408 + (fc + 1) * 128],
                                               x3Tb[:, dc * T + tt * 512:dc * T + (tt + 1) * 512],
                                               start=(dc == 0), stop=(dc == DCH - 1))
                            p.inc(ins, "m1", 1)

                    @blk.scalar
                    def _(s, ei=ei, li=li, m1c=m1c, sgc=sgc, dac=dac):
                        off = 1 if ei > 0 else 0
                        for fc in range(FC):
                            s.wait_ge(p.sems["m1"], m1c + off + 2 * fc + 1)
                            W(s, "da", dac + off + 2 * fc - 1)
                            ins = s.activation(sgE[:, :], bankA[:, :], AF.Silu)
                            p.inc(ins, "sg", 1)

                    @blk.vector
                    def _(v, ei=ei, tq=tq, li=li, m1c=m1c, sgc=sgc, dac=dac, m2q=m2q):
                        off = 1 if ei > 0 else 0
                        W(v, "m2", m2q)
                        if ei > 0:
                            v.wait_ge(p.sems["m1"], m1c + 1)
                            ins = v.tensor_copy(gateS[:, :], bankC[:, :])
                            p.inc(ins, "da", 1)
                        for fc in range(FC):
                            v.wait_ge(p.sems["m1"], m1c + off + 2 * fc + 2)
                            v.wait_ge(p.sems["sg"], sgc + fc + 1)
                            v.tensor_mul(t1f[:, :], bankB[:, :], sgE[:, :])
                            p.inc(v.engine_nop(), "da", 1)
                            v.drain()
                            if ei > 0:
                                v.tensor_mul(t1f[:, :], t1f[:, :], gateS[:, :])
                                v.drain()
                            ins = v.tensor_copy(glg[:, fc * 512:(fc + 1) * 512], t1f[:, :])
                            p.inc(ins, "da", 1)
                            v.drain()

                    da_glg = p.c["da"]
                    m2c = p.c["m2"]; dyc = p.c["dy"]

                    @blk.tensor
                    def _(t, ei=ei, li=li, tq=tq, da_glg=da_glg, m2c=m2c, dyc=dyc):
                        t.wait_ge(p.sems["da"], da_glg)
                        for oc in range(DCH):
                            W(t, "dy", dyc + oc - 1)
                            for fc in range(FC):
                                ins = t.matmul(bankD[:, :] if oc % 2 == 0 else bankE[:, :],
                                               wdb[:, fc * D + oc * 128:fc * D + (oc + 1) * 128],
                                               glg[:, fc * 512:(fc + 1) * 512],
                                               start=(fc == 0), stop=(fc == FC - 1))
                            p.inc(ins, "m2", 1)

                    @blk.vector
                    def _(v, ei=ei, li=li, tq=tq, m2c=m2c, dyc=dyc, dva_snap=dva_snap):
                        if ei == 0:
                            W(v, "dva", dva_snap)
                        for oc in range(DCH):
                            v.wait_ge(p.sems["m2"], m2c + oc + 1)
                            src = bankD[:, :] if oc % 2 == 0 else bankE[:, :]
                            dst = yaccT[:, oc * 1024 + tq * 512:oc * 1024 + (tq + 1) * 512]
                            if ei == 0:
                                ins = v.tensor_copy(dst, src)
                            else:
                                ins = v.tensor_add(dst, dst, src)
                            p.inc(ins, "dy", 1)
                            v.drain()

            dy_half = p.c["dy"]
            for tq in range(2):
                tt = hf * 2 + tq

                @blk.sync
                def _(sy, tt=tt):
                    if tt > 0:
                        sy.wait_ge(p.sems["dva"], tt * 34)
                    p.dma(sy, "lx", _chunked_sbuf_ap(xf, DCH, 512),
                          x3T.rearrange("(c p) t -> p c t", p=128)[:, :, tt * 512:(tt + 1) * 512])
                lx_now = p.c["lx0"]

                mpb = p.c["mp"]
                dvab = p.c["dva"]

                @blk.vector
                def _(v, hf=hf, tq=tq, tt=tt, mpb=mpb, dvab=dvab, dy_half=dy_half,
                      lx_now=lx_now):
                    v.wait_ge(p.sems["dy"], dy_half)
                    v.wait_ge(p.sems["lx0"], lx_now)
                    for oc in range(DCH):
                        W(v, "mp", mpb + oc)
                        v.scalar_tensor_tensor(ts[:, oc * 512:(oc + 1) * 512],
                                               xf[:, oc * 512:(oc + 1) * 512], 2.0,
                                               yaccT[:, oc * 1024 + tq * 512:oc * 1024 + (tq + 1) * 512],
                                               ALU.mult, ALU.add)
                        p.inc(v.engine_nop(), "dva", 1)
                        v.drain()
                        ins = v.tensor_copy(tb[:, :], ts[:, oc * 512:(oc + 1) * 512])
                        p.inc(ins, "dva", 1)
                        v.drain()
                        ins = v.tensor_mul(sqb[:, :], tb[:, :], tb[:, :])
                        p.inc(ins, "dva", 1)

                @blk.tensor
                def _(t, tt=tt, mpb=mpb, dvab=dvab):
                    for c in range(DCH):
                        t.wait_ge(p.sems["dva"], dvab + 3 * c + 3)
                        t.matmul(bankF[0:4, :], ones128c[:, :], tb[:, :],
                                 start=(c == 0), stop=(c == DCH - 1), skip_group_check=True)
                        ins = t.matmul(bankG[0:4, :], ones128c[:, :], sqb[:, :],
                                       start=(c == 0), stop=(c == DCH - 1), skip_group_check=True)
                        p.inc(ins, "mp", 1)

                emit_ln_tail2(p, blk, dvab, mpb, tt, ts, tb, sqb, tmpf, x1c, rows,
                              ones128c, ones1x128f, bankF, bankG, yT)

        @blk.gpsimd
        def _(g):
            p.gwait(g, "so")
    return nc



L1SEG = 320
L1NSL = 16 * L1SEG


def build_s_l1slot():
    p = P("s_l1slot")
    nc = p.nc
    E = 16
    NT = 16
    CAP = 2560.0
    m1i = p.inp("m1i", [128, NT * E])
    m2i = p.inp("m2i", [128, NT * E])
    counts_all = p.inp("counts_all", [8, E])
    core_sel = p.inp("core_sel", [8, 1])
    dst1o = p.out("dst1o", [128, NT])
    dst2o = p.out("dst2o", [128, NT])
    with p.es:
        m1s = p.sb("m1s", [128, NT * E], F32)
        m2s = p.sb("m2s", [128, NT * E], F32)
        ca = p.sb("ca", [8, E], F32)
        cse = p.sb("cse", [8, 1], F32)
        B8 = p.sb("B8", [8, 8], F32)
        m8 = p.sb("m8", [8, 1], F32)
        offsf = p.sb("offsf", [16, 1], F32)
        acc = p.sb("acc", [16, 1], F32)
        triL = p.sb("triL", [128, 128], BF16)
        identf = p.sb("identf", [128, 128], F32)
        ri128 = p.sb("ri128", [128, 128], F32)
        ci128 = p.sb("ci128", [128, 128], F32)
        ones128c = p.sb("ones128c", [128, 4], BF16)
        cnt2 = p.sb("cnt2", [128, E], BF16)
        grT = p.sb("grT", [16, 128], F32)
        t16 = p.sb("t16", [128, E], F32)
        tm = p.sb("tm", [128, E], F32)
        rka = [p.sb("rka%d" % k, [128, NT], F32) for k in range(2)]
        eka = [p.sb("eka%d" % k, [128, NT], F32) for k in range(2)]
        qk = p.sb("qk", [128, NT], F32)
        q2 = p.sb("q2", [128, NT], F32)
        kp = p.sb("kp", [128, NT], F32)
        d1s = p.sb("d1s", [128, NT], F32)
        d2s = p.sb("d2s", [128, NT], F32)
        bankC = p.ps("bankC", [128, 512])
        bankD = p.ps("bankD", [128, 512])
        bankE = p.ps("bankE", [128, 512])
        for s_ in ("vm", "mm", "dq"):
            p.sem(s_)
        p.dma_group("ld", 2)
        p.dma_group("so", 2)
        blk = p.block()

        def W(eng, name, val):
            if val > 0:
                eng.wait_ge(p.sems[name], val)

        @blk.gpsimd
        def _(g):
            p.dma(g, "ld", m1s[:, :], m1i[:, :])
            p.dma(g, "ld", m2s[:, :], m2i[:, :])
            p.dma(g, "ld", ca[:, :], counts_all[:, :])
            p.dma(g, "ld", cse[:, :], core_sel[:, :])

        @blk.vector
        def _(v):
            v.memset(triL[:, :], 1.0)
            v.memset(identf[:, :], 1.0)
            v.memset(B8[:, :], 1.0)
            v.memset(ones128c[:, :], 1.0)
            p.inc(v.engine_nop(), "vm", 1)

        @blk.gpsimd
        def _(g):
            g.wait_ge(p.sems["vm"], 1)
            g.iota(ri128[:, :], [[0, 128]], channel_multiplier=1,
                   allow_small_or_imprecise_dtypes=True)
            g.iota(ci128[:, :], [[1, 128]], channel_multiplier=0,
                   allow_small_or_imprecise_dtypes=True)
            ins = g.affine_select(identf[:, :], identf[:, :], [[-1, 128]],
                                  ALU.is_equal, 0.0, channel_multiplier=1)
            p.inc(ins, "vm", 1)

        @blk.vector
        def _(v):
            v.wait_ge(p.sems["vm"], 2)
            v.tensor_tensor(triL[:, :], ri128[:, :], ci128[:, :], ALU.is_lt)
            ins = v.tensor_tensor(B8[:, :], ri128[0:8, 0:8], ci128[0:8, 0:8],
                                  ALU.is_gt)
            p.inc(ins, "vm", 1)

        @blk.tensor
        def _(t):
            p.gwait(t, "ld")
            t.wait_ge(p.sems["vm"], 3)
            ins = t.matmul(bankC[0:8, 0:1], B8[:, :], cse[:, :], start=True, stop=True)
            p.inc(ins, "mm", 1)

        @blk.vector
        def _(v):
            v.wait_ge(p.sems["mm"], 1)
            ins = v.tensor_copy(m8[:, :], bankC[0:8, 0:1])
            p.inc(ins, "dq", 1)

        @blk.tensor
        def _(t):
            t.wait_ge(p.sems["dq"], 1)
            ins = t.matmul(bankC[0:16, 0:1], ca[:, :], m8[:, :], start=True, stop=True)
            p.inc(ins, "mm", 1)

        @blk.vector
        def _(v):
            v.wait_ge(p.sems["mm"], 2)
            v.tensor_copy(offsf[:, :], bankC[0:16, 0:1])
            v.drain()
            ins = v.tensor_copy(acc[:, :], offsf[:, :])
            p.inc(ins, "dq", 1)

        for tc in range(NT):
            dqb = p.c["dq"]; mmb = p.c["mm"]

            @blk.vector
            def _(v, tc=tc, mmb=mmb):
                W(v, "mm", mmb - 1)
                ins = v.tensor_add(cnt2[:, :], m1s[:, tc * E:(tc + 1) * E],
                                   m2s[:, tc * E:(tc + 1) * E])
                p.inc(ins, "dq", 1)

            @blk.tensor
            def _(t, dqb=dqb):
                t.wait_ge(p.sems["dq"], dqb + 1)
                t.matmul(bankC[0:16, 0:128], cnt2[:, :], triL[:, :], start=True,
                         stop=True, skip_group_check=True)
                ins = t.matmul(bankD[0:16, 0:4], cnt2[:, :], ones128c[:, :],
                               start=True, stop=True, skip_group_check=True)
                p.inc(ins, "mm", 2)

            @blk.vector
            def _(v, dqb=dqb, mmb=mmb):
                v.wait_ge(p.sems["mm"], mmb + 2)
                v.tensor_single_scalar(grT[:, :], bankC[0:16, 0:128], acc[0:16, 0:1],
                                       ALU.add)
                v.drain()
                ins = v.tensor_add(acc[:, :], acc[:, :], bankD[0:16, 0:1])
                p.inc(ins, "dq", 1)

            @blk.tensor
            def _(t, dqb=dqb):
                t.wait_ge(p.sems["dq"], dqb + 2)
                ins = t.transpose(bankE[:, 0:16], grT[:, :], identf[0:16, 0:16])
                p.inc(ins, "mm", 1)

            @blk.vector
            def _(v, tc=tc, mmb=mmb):
                v.wait_ge(p.sems["mm"], mmb + 3)
                v.tensor_copy(t16[:, :], bankE[:, 0:16])
                v.drain()
                for k, mk in ((0, m1s), (1, m2s)):
                    v.tensor_mul(tm[:, :], mk[:, tc * E:(tc + 1) * E], t16[:, :])
                    v.drain()
                    v.reduce_sum(rka[k][:, tc:tc + 1], tm[:, :], axis=AX)
                    v.drain()
                    v.tensor_mul(tm[:, :], mk[:, tc * E:(tc + 1) * E], ci128[:, 0:E])
                    v.drain()
                    v.reduce_sum(eka[k][:, tc:tc + 1], tm[:, :], axis=AX)
                    v.drain()
                p.inc(v.engine_nop(), "dq", 1)

        @blk.vector
        def _(v):
            v.wait_ge(p.sems["dq"], p.c["dq"])
            for k, dXs in ((0, d1s), (1, d2s)):
                rk, ek = rka[k], eka[k]
                v.tensor_single_scalar(qk[:, :], rk[:, :], L1SEG - 0.5, ALU.is_gt)
                v.tensor_single_scalar(kp[:, :], rk[:, :], CAP - 0.5, ALU.is_lt)
                v.drain()
                for j in range(2, 8):
                    v.tensor_single_scalar(q2[:, :], rk[:, :], j * L1SEG - 0.5,
                                           ALU.is_gt)
                    v.drain()
                    v.tensor_add(qk[:, :], qk[:, :], q2[:, :])
                    v.drain()
                v.scalar_tensor_tensor(ek[:, :], ek[:, :], float(L1SEG), rk[:, :],
                                       ALU.mult, ALU.add)
                v.drain()
                v.scalar_tensor_tensor(ek[:, :], qk[:, :], float(15 * L1SEG),
                                       ek[:, :], ALU.mult, ALU.add)
                v.drain()
                v.tensor_scalar_add(ek[:, :], ek[:, :], 1.0)
                v.drain()
                v.tensor_mul(ek[:, :], ek[:, :], kp[:, :])
                v.drain()
                v.tensor_scalar_add(dXs[:, :], ek[:, :], -1.0)
                v.drain()
            p.inc(v.engine_nop(), "dq", 1)
        dq_all = p.c["dq"]

        @blk.sync
        def _(sy):
            sy.wait_ge(p.sems["dq"], dq_all)
            p.dma(sy, "so", dst1o[:, :], d1s[:, :])
            p.dma(sy, "so", dst2o[:, :], d2s[:, :])

        @blk.gpsimd
        def _(g):
            p.gwait(g, "so")
    return nc


def build_s_l1ffn_r():
    p = P("s_l1ffn_r")
    nc = p.nc
    E = 16
    FC = 11
    x3T = p.inp("x3T", [D, T])
    xgi = p.inp("xgi", [D, L1NSL])
    gsl = p.inp("gsl", [1, L1NSL])
    w_gate = p.inp("w_gate", [E, D, 1408])
    w_up = p.inp("w_up", [E, D, 1408])
    w_down = p.inp("w_down", [E, 1408, D])
    sw_gate = p.inp("sw_gate", [D, 1408])
    sw_up = p.inp("sw_up", [D, 1408])
    sw_down = p.inp("sw_down", [1408, D])
    ygT = p.out("ygT", [D, L1NSL], BF16)
    yshT = p.out("yshT", [D, T], BF16)
    with p.es:
        x3Tb = p.sb("x3Tb", [128, DCH * T], BF16)
        xgb = [p.sb("xgb%d" % i, [128, DCH * L1SEG], BF16) for i in range(2)]
        gsr = [p.sb("gsr%d" % i, [1, L1SEG], BF16) for i in range(2)]
        gateS = p.sb("gateS", [128, L1SEG], F32)
        wgp = [p.sb("wgp%d" % i, [128, 2 * DCH * 3 * 128], BF16) for i in range(2)]
        wdb = p.sb("wdb", [128, FC * D], BF16)
        sgE = p.sb("sgE", [128, 512], BF16)
        t1f = p.sb("t1f", [128, 512], F32)
        glg = p.sb("glg", [128, FC * 512], BF16)
        ygb = [p.sb("ygb%d" % i, [128, DCH * 512], BF16) for i in range(2)]
        onesr128 = p.sb("onesr128", [1, 128], BF16)
        bankA = p.ps("bankA", [128, 512])
        bankB = p.ps("bankB", [128, 512])
        bankD = p.ps("bankD", [128, 512])
        bankE = p.ps("bankE", [128, 512])
        bankG = p.ps("bankG", [128, 512])
        for s_ in ("vm", "m1", "sg", "da", "m2", "dy"):
            p.sem(s_)
        p.dma_group("lw", 2)
        p.dma_group("lt", 2)
        p.dma_group("lg", 2)
        p.dma_group("lx", 1)
        p.dma_group("so", 2)
        blk = p.block()

        def W(eng, name, val):
            if val > 0:
                eng.wait_ge(p.sems[name], val)

        @blk.gpsimd
        def _(g):
            p.dma(g, "lx", _chunked_sbuf_ap(x3Tb, DCH, T), _featmaj_ap(x3T, D, T))

        @blk.vector
        def _(v):
            v.memset(onesr128[:, :], 1.0)
            p.inc(v.engine_nop(), "vm", 1)

        units = [(-1, tt) for tt in range(NTT)] + [(e, 0) for e in range(E)]
        xg_guard = [0, 0]
        yg_tok = [None, None]
        wg_guard = [0, 0]
        piece_lw = {}
        pc_idx = [0]
        wd_lw = [(0, 0)]
        for ui, (e, tt) in enumerate(units):
            buf = ui % 2
            nco = L1SEG if e >= 0 else 512
            m1b = p.c["m1"]; m2b = p.c["m2"]; dab = p.c["da"]
            sgb_ = p.c["sg"]; dyb = p.c["dy"]
            off = 1 if e >= 0 else 0

            wg_src = sw_gate if e < 0 else w_gate[e, :, :]
            wu_src = sw_up if e < 0 else w_up[e, :, :]
            wd_src = sw_down if e < 0 else w_down[e, :, :]
            pieces = [(0, 3), (3, 3), (6, 3), (9, 2)]

            if e >= 0:
                @blk.gpsimd
                def _(g, e=e, buf=buf, xgg=xg_guard[buf]):
                    W(g, "m1", xgg)
                    p.dma(g, "lt", _chunked_sbuf_ap(xgb[buf], DCH, L1SEG),
                          _featmaj_ap(xgi, D, L1NSL)[:, :, e * L1SEG:(e + 1) * L1SEG])
                    p.dma(g, "lg", gsr[buf][:, :], gsl[:, e * L1SEG:(e + 1) * L1SEG])
            lt0_now = p.c["lt0"]; lt1_now = p.c["lt1"]
            lg0_now = p.c["lg0"]; lg1_now = p.c["lg1"]

            if (e < 0 and tt == 0) or e >= 0:
                @blk.gpsimd
                def _(g, m2b=m2b):
                    W(g, "m2", m2b)
                    p.dma(g, "lw", _chunked_sbuf_ap(wdb, FC, D),
                          _featmaj_ap(wd_src, 1408, D))
                wd_lw[0] = (p.c["lw0"], p.c["lw1"])
            if True:
                for pi, (f0, nf) in enumerate(pieces):
                    pb = (pc_idx[0] + pi) % 2
                    if pi < 2:
                        gd = wg_guard[pb]
                    else:
                        gd = p.c["m1"] + off + 6 * (pi - 1)

                    @blk.gpsimd
                    def _(g, pb=pb, f0=f0, nf=nf, gd=gd):
                        W(g, "m1", gd)
                        p.dma(g, "lw",
                              bass.AP(wgp[pb], 0, [[2 * DCH * 384, 128], [384, DCH], [1, nf * 128]]),
                              _featmaj_ap(wg_src, D, 1408)[:, :, f0 * 128:(f0 + nf) * 128])
                        p.dma(g, "lw",
                              bass.AP(wgp[pb], DCH * 384, [[2 * DCH * 384, 128], [384, DCH], [1, nf * 128]]),
                              _featmaj_ap(wu_src, D, 1408)[:, :, f0 * 128:(f0 + nf) * 128])
                    piece_lw[(ui, pi)] = (p.c["lw0"], p.c["lw1"])

            pcb = pc_idx[0]
            wg_done = wg_guard

            @blk.tensor
            def _(t, e=e, tt=tt, buf=buf, ui=ui, nco=nco, m1b=m1b, dab=dab,
                  sgb_=sgb_, off=off, pcb=pcb, wg_done=wg_done,
                  lt0_now=lt0_now, lt1_now=lt1_now, lg0_now=lg0_now,
                  lg1_now=lg1_now):
                if ui == 0:
                    p.gwait(t, "lx")
                    t.wait_ge(p.sems["vm"], 1)
                if e >= 0:
                    W(t, "lt0", lt0_now)
                    W(t, "lt1", lt1_now)
                    W(t, "lg0", lg0_now)
                    W(t, "lg1", lg1_now)
                    W(t, "da", dab)
                    ins = t.matmul(bankG[:, 0:L1SEG], onesr128[:, :], gsr[buf][:, :],
                                   start=True, stop=True)
                    p.inc(ins, "m1", 1)

                def mv(dc):
                    if e >= 0:
                        return xgb[buf][:, dc * L1SEG:(dc + 1) * L1SEG]
                    return x3Tb[:, dc * T + tt * 512:dc * T + (tt + 1) * 512]

                for fc in range(FC):
                    pi = min(fc // 3, 3)
                    pb = (pcb + pi) % 2
                    fo = fc - [0, 3, 6, 9][pi]
                    lw = piece_lw[(ui, pi)]
                    W(t, "lw0", lw[0])
                    W(t, "lw1", lw[1])
                    W(t, "sg", sgb_ + fc)
                    for dc in range(DCH):
                        ins = t.matmul(bankA[:, 0:nco],
                                       wgp[pb][:, dc * 384 + fo * 128:dc * 384 + (fo + 1) * 128],
                                       mv(dc), start=(dc == 0), stop=(dc == DCH - 1))
                    p.inc(ins, "m1", 1)
                    W(t, "da", dab + off + fc)
                    for dc in range(DCH):
                        ins = t.matmul(bankB[:, 0:nco],
                                       wgp[pb][:, (DCH + dc) * 384 + fo * 128:(DCH + dc) * 384 + (fo + 1) * 128],
                                       mv(dc), start=(dc == 0), stop=(dc == DCH - 1))
                    p.inc(ins, "m1", 1)
                    if fc in (2, 5, 8, 10):
                        wg_done[pb] = p.c["m1"]
            xg_guard[buf] = p.c["m1"]
            pc_idx[0] += 4

            @blk.scalar
            def _(s, e=e, nco=nco, m1b=m1b, dab=dab, sgb_=sgb_, off=off):
                for fc in range(FC):
                    s.wait_ge(p.sems["m1"], m1b + off + 2 * fc + 1)
                    W(s, "da", dab + off + fc)
                    ins = s.activation(sgE[:, 0:nco], bankA[:, 0:nco], AF.Silu)
                    p.inc(ins, "sg", 1)

            @blk.vector
            def _(v, e=e, nco=nco, m1b=m1b, dab=dab, sgb_=sgb_, m2b=m2b, off=off):
                W(v, "m2", m2b)
                if e >= 0:
                    v.wait_ge(p.sems["m1"], m1b + 1)
                    ins = v.tensor_copy(gateS[:, :], bankG[:, 0:L1SEG])
                    p.inc(ins, "da", 1)
                    v.drain()
                for fc in range(FC):
                    v.wait_ge(p.sems["m1"], m1b + off + 2 * fc + 2)
                    v.wait_ge(p.sems["sg"], sgb_ + fc + 1)
                    v.tensor_mul(t1f[:, 0:nco], bankB[:, 0:nco], sgE[:, 0:nco])
                    v.drain()
                    if e >= 0:
                        v.tensor_mul(t1f[:, 0:nco], t1f[:, 0:nco], gateS[:, :])
                        v.drain()
                    ins = v.tensor_copy(glg[:, fc * 512:fc * 512 + nco], t1f[:, 0:nco])
                    p.inc(ins, "da", 1)
                    v.drain()
            da_glg = p.c["da"]
            m2c = p.c["m2"]; dyc = p.c["dy"]

            @blk.tensor
            def _(t, e=e, nco=nco, da_glg=da_glg, dyc=dyc, wlw=wd_lw[0]):
                W(t, "lw0", wlw[0])
                W(t, "lw1", wlw[1])
                t.wait_ge(p.sems["da"], da_glg)
                for oc in range(DCH):
                    W(t, "dy", dyc + oc - 1)
                    for fc in range(FC):
                        pa = bankD if oc % 2 == 0 else bankE
                        ins = t.matmul(pa[:, 0:nco],
                                       wdb[:, fc * D + oc * 128:fc * D + (oc + 1) * 128],
                                       glg[:, fc * 512:fc * 512 + nco],
                                       start=(fc == 0), stop=(fc == FC - 1))
                    p.inc(ins, "m2", 1)

            @blk.scalar
            def _(s, e=e, nco=nco, buf=buf, m2c=m2c, dyc=dyc, tok=yg_tok[buf]):
                for oc in range(DCH):
                    s.wait_ge(p.sems["m2"], m2c + oc + 1)
                    if oc == 0 and tok is not None:
                        s.wait_ge(p.sems[tok[0]], tok[1])
                    pa = bankD if oc % 2 == 0 else bankE
                    ins = s.copy(ygb[buf][:, oc * 512:oc * 512 + nco], pa[:, 0:nco])
                    p.inc(ins, "dy", 1)
            dy_done = p.c["dy"]

            @blk.sync
            def _(sy, e=e, tt=tt, buf=buf, dy_done=dy_done):
                sy.wait_ge(p.sems["dy"], dy_done)
                if e >= 0:
                    tok = p.dma(sy, "so",
                                ygT.rearrange("(c p) s -> p c s", p=128)[:, :, e * L1SEG:(e + 1) * L1SEG],
                                bass.AP(ygb[buf], 0, [[DCH * 512, 128], [512, DCH], [1, L1SEG]]))
                else:
                    tok = p.dma(sy, "so",
                                yshT.rearrange("(c p) s -> p c s", p=128)[:, :, tt * 512:(tt + 1) * 512],
                                _chunked_sbuf_ap(ygb[buf], DCH, 512))
                yg_tok[buf] = tok

        @blk.gpsimd
        def _(g):
            p.gwait(g, "so")
    return nc


def build_s_l1comb():
    p = P("s_l1comb")
    nc = p.nc
    x3T = p.inp("x3T", [D, T])
    yshT = p.inp("yshT", [D, T], BF16)
    y1T = p.inp("y1T", [D, T], BF16)
    y2T = p.inp("y2T", [D, T], BF16)
    yT = p.out("yT", [D, T])
    with p.es:
        xf = p.sb("xf", [128, DCH * 512], F32)
        ysc = p.sb("ysc", [128, DCH * 512], BF16)
        y1c = p.sb("y1c", [128, DCH * 512], BF16)
        y2c = p.sb("y2c", [128, DCH * 512], BF16)
        ts = p.sb("ts", [128, DCH * 512], F32)
        tb = p.sb("tb", [128, 512], BF16)
        sqb = p.sb("sqb", [128, 512], BF16)
        tmpf = p.sb("tmpf", [128, 512], F32)
        x1c = [p.sb("x1c%d" % i, [128, 512], F32) for i in range(2)]
        _r = {n: p.sb(n, [1, 512], F32) for n in ("mrow", "m2row", "vrow")}
        rows = (_r["mrow"], _r["m2row"], _r["vrow"], _r["vrow"], _r["m2row"])
        ones128c = p.sb("ones128c", [128, 4], BF16)
        ones1x128f = p.sb("ones1x128f", [1, 128], F32)
        bankE = p.ps("bankE", [128, 512])
        bankF = p.ps("bankF", [128, 512])
        for s_ in ("vm", "mp", "dva", "xr", "mn"):
            p.sem(s_)
        p.dma_group("lx", 2)
        p.dma_group("so", 2)
        blk = p.block()

        def W(eng, name, val):
            if val > 0:
                eng.wait_ge(p.sems[name], val)

        @blk.vector
        def _(v):
            v.memset(ones128c[:, :], 1.0)
            v.memset(ones1x128f[:, :], 1.0)
            p.inc(v.engine_nop(), "vm", 1)

        for tt in range(NTT):
            @blk.sync
            def _(sy, tt=tt):
                if tt > 0:
                    sy.wait_ge(p.sems["dva"], tt * 34)
                p.dma(sy, "lx", _chunked_sbuf_ap(xf, DCH, 512),
                      x3T.rearrange("(c p) t -> p c t", p=128)[:, :, tt * 512:(tt + 1) * 512])
                p.dma(sy, "lx", _chunked_sbuf_ap(ysc, DCH, 512),
                      yshT.rearrange("(c p) t -> p c t", p=128)[:, :, tt * 512:(tt + 1) * 512])
                p.dma(sy, "lx", _chunked_sbuf_ap(y1c, DCH, 512),
                      y1T.rearrange("(c p) t -> p c t", p=128)[:, :, tt * 512:(tt + 1) * 512])
                p.dma(sy, "lx", _chunked_sbuf_ap(y2c, DCH, 512),
                      y2T.rearrange("(c p) t -> p c t", p=128)[:, :, tt * 512:(tt + 1) * 512])
            lx0_now = p.c["lx0"]; lx1_now = p.c["lx1"]
            mpb = p.c["mp"]; dvab = p.c["dva"]

            @blk.vector
            def _(v, tt=tt, mpb=mpb, lx0_now=lx0_now, lx1_now=lx1_now):
                v.wait_ge(p.sems["vm"], 1)
                v.wait_ge(p.sems["lx0"], lx0_now)
                v.wait_ge(p.sems["lx1"], lx1_now)
                for oc in range(DCH):
                    W(v, "mp", mpb + oc)
                    sl = slice(oc * 512, (oc + 1) * 512)
                    v.scalar_tensor_tensor(ts[:, sl], xf[:, sl], 2.0, ysc[:, sl],
                                           ALU.mult, ALU.add)
                    v.drain()
                    v.tensor_add(ts[:, sl], ts[:, sl], y1c[:, sl])
                    v.drain()
                    ins = v.tensor_add(ts[:, sl], ts[:, sl], y2c[:, sl])
                    p.inc(ins, "dva", 1)
                    v.drain()
                    ins = v.tensor_copy(tb[:, :], ts[:, sl])
                    p.inc(ins, "dva", 1)
                    v.drain()
                    ins = v.tensor_mul(sqb[:, :], tb[:, :], tb[:, :])
                    p.inc(ins, "dva", 1)

            @blk.tensor
            def _(t, tt=tt, mpb=mpb, dvab=dvab):
                for c in range(DCH):
                    t.wait_ge(p.sems["dva"], dvab + 3 * c + 3)
                    t.matmul(bankE[0:4, :], ones128c[:, :], tb[:, :],
                             start=(c == 0), stop=(c == DCH - 1), skip_group_check=True)
                    ins = t.matmul(bankF[0:4, :], ones128c[:, :], sqb[:, :],
                                   start=(c == 0), stop=(c == DCH - 1), skip_group_check=True)
                    p.inc(ins, "mp", 1)

            emit_ln_tail2(p, blk, dvab, mpb, tt, ts, tb, sqb, tmpf, x1c, rows,
                          ones128c, ones1x128f, bankE, bankF, yT)

        @blk.gpsimd
        def _(g):
            p.gwait(g, "so")
    return nc



def build_s_l0rout():
    p = P("s_l0rout")
    nc = p.nc
    E = 8
    NT = T // 128
    x1T = p.inp("x1T", [D, T])
    Wgt = p.inp("Wgt", [D, E])
    m1o = p.out("m1o", [128, NT * E])
    m2o = p.out("m2o", [128, NT * E])
    w1o = p.out("w1o", [128, NT])
    w2o = p.out("w2o", [128, NT])
    counts = p.out("counts", [1, E])
    with p.es:
        xrc = p.sb("xrc", [128, DCH * 128], F32)
        wrf = p.sb("wrf", [128, DCH * E], F32)
        ones128c = p.sb("ones128c", [128, 4], BF16)
        nm = p.sb("nm", [128, 1], F32)
        e16 = p.sb("e16", [128, E], F32)
        se = p.sb("se", [128, 1], F32)
        gs = p.sb("gs", [128, E], F32)
        m1v = p.sb("m1v", [128, 1], F32)
        m2v = p.sb("m2v", [128, 1], F32)
        msk1 = p.sb("msk1", [128, NT * E], F32)
        msk2 = p.sb("msk2", [128, NT * E], F32)
        g2 = p.sb("g2", [128, E], F32)
        den = p.sb("den", [128, 1], F32)
        w1v = p.sb("w1v", [128, NT], F32)
        w2v = p.sb("w2v", [128, NT], F32)
        cntb = p.sb("cntb", [128, E], BF16)
        crow = p.sb("crow", [1, E], F32)
        bankA = p.ps("bankA", [128, 512])
        bankB = p.ps("bankB", [128, 512])
        bankC = p.ps("bankC", [128, 512])
        for s_ in ("vm", "mm", "xp", "dq", "mc", "dc"):
            p.sem(s_)
        p.dma_group("ld", 2)
        p.dma_group("lr", 1)
        p.dma_group("so", 2)
        blk = p.block()

        def W(eng, name, val):
            if val > 0:
                eng.wait_ge(p.sems[name], val)

        @blk.gpsimd
        def _(g):
            p.dma(g, "ld", _chunked_sbuf_ap(wrf, DCH, E), _featmaj_ap(Wgt, D, E))
            for tc in range(NT):
                if tc >= 1:
                    g.wait_ge(p.sems["mm"], tc)
                p.dma(g, "lr", _chunked_sbuf_ap(xrc, DCH, 128),
                      x1T.rearrange("(c p) t -> p c t", p=128)[:, :, tc * 128:(tc + 1) * 128])

        @blk.vector
        def _(v):
            v.memset(ones128c[:, :], 1.0)
            p.inc(v.engine_nop(), "vm", 1)

        pRb = [bankA[:, 0:E], bankB[:, 0:E]]

        @blk.tensor
        def _(t):
            p.gwait(t, "ld")
            t.wait_ge(p.sems["vm"], 1)

            def cnt_mm(j):
                t.wait_ge(p.sems["dq"], 18 * (j + 1))
                ins = t.matmul(bankC[0:4, :E], ones128c[:, :], cntb[:, :],
                               start=(j == 0), stop=(j == NT - 1), skip_group_check=True)
                p.inc(ins, "mc", 1)

            for tc in range(NT):
                t.wait_ge(p.sems["lr0"], 16 * (tc + 1))
                W(t, "xp", tc - 1)
                for dc in range(DCH):
                    ins = t.matmul(pRb[tc % 2], xrc[:, dc * 128:(dc + 1) * 128],
                                   wrf[:, dc * E:(dc + 1) * E],
                                   start=(dc == 0), stop=(dc == DCH - 1),
                                   skip_group_check=True)
                p.inc(ins, "mm", 1)
                if tc >= 2:
                    cnt_mm(tc - 2)
            cnt_mm(NT - 2)
            cnt_mm(NT - 1)

        @blk.scalar
        def _(s):
            for tc in range(NT):
                s.wait_ge(p.sems["dq"], 18 * tc + 1)
                ins = s.activation(e16[:, :], pRb[tc % 2], AF.Exp, bias=nm[:, :],
                                   accum_out=se[:, :])
                p.inc(ins, "xp", 1)

        @blk.vector
        def _(v):
            for tc in range(NT):
                v.wait_ge(p.sems["mm"], tc + 1)
                W(v, "xp", tc)
                ins = v.reduce_max(nm[:, :], pRb[tc % 2], axis=AX, negate=True)
                p.inc(ins, "dq", 1)
                v.wait_ge(p.sems["xp"], tc + 1)
                v.reciprocal(den[:, :], se[:, :])
                v.drain()
                v.tensor_single_scalar(gs[:, :], e16[:, :], den[0:128, 0:1], ALU.mult)
                v.drain()
                v.reduce_max(m1v[:, :], gs[:, :], axis=AX)
                v.drain()
                v.tensor_single_scalar(msk1[:, tc * E:(tc + 1) * E], gs[:, :],
                                       m1v[0:128, 0:1], ALU.is_ge)
                v.drain()
                v.scalar_tensor_tensor(g2[:, :], msk1[:, tc * E:(tc + 1) * E], -1e30,
                                       gs[:, :], ALU.mult, ALU.add)
                v.drain()
                v.reduce_max(m2v[:, :], g2[:, :], axis=AX)
                v.drain()
                v.tensor_single_scalar(msk2[:, tc * E:(tc + 1) * E], g2[:, :],
                                       m2v[0:128, 0:1], ALU.is_ge)
                v.tensor_add(den[:, :], m1v[:, :], m2v[:, :])
                v.drain()
                v.tensor_scalar_add(den[:, :], den[:, :], 1e-9)
                v.drain()
                v.reciprocal(den[:, :], den[:, :])
                v.drain()
                v.tensor_single_scalar(w1v[:, tc:tc + 1], m1v[:, :], den[0:128, 0:1], ALU.mult)
                v.tensor_single_scalar(w2v[:, tc:tc + 1], m2v[:, :], den[0:128, 0:1], ALU.mult)
                v.drain()
                W(v, "mc", tc)
                ins = v.tensor_add(cntb[:, :], msk1[:, tc * E:(tc + 1) * E],
                                   msk2[:, tc * E:(tc + 1) * E])
                p.inc(ins, "dq", 17)

        @blk.vector
        def _(v):
            v.wait_ge(p.sems["mc"], NT)
            ins = v.tensor_copy(crow[:, :], bankC[0:1, 0:E])
            p.inc(ins, "dc", 1)

        @blk.sync
        def _(sy):
            sy.wait_ge(p.sems["dq"], 18 * NT)
            p.dma(sy, "so", m1o[:, :], msk1[:, :])
            p.dma(sy, "so", m2o[:, :], msk2[:, :])
            p.dma(sy, "so", w1o[:, :], w1v[:, :])
            p.dma(sy, "so", w2o[:, :], w2v[:, :])
            sy.wait_ge(p.sems["dc"], 1)
            p.dma(sy, "so", counts[:, :], crow[:, :])

        @blk.gpsimd
        def _(g):
            p.gwait(g, "so")
    return nc


def build_s_l0slot(segs):
    p = P("s_l0slot")
    nc = p.nc
    E = 8
    NT = 16
    bases = [sum(segs[:e]) for e in range(E)]
    nsl = sum(segs)
    m1i = p.inp("m1i", [128, NT * E])
    m2i = p.inp("m2i", [128, NT * E])
    counts_all = p.inp("counts_all", [8, E])
    core_sel = p.inp("core_sel", [8, 1])
    dst1o = p.out("dst1o", [128, NT])
    dst2o = p.out("dst2o", [128, NT])
    with p.es:
        m1s = p.sb("m1s", [128, NT * E], F32)
        m2s = p.sb("m2s", [128, NT * E], F32)
        ca = p.sb("ca", [8, E], F32)
        cse = p.sb("cse", [8, 1], F32)
        B8 = p.sb("B8", [8, 8], F32)
        m8 = p.sb("m8", [8, 1], F32)
        offsf = p.sb("offsf", [E, 1], F32)
        acc = p.sb("acc", [E, 1], F32)
        triL = p.sb("triL", [128, 128], BF16)
        identf = p.sb("identf", [128, 128], F32)
        ri128 = p.sb("ri128", [128, 128], F32)
        ci128 = p.sb("ci128", [128, 128], F32)
        ones128c = p.sb("ones128c", [128, 4], BF16)
        segbc = p.sb("segbc", [128, E], F32)
        basebc = p.sb("basebc", [128, E], F32)
        cnt2 = p.sb("cnt2", [128, E], BF16)
        grT = p.sb("grT", [E, 128], F32)
        t16 = p.sb("t16", [128, E], F32)
        tm = p.sb("tm", [128, E], F32)
        rka = [p.sb("rka%d" % k, [128, NT], F32) for k in range(2)]
        bka = [p.sb("bka%d" % k, [128, NT], F32) for k in range(2)]
        ska = [p.sb("ska%d" % k, [128, NT], F32) for k in range(2)]
        qk = p.sb("qk", [128, NT], F32)
        q2 = p.sb("q2", [128, NT], F32)
        d1s = p.sb("d1s", [128, NT], F32)
        d2s = p.sb("d2s", [128, NT], F32)
        bankC = p.ps("bankC", [128, 512])
        bankD = p.ps("bankD", [128, 512])
        bankE = p.ps("bankE", [128, 512])
        for s_ in ("vm", "mm", "dq"):
            p.sem(s_)
        p.dma_group("ld", 2)
        p.dma_group("so", 2)
        blk = p.block()

        def W(eng, name, val):
            if val > 0:
                eng.wait_ge(p.sems[name], val)

        @blk.gpsimd
        def _(g):
            p.dma(g, "ld", m1s[:, :], m1i[:, :])
            p.dma(g, "ld", m2s[:, :], m2i[:, :])
            p.dma(g, "ld", ca[:, :], counts_all[:, :])
            p.dma(g, "ld", cse[:, :], core_sel[:, :])

        @blk.vector
        def _(v):
            v.memset(triL[:, :], 1.0)
            v.memset(identf[:, :], 1.0)
            v.memset(B8[:, :], 1.0)
            v.memset(ones128c[:, :], 1.0)
            for e in range(E):
                v.memset(segbc[:, e:e + 1], float(segs[e]))
                v.memset(basebc[:, e:e + 1], float(bases[e]))
            p.inc(v.engine_nop(), "vm", 1)

        @blk.gpsimd
        def _(g):
            g.wait_ge(p.sems["vm"], 1)
            g.iota(ri128[:, :], [[0, 128]], channel_multiplier=1,
                   allow_small_or_imprecise_dtypes=True)
            g.iota(ci128[:, :], [[1, 128]], channel_multiplier=0,
                   allow_small_or_imprecise_dtypes=True)
            ins = g.affine_select(identf[:, :], identf[:, :], [[-1, 128]],
                                  ALU.is_equal, 0.0, channel_multiplier=1)
            p.inc(ins, "vm", 1)

        @blk.vector
        def _(v):
            v.wait_ge(p.sems["vm"], 2)
            v.tensor_tensor(triL[:, :], ri128[:, :], ci128[:, :], ALU.is_lt)
            ins = v.tensor_tensor(B8[:, :], ri128[0:8, 0:8], ci128[0:8, 0:8],
                                  ALU.is_gt)
            p.inc(ins, "vm", 1)

        @blk.tensor
        def _(t):
            p.gwait(t, "ld")
            t.wait_ge(p.sems["vm"], 3)
            ins = t.matmul(bankC[0:8, 0:1], B8[:, :], cse[:, :], start=True, stop=True)
            p.inc(ins, "mm", 1)

        @blk.vector
        def _(v):
            v.wait_ge(p.sems["mm"], 1)
            ins = v.tensor_copy(m8[:, :], bankC[0:8, 0:1])
            p.inc(ins, "dq", 1)

        @blk.tensor
        def _(t):
            t.wait_ge(p.sems["dq"], 1)
            ins = t.matmul(bankC[0:E, 0:1], ca[:, :], m8[:, :], start=True, stop=True)
            p.inc(ins, "mm", 1)

        @blk.vector
        def _(v):
            v.wait_ge(p.sems["mm"], 2)
            v.tensor_copy(offsf[:, :], bankC[0:E, 0:1])
            v.drain()
            ins = v.tensor_copy(acc[:, :], offsf[:, :])
            p.inc(ins, "dq", 1)

        for tc in range(NT):
            dqb = p.c["dq"]; mmb = p.c["mm"]

            @blk.vector
            def _(v, tc=tc, mmb=mmb):
                W(v, "mm", mmb - 1)
                ins = v.tensor_add(cnt2[:, :], m1s[:, tc * E:(tc + 1) * E],
                                   m2s[:, tc * E:(tc + 1) * E])
                p.inc(ins, "dq", 1)

            @blk.tensor
            def _(t, dqb=dqb):
                t.wait_ge(p.sems["dq"], dqb + 1)
                t.matmul(bankC[0:E, 0:128], cnt2[:, :], triL[:, :], start=True,
                         stop=True, skip_group_check=True)
                ins = t.matmul(bankD[0:E, 0:4], cnt2[:, :], ones128c[:, :],
                               start=True, stop=True, skip_group_check=True)
                p.inc(ins, "mm", 2)

            @blk.vector
            def _(v, dqb=dqb, mmb=mmb):
                v.wait_ge(p.sems["mm"], mmb + 2)
                v.tensor_single_scalar(grT[:, :], bankC[0:E, 0:128], acc[0:E, 0:1],
                                       ALU.add)
                v.drain()
                ins = v.tensor_add(acc[:, :], acc[:, :], bankD[0:E, 0:1])
                p.inc(ins, "dq", 1)

            @blk.tensor
            def _(t, dqb=dqb):
                t.wait_ge(p.sems["dq"], dqb + 2)
                ins = t.transpose(bankE[:, 0:E], grT[:, :], identf[0:E, 0:E])
                p.inc(ins, "mm", 1)

            @blk.vector
            def _(v, tc=tc, mmb=mmb):
                v.wait_ge(p.sems["mm"], mmb + 3)
                v.tensor_copy(t16[:, :], bankE[:, 0:E])
                v.drain()
                for k, mk in ((0, m1s), (1, m2s)):
                    v.tensor_mul(tm[:, :], mk[:, tc * E:(tc + 1) * E], t16[:, :])
                    v.drain()
                    v.reduce_sum(rka[k][:, tc:tc + 1], tm[:, :], axis=AX)
                    v.drain()
                    v.tensor_mul(tm[:, :], mk[:, tc * E:(tc + 1) * E], segbc[:, :])
                    v.drain()
                    v.reduce_sum(ska[k][:, tc:tc + 1], tm[:, :], axis=AX)
                    v.drain()
                    v.tensor_mul(tm[:, :], mk[:, tc * E:(tc + 1) * E], basebc[:, :])
                    v.drain()
                    v.reduce_sum(bka[k][:, tc:tc + 1], tm[:, :], axis=AX)
                    v.drain()
                p.inc(v.engine_nop(), "dq", 1)

        @blk.vector
        def _(v):
            v.wait_ge(p.sems["dq"], p.c["dq"])
            for k, dXs in ((0, d1s), (1, d2s)):
                rk, sk, bk = rka[k], ska[k], bka[k]
                v.tensor_scalar_add(q2[:, :], sk[:, :], -0.5)
                v.drain()
                v.tensor_tensor(qk[:, :], rk[:, :], q2[:, :], ALU.is_gt)
                v.drain()
                for j in range(2, 8):
                    v.tensor_scalar_mul(q2[:, :], sk[:, :], float(j))
                    v.drain()
                    v.tensor_scalar_add(q2[:, :], q2[:, :], -0.5)
                    v.drain()
                    v.tensor_tensor(q2[:, :], rk[:, :], q2[:, :], ALU.is_gt)
                    v.drain()
                    v.tensor_add(qk[:, :], qk[:, :], q2[:, :])
                    v.drain()
                v.tensor_mul(q2[:, :], qk[:, :], sk[:, :])
                v.drain()
                v.scalar_tensor_tensor(qk[:, :], qk[:, :], float(nsl), rk[:, :],
                                       ALU.mult, ALU.add)
                v.drain()
                v.tensor_add(qk[:, :], qk[:, :], bk[:, :])
                v.drain()
                v.tensor_sub(dXs[:, :], qk[:, :], q2[:, :])
                v.drain()
            p.inc(v.engine_nop(), "dq", 1)
        dq_all = p.c["dq"]

        @blk.sync
        def _(sy):
            sy.wait_ge(p.sems["dq"], dq_all)
            p.dma(sy, "so", dst1o[:, :], d1s[:, :])
            p.dma(sy, "so", dst2o[:, :], d2s[:, :])

        @blk.gpsimd
        def _(g):
            p.gwait(g, "so")
    return nc



def build_s_l0ffn_r(segs):
    p = P("s_l0ffn_r")
    nc = p.nc
    E = 8
    bases = [sum(segs[:e]) for e in range(E)]
    nsl = sum(segs)
    xgi = p.inp("xgi", [D, nsl])
    gsl = p.inp("gsl", [1, nsl])
    W1t = p.inp("W1t", [E, D, 4096])
    b1 = p.inp("b1", [E, 4096])
    hgT = p.out("hgT", [17 * 128, nsl], BF16)
    with p.es:
        whb = [p.sb("whb%d" % i, [128, DCH * 2048], BF16) for i in range(2)]
        b1h = [p.sb("b1h%d" % i, [1, 2048], BF16) for i in range(2)]
        xgb = [p.sb("xgb%d" % i, [128, DCH * 512], BF16) for i in range(2)]
        gsr = [p.sb("gsr%d" % i, [1, 512], BF16) for i in range(2)]
        gateS = p.sb("gateS", [128, 512], F32)
        sgE = p.sb("sgE", [128, 512], BF16)
        t1f = p.sb("t1f", [128, 512], F32)
        hgb = [p.sb("hgb%d" % i, [128, 9 * 512], BF16) for i in range(2)]
        onesr = p.sb("onesr", [1, 512], BF16)
        bankA = p.ps("bankA", [128, 512])
        bankB = p.ps("bankB", [128, 512])
        bankC = p.ps("bankC", [128, 512])
        bankD = p.ps("bankD", [128, 512])
        bankE = p.ps("bankE", [128, 512])
        for s_ in ("vm", "m1", "sg", "da", "mg"):
            p.sem(s_)
        p.dma_group("lw", 2)
        p.dma_group("lt", 2)
        p.dma_group("lg", 2)
        p.dma_group("so", 2)
        blk = p.block()

        def W(eng, name, val):
            if val > 0:
                eng.wait_ge(p.sems[name], val)

        @blk.vector
        def _(v):
            v.memset(onesr[:, :], 1.0)
            p.inc(v.engine_nop(), "vm", 1)

        units = []
        for e in range(E):
            for h in range(2):
                units.append((e, h))
        xg_guard = [0, 0]
        wh_guard = [0, 0]
        hg_tok = [None, None]
        bi_global = 0
        for ui, (e, h) in enumerate(units):
            wbuf = ui % 2
            nb = (segs[e] + 511) // 512

            @blk.gpsimd
            def _(g, e=e, h=h, wbuf=wbuf, m1g=wh_guard[wbuf]):
                W(g, "m1", m1g)
                ap = bass.AP(whb[wbuf], 0, [[DCH * 2048, 128], [2048, DCH], [1, 1024]])
                p.dma(g, "lw", ap,
                      _featmaj_ap(W1t[e, :, :], D, 4096)[:, :, h * 1024:(h + 1) * 1024])
                ap = bass.AP(whb[wbuf], 1024, [[DCH * 2048, 128], [2048, DCH], [1, 1024]])
                p.dma(g, "lw", ap,
                      _featmaj_ap(W1t[e, :, :], D, 4096)[:, :, 2048 + h * 1024:2048 + (h + 1) * 1024])
                p.dma(g, "lg", b1h[wbuf][:, 0:1024], b1[e:e + 1, h * 1024:(h + 1) * 1024])
                p.dma(g, "lg", b1h[wbuf][:, 1024:2048],
                      b1[e:e + 1, 2048 + h * 1024:2048 + (h + 1) * 1024])
            lw0_now = p.c["lw0"]; lw1_now = p.c["lw1"]

            for bi in range(nb):
                bbuf = bi_global % 2
                bi_global += 1
                col0 = bases[e] + bi * 512
                ncols = min(512, segs[e] - bi * 512)
                m1b = p.c["m1"]; sgb_ = p.c["sg"]; dab = p.c["da"]

                @blk.gpsimd
                def _(g, col0=col0, ncols=ncols, bbuf=bbuf, xgg=xg_guard[bbuf]):
                    W(g, "mg", xgg)
                    p.dma(g, "lt",
                          bass.AP(xgb[bbuf], 0, [[DCH * 512, 128], [512, DCH], [1, ncols]]),
                          _featmaj_ap(xgi, D, nsl)[:, :, col0:col0 + ncols])
                    p.dma(g, "lg", gsr[bbuf][:, 0:ncols], gsl[:, col0:col0 + ncols])
                lt0_now = p.c["lt0"]; lt1_now = p.c["lt1"]
                lg0_now = p.c["lg0"]; lg1_now = p.c["lg1"]

                @blk.tensor
                def _(t, e=e, h=h, wbuf=wbuf, bbuf=bbuf, ncols=ncols, ui=ui, bi=bi,
                      m1b=m1b, dab=dab, sgb_=sgb_, lw0_now=lw0_now, lw1_now=lw1_now,
                      lt0_now=lt0_now, lt1_now=lt1_now, lg0_now=lg0_now,
                      lg1_now=lg1_now):
                    W(t, "lw0", lw0_now)
                    W(t, "lw1", lw1_now)
                    W(t, "lt0", lt0_now)
                    W(t, "lt1", lt1_now)
                    W(t, "lg0", lg0_now)
                    W(t, "lg1", lg1_now)
                    if ui == 0 and bi == 0:
                        t.wait_ge(p.sems["vm"], 1)
                    W(t, "da", dab)
                    ins = t.matmul(bankE[:, 0:ncols], onesr[:, 0:128],
                                   gsr[bbuf][:, 0:ncols], start=True, stop=True)
                    p.inc(ins, "m1", 1)
                    for hc in range(8):
                        pA = bankA if hc % 2 == 0 else bankB
                        pB = bankC if hc % 2 == 0 else bankD
                        W(t, "da", dab + 1 + hc - 1)
                        W(t, "sg", sgb_ + hc - 1)
                        for dc in range(DCH):
                            t.matmul(pA[:, 0:ncols],
                                     whb[wbuf][:, dc * 2048 + hc * 128:dc * 2048 + (hc + 1) * 128],
                                     xgb[bbuf][:, dc * 512:dc * 512 + ncols],
                                     start=(dc == 0), stop=False, skip_group_check=True)
                        t.matmul(pA[:, 0:ncols], b1h[wbuf][:, hc * 128:(hc + 1) * 128],
                                 onesr[:, 0:ncols], start=False, stop=True,
                                 skip_group_check=True)
                        for dc in range(DCH):
                            t.matmul(pB[:, 0:ncols],
                                     whb[wbuf][:, dc * 2048 + 1024 + hc * 128:dc * 2048 + 1024 + (hc + 1) * 128],
                                     xgb[bbuf][:, dc * 512:dc * 512 + ncols],
                                     start=(dc == 0), stop=False, skip_group_check=True)
                        ins = t.matmul(pB[:, 0:ncols],
                                       b1h[wbuf][:, 1024 + hc * 128:1024 + (hc + 1) * 128],
                                       onesr[:, 0:ncols], start=False, stop=True,
                                       skip_group_check=True)
                        p.inc(ins, "m1", 1)

                @blk.scalar
                def _(s, ncols=ncols, m1b=m1b, dab=dab, sgb_=sgb_):
                    for hc in range(8):
                        s.wait_ge(p.sems["m1"], m1b + hc + 2)
                        W(s, "da", dab + 1 + hc)
                        pB = bankC if hc % 2 == 0 else bankD
                        ins = s.activation(sgE[:, 0:ncols], pB[:, 0:ncols], AF.Sigmoid)
                        p.inc(ins, "sg", 1)

                @blk.vector
                def _(v, e=e, h=h, bbuf=bbuf, ncols=ncols, m1b=m1b, dab=dab,
                      sgb_=sgb_, tok=hg_tok[bbuf]):
                    v.wait_ge(p.sems["m1"], m1b + 1)
                    if tok is not None:
                        v.wait_ge(p.sems[tok[0]], tok[1])
                    ins = v.tensor_copy(gateS[:, 0:ncols], bankE[:, 0:ncols])
                    p.inc(ins, "da", 1)
                    v.drain()
                    for hc in range(8):
                        v.wait_ge(p.sems["m1"], m1b + hc + 2)
                        v.wait_ge(p.sems["sg"], sgb_ + hc + 1)
                        pA = bankA if hc % 2 == 0 else bankB
                        v.tensor_mul(t1f[:, 0:ncols], pA[:, 0:ncols], sgE[:, 0:ncols])
                        v.drain()
                        v.tensor_mul(t1f[:, 0:ncols], t1f[:, 0:ncols], gateS[:, 0:ncols])
                        v.drain()
                        ins = v.tensor_copy(hgb[bbuf][:, hc * 512:hc * 512 + ncols],
                                            t1f[:, 0:ncols])
                        p.inc(ins, "da", 1)
                        v.drain()
                    if h == 1:
                        v.memset(hgb[bbuf][:, 8 * 512:8 * 512 + ncols], 0.0)
                        v.drain()
                        v.tensor_copy(hgb[bbuf][0:1, 8 * 512:8 * 512 + ncols],
                                      gsr[bbuf][:, 0:ncols])
                        v.drain()
                    p.inc(v.engine_nop(), "mg", 1)
                mg_now = p.c["mg"]

                @blk.sync
                def _(sy, e=e, h=h, bbuf=bbuf, col0=col0, ncols=ncols, mg_now=mg_now):
                    sy.wait_ge(p.sems["mg"], mg_now)
                    nch = 9 if h == 1 else 8
                    tok = p.dma(sy, "so",
                                hgT.rearrange("(c p) s -> p c s", p=128)[:, h * 8:h * 8 + nch, col0:col0 + ncols],
                                bass.AP(hgb[bbuf], 0, [[9 * 512, 128], [512, nch], [1, ncols]]))
                    hg_tok[bbuf] = tok
                xg_guard[bbuf] = p.c["mg"]
            wh_guard[wbuf] = p.c["m1"]

        @blk.gpsimd
        def _(g):
            p.gwait(g, "so")
    return nc


def build_s_l0comb():
    p = P("s_l0comb")
    nc = p.nc
    KC = 17
    x1T = p.inp("x1T", [D, T])
    h1T = p.inp("h1T", [KC * 128, T], BF16)
    h2T = p.inp("h2T", [KC * 128, T], BF16)
    WotP = p.inp("WotP", [KC * 128, D])
    x2T = p.out("x2T", [D, T])
    with p.es:
        wotb = p.sb("wotb", [128, KC * D], BF16)
        h1c = p.sb("h1c", [128, KC * 512], BF16)
        h2c = p.sb("h2c", [128, KC * 512], BF16)
        gsb = p.sb("gsb", [128, KC * 512], BF16)
        xf = p.sb("xf", [128, DCH * 512], F32)
        ts = p.sb("ts", [128, DCH * 512], F32)
        tb = p.sb("tb", [128, 512], BF16)
        sqb = p.sb("sqb", [128, 512], BF16)
        tmpf = p.sb("tmpf", [128, 512], F32)
        x1c = [p.sb("x1c%d" % i, [128, 512], F32) for i in range(2)]
        _r = {n: p.sb(n, [1, 512], F32) for n in ("mrow", "m2row", "vrow")}
        rows = (_r["mrow"], _r["m2row"], _r["vrow"], _r["vrow"], _r["m2row"])
        ones128c = p.sb("ones128c", [128, 4], BF16)
        ones1x128f = p.sb("ones1x128f", [1, 128], F32)
        bankA = p.ps("bankA", [128, 512])
        bankB = p.ps("bankB", [128, 512])
        bankE = p.ps("bankE", [128, 512])
        bankF = p.ps("bankF", [128, 512])
        for s_ in ("vm", "dg", "mn", "mp", "dva", "xr"):
            p.sem(s_)
        p.dma_group("ld", 2)
        p.dma_group("lx", 2)
        p.dma_group("so", 2)
        blk = p.block()

        def W(eng, name, val):
            if val > 0:
                eng.wait_ge(p.sems[name], val)

        @blk.gpsimd
        def _(g):
            p.dma(g, "ld", _chunked_sbuf_ap(wotb, KC, D), _featmaj_ap(WotP, KC * 128, D))

        @blk.vector
        def _(v):
            v.memset(ones128c[:, :], 1.0)
            v.memset(ones1x128f[:, :], 1.0)
            p.inc(v.engine_nop(), "vm", 1)

        pA = [bankA[:, :], bankB[:, :]]
        for tt in range(NTT):
            @blk.sync
            def _(sy, tt=tt):
                if tt > 0:
                    sy.wait_ge(p.sems["dva"], tt * 34)
                    sy.wait_ge(p.sems["dg"], tt * KC)
                p.dma(sy, "lx", _chunked_sbuf_ap(xf, DCH, 512),
                      x1T.rearrange("(c p) t -> p c t", p=128)[:, :, tt * 512:(tt + 1) * 512])
                p.dma(sy, "lx", _chunked_sbuf_ap(h1c, KC, 512),
                      h1T.rearrange("(c p) t -> p c t", p=128)[:, :, tt * 512:(tt + 1) * 512])
                p.dma(sy, "lx", _chunked_sbuf_ap(h2c, KC, 512),
                      h2T.rearrange("(c p) t -> p c t", p=128)[:, :, tt * 512:(tt + 1) * 512])
            lx0_now = p.c["lx0"]; lx1_now = p.c["lx1"]
            mn_prev = p.c["mn"]
            dgb = p.c["dg"]

            @blk.vector
            def _(v, tt=tt, lx0_now=lx0_now, lx1_now=lx1_now, mn_prev=mn_prev):
                v.wait_ge(p.sems["vm"], 1)
                v.wait_ge(p.sems["lx0"], lx0_now)
                v.wait_ge(p.sems["lx1"], lx1_now)
                W(v, "mn", mn_prev)
                for kc in range(KC):
                    sl = slice(kc * 512, (kc + 1) * 512)
                    ins = v.tensor_add(gsb[:, sl], h1c[:, sl], h2c[:, sl])
                    p.inc(ins, "dg", 1)
                    v.drain()

            mpb = p.c["mp"]
            dvab = p.c["dva"]

            @blk.tensor
            def _(t, tt=tt, dgb=dgb, mpb=mpb, dvab=dvab):
                if tt == 0:
                    p.gwait(t, "ld")
                t.wait_ge(p.sems["dg"], dgb + KC)

                def stats(c):
                    t.wait_ge(p.sems["dva"], dvab + 3 * c + 3)
                    t.matmul(bankE[0:4, :], ones128c[:, :], tb[:, :],
                             start=(c == 0), stop=(c == DCH - 1), skip_group_check=True)
                    ins = t.matmul(bankF[0:4, :], ones128c[:, :], sqb[:, :],
                                   start=(c == 0), stop=(c == DCH - 1), skip_group_check=True)
                    p.inc(ins, "mp", 1)

                for oc in range(DCH):
                    if oc >= 2:
                        t.wait_ge(p.sems["dva"], dvab + 3 * (oc - 2) + 1)
                    for kc in range(KC):
                        ins = t.matmul(pA[oc % 2], wotb[:, kc * D + oc * 128:kc * D + (oc + 1) * 128],
                                       gsb[:, kc * 512:(kc + 1) * 512],
                                       start=(kc == 0), stop=(kc == KC - 1),
                                       skip_group_check=True)
                    p.inc(ins, "mn", 1)
                    if oc >= 2:
                        stats(oc - 2)
                stats(DCH - 2)
                stats(DCH - 1)
            mn_prj = p.c["mn"]

            @blk.vector
            def _(v, tt=tt, mpb=mpb, mn_prj=mn_prj):
                for oc in range(DCH):
                    v.wait_ge(p.sems["mn"], mn_prj - DCH + oc + 1)
                    W(v, "mp", mpb + oc)
                    ins = v.tensor_add(ts[:, oc * 512:(oc + 1) * 512],
                                       xf[:, oc * 512:(oc + 1) * 512], pA[oc % 2])
                    p.inc(ins, "dva", 1)
                    v.drain()
                    ins = v.tensor_copy(tb[:, :], ts[:, oc * 512:(oc + 1) * 512])
                    p.inc(ins, "dva", 1)
                    v.drain()
                    ins = v.tensor_mul(sqb[:, :], tb[:, :], tb[:, :])
                    p.inc(ins, "dva", 1)

            emit_ln_tail2(p, blk, dvab, mpb, tt, ts, tb, sqb, tmpf, x1c, rows,
                          ones128c, ones1x128f, bankE, bankF, x2T)

        @blk.gpsimd
        def _(g):
            p.gwait(g, "so")
    return nc


def _run(nc, in_maps):
    return run_bass_kernel_spmd(nc, in_maps, core_ids=list(range(NCORES))).results


def kernel(**inputs):
    f32 = np.float32
    g = inputs
    x = np.ascontiguousarray(g["L0_attn_h_t"].reshape(2 * S, D))
    xT = [np.ascontiguousarray(x[T * c:T * (c + 1)].T) for c in range(NCORES)]

    Wck = np.ascontiguousarray(np.concatenate([g["L0_attn_W_DKV"], g["L0_attn_W_KR"]], axis=1))
    bck = np.ascontiguousarray(np.concatenate([g["L0_attn_b_DKV"], g["L0_attn_b_KR"]]))
    r1 = _run(build_s_ck(), [{"xT": xT[c], "Wck": Wck, "bck": bck} for c in range(NCORES)])
    ckd_b = [np.concatenate([r1[4 * b + i]["ck_dup"] for i in range(4)], axis=0)
             for b in range(2)]
    vh_b = [np.concatenate([r1[4 * b + i]["vhat"] for i in range(4)], axis=1)
            for b in range(2)]

    W_UKt = np.ascontiguousarray(g["L0_attn_W_UK"].T)
    W_UVt = np.ascontiguousarray(g["L0_attn_W_UV"].T)
    r2 = _run(build_s_l0attn(), [{
        "xT": xT[c],
        "ckdup": np.ascontiguousarray(ckd_b[c // 4]),
        "vhat": np.ascontiguousarray(vh_b[c // 4]),
        "W_DQ": g["L0_attn_W_DQ"], "b_DQ": g["L0_attn_b_DQ"],
        "W_UQ": g["L0_attn_W_UQ"], "b_UQ": g["L0_attn_b_UQ"],
        "W_UKt": W_UKt,
        "W_QR": g["L0_attn_W_QR"], "b_QR": g["L0_attn_b_QR"],
        "W_UVt": W_UVt,
        "W_O": g["L0_attn_W_O"], "b_O": g["L0_attn_b_O"],
        "b_UV": g["L0_attn_b_UV"], "b_UK": g["L0_attn_b_UK"],
    } for c in range(NCORES)])

    Wgt = np.ascontiguousarray(g["L0_ffn_Wg"].T)
    W1t = np.ascontiguousarray(g["L0_ffn_W1"].transpose(0, 2, 1))
    WotP = np.ascontiguousarray(np.concatenate(
        [g["L0_ffn_Wo"].T, g["L0_ffn_bo"][None, :], np.zeros((127, D), f32)], axis=0))
    r3r = _run(build_s_l0rout(), [{
        "x1T": r2[c]["x1T"], "Wgt": Wgt,
    } for c in range(NCORES)])
    counts0 = np.ascontiguousarray(
        np.concatenate([r3r[c]["counts"] for c in range(NCORES)], axis=0))
    tot0 = counts0.sum(axis=0)
    segs0 = [max(128, int(128 * math.ceil(t / 8.0 / 128.0))) for t in tot0]
    nsl0 = sum(segs0)
    r3b_in = []
    for c in range(NCORES):
        sel = np.zeros((NCORES, 1), f32)
        sel[c, 0] = 1.0
        r3b_in.append({
            "m1i": r3r[c]["m1o"], "m2i": r3r[c]["m2o"],
            "counts_all": counts0, "core_sel": sel,
        })
    r3b = _run(build_s_l0slot(segs0), r3b_in)
    dst0 = {}
    for c in range(NCORES):
        dst0[(c, 1)] = r3b[c]["dst1o"].T.reshape(T).astype(np.int64)
        dst0[(c, 2)] = r3b[c]["dst2o"].T.reshape(T).astype(np.int64)
    x1T_all = np.concatenate([r2[c]["x1T"] for c in range(NCORES)], axis=1)
    perm0 = np.full(NCORES * nsl0, -1, np.int64)
    gv0 = np.zeros(NCORES * nsl0, f32)
    for c in range(NCORES):
        for k, w in ((1, r3r[c]["w1o"]), (2, r3r[c]["w2o"])):
            d = dst0[(c, k)]
            m = d >= 0
            toks = c * T + np.arange(T)
            perm0[d[m]] = toks[m]
            gv0[d[m]] = w.T.reshape(T)[m]
    r3c_in = []
    for c in range(NCORES):
        pq = perm0[c * nsl0:(c + 1) * nsl0]
        xg = np.zeros((D, nsl0), f32)
        mq = pq >= 0
        xg[:, mq] = x1T_all[:, pq[mq]]
        r3c_in.append({
            "xgi": xg,
            "gsl": np.ascontiguousarray(gv0[c * nsl0:(c + 1) * nsl0][None, :]),
            "W1t": W1t, "b1": g["L0_ffn_b1"],
        })
    r3c = _run(build_s_l0ffn_r(segs0), r3c_in)
    hgT_all = np.concatenate([r3c[c]["hgT"] for c in range(NCORES)], axis=1)
    r3d_in = []
    for c in range(NCORES):
        h1 = np.zeros((17 * 128, T), hgT_all.dtype)
        h2 = np.zeros((17 * 128, T), hgT_all.dtype)
        d1 = dst0[(c, 1)]; m1_ = d1 >= 0; h1[:, m1_] = hgT_all[:, d1[m1_]]
        d2 = dst0[(c, 2)]; m2_ = d2 >= 0; h2[:, m2_] = hgT_all[:, d2[m2_]]
        r3d_in.append({
            "x1T": r2[c]["x1T"], "h1T": h1, "h2T": h2, "WotP": WotP,
        })
    r3 = _run(build_s_l0comb(), r3d_in)

    r4 = _run(build_s_l1qkv(), [{
        "x2T": r3[c]["x2T"], "W_qkv": g["L1_attn_W_qkv"],
    } for c in range(NCORES)])

    kn_t = [r4[c]["kno"].reshape(128, 16, 16).transpose(1, 0, 2).reshape(T, 16)
            for c in range(NCORES)]
    knT_b = [np.ascontiguousarray(np.concatenate(kn_t[4 * b:4 * b + 4], axis=0).T)
             for b in range(2)]
    qkv_b, vsw_b = [], []
    for b in range(2):
        qb = np.concatenate([r4[4 * b + i]["qkv"] for i in range(4)], axis=0)
        qkv_b.append(np.ascontiguousarray(qb.T))
        v130 = np.ones((8, S, 130), qb.dtype)
        qh2 = qb.reshape(S, 8, 2, 64).transpose(1, 0, 2, 3)
        v130[:, :, 0:64] = qh2[:, :, 0, :]
        v130[:, :, 65:129] = qh2[:, :, 1, :]
        vsw_b.append(np.ascontiguousarray(
            v130.reshape(8, 64, 128, 130).transpose(0, 2, 1, 3)).reshape(8 * 128, 64 * 130))
    r5_in = []
    for c in range(NCORES):
        b = c // 4
        qo = r4[c]["qkv"]
        r5_in.append({
            "x2T": r3[c]["x2T"],
            "khat_in": qkv_b[b],
            "qhpT_in": np.ascontiguousarray(qo.T),
            "vsw_in": vsw_b[b],
            "knTo": np.ascontiguousarray(kn_t[c].T),
            "knTb": knT_b[b],
            "W_out": g["L1_attn_W_out"], "b_out": g["L1_attn_b_out"],
        })
    r5 = _run(build_s_l1attn(), r5_in)

    r6 = _run(build_s_l1rout(), [{
        "x3T": r5[c]["x3T"], "router_w": g["L1_ffn_router_w"],
        "router_b": g["L1_ffn_router_b"],
    } for c in range(NCORES)])
    counts_all = np.ascontiguousarray(
        np.concatenate([r6[c]["counts"] for c in range(NCORES)], axis=0))

    r6b_in = []
    for c in range(NCORES):
        sel = np.zeros((NCORES, 1), f32)
        sel[c, 0] = 1.0
        r6b_in.append({
            "m1i": r6[c]["m1o"], "m2i": r6[c]["m2o"],
            "counts_all": counts_all, "core_sel": sel,
        })
    r6b = _run(build_s_l1slot(), r6b_in)
    NSL = L1NSL
    dstt = {}
    for c in range(NCORES):
        dstt[(c, 1)] = r6b[c]["dst1o"].T.reshape(T).astype(np.int64)
        dstt[(c, 2)] = r6b[c]["dst2o"].T.reshape(T).astype(np.int64)

    x3T_all = np.concatenate([r5[c]["x3T"] for c in range(NCORES)], axis=1)
    perm = np.full(NCORES * NSL, -1, np.int64)
    gv = np.zeros(NCORES * NSL, f32)
    for c in range(NCORES):
        for k, w in ((1, r6[c]["w1o"]), (2, r6[c]["w2o"])):
            d = dstt[(c, k)]
            m = d >= 0
            toks = c * T + np.arange(T)
            perm[d[m]] = toks[m]
            gv[d[m]] = w.T.reshape(T)[m]
    r7a_in = []
    for c in range(NCORES):
        pq = perm[c * NSL:(c + 1) * NSL]
        xg = np.zeros((D, NSL), f32)
        mq = pq >= 0
        xg[:, mq] = x3T_all[:, pq[mq]]
        r7a_in.append({
            "x3T": r5[c]["x3T"], "xgi": xg,
            "gsl": np.ascontiguousarray(gv[c * NSL:(c + 1) * NSL][None, :]),
            "w_gate": g["L1_ffn_w_gate"], "w_up": g["L1_ffn_w_up"],
            "w_down": g["L1_ffn_w_down"],
            "sw_gate": g["L1_ffn_sw_gate"], "sw_up": g["L1_ffn_sw_up"],
            "sw_down": g["L1_ffn_sw_down"],
        })
    r7a = _run(build_s_l1ffn_r(), r7a_in)

    ygT_all = np.concatenate([r7a[c]["ygT"] for c in range(NCORES)], axis=1)
    r7b_in = []
    for c in range(NCORES):
        y1 = np.zeros((D, T), ygT_all.dtype)
        y2 = np.zeros((D, T), ygT_all.dtype)
        d1 = dstt[(c, 1)]; m1_ = d1 >= 0; y1[:, m1_] = ygT_all[:, d1[m1_]]
        d2 = dstt[(c, 2)]; m2_ = d2 >= 0; y2[:, m2_] = ygT_all[:, d2[m2_]]
        r7b_in.append({
            "x3T": r5[c]["x3T"], "yshT": r7a[c]["yshT"],
            "y1T": y1, "y2T": y2,
        })
    r7b = _run(build_s_l1comb(), r7b_in)

    y = np.concatenate([r7b[c]["yT"].T for c in range(NCORES)], axis=0)
    return np.ascontiguousarray(y.reshape(2, S, D)).astype(np.float32)
